# version 16
# baseline (speedup 1.0000x reference)
import os
from contextlib import ExitStack

import numpy as np
import ml_dtypes

import concourse.bass as bass
import concourse.mybir as mybir
from concourse.bass_utils import run_bass_kernel_spmd
from concourse.library_config import mlp as _MLP_LIB

F32 = mybir.dt.float32
BF16 = mybir.dt.bfloat16
I16 = mybir.dt.int16
I32 = mybir.dt.int32
U32 = mybir.dt.uint32
AF = mybir.ActivationFunctionType
ALU = mybir.AluOpType
AX = mybir.AxisListType

L, D, DCKV, DCQ = 2048, 1024, 256, 512
NH, DH, DR, HALF, NIH = 8, 64, 32, 16, 4
NT = L // 128
SCALE = float((DH + DR) ** -0.5)
ROPE_BASE = 10000.0
BIGNEG = -1.0e9
REPL = -1.0e30
GW = 320
_CONSTS = (1e-5, 1.0, -1.0, 1.0 / 1024.0, float((64 + 32) ** -0.5), 0.0, 127.0, 128.5)

ENGS = ("tensor", "vector", "scalar", "gpsimd", "sync")


def _trig4(angles):
    c, s = np.cos(angles), np.sin(angles)
    return np.concatenate([c, s, c, s], axis=0)


def build_statics():
    th = 1.0 / ROPE_BASE ** (2.0 * np.arange(HALF, dtype=np.float64) / DR)
    t = np.arange(L, dtype=np.float64)
    bf = ml_dtypes.bfloat16
    st = {}
    st["ident32"] = np.eye(128, dtype=np.float32)
    st["ident16"] = np.eye(128).astype(bf)
    st["ones16"] = np.ones((128, 128), dtype=bf)
    st["ones32"] = np.ones((128, 128), dtype=np.float32)
    st["tq1"] = _trig4(th[:, None] * t[None, :]).astype(bf)
    st["tq2"] = _trig4(2.0 * th[:, None] * t[None, :]).astype(bf)
    st["trigb"] = _trig4(th[:, None] * (t[None, :] + 63.0)).astype(bf)
    st["trigb0"] = _trig4(th[:, None] * t[None, :192]).astype(bf)
    st["trigk"] = _trig4(th[:, None] * (64.0 + np.arange(32.0)[None, :])).astype(bf)
    c2 = np.zeros((64, 64), dtype=np.float32)
    for m in range(16):
        c2[m, m] = 1.0
        c2[48 + m, m] = -1.0
    for m in range(16, 48):
        i = (m - 16) % 16
        c2[16 + i, m] = 1.0
        c2[32 + i, m] = 1.0
    for m in range(48, 64):
        i = m - 48
        c2[i, m] = -1.0
        c2[48 + i, m] = 1.0
    st["c2"] = c2.astype(bf)
    dup = np.zeros((32, 64), dtype=np.float32)
    for m in range(64):
        dup[(m % 16) if m < 32 else 16 + (m % 16), m] = 1.0
    st["dupr"] = np.tile(dup, (4, 1)).astype(bf)
    s = np.arange(L, dtype=np.float64)
    st["fut"] = np.tile((1.0e8 - 16.0 * s).astype(np.float32), (128, 1))
    st["pert"] = np.tile((-1.0e-20 * (s + 1.0)).astype(np.float32), (128, 1))
    p = np.arange(128)
    dq = p // 8
    col = np.arange(192)
    for v in range(8):
        lo = dq[:, None] + 1 + 16 * v
        hi = dq[:, None] + 64 + 16 * v
        st[f"bm{v}"] = np.where((col[None, :] >= lo) & (col[None, :] <= hi),
                                0.0, BIGNEG).astype(np.float32)
    st["mst96"] = np.tile(np.where(col < 96, 0.0, BIGNEG).astype(np.float32),
                          (128, 1))
    colg = np.arange(128)
    st["diagmask"] = np.where((colg // 32)[None, :] == (dq % 4)[:, None],
                              0.0, BIGNEG).astype(np.float32)
    st["constv"] = np.tile(np.array(_CONSTS, dtype=np.float32), (128, 1))
    st["rep16"] = (np.arange(128)[None, :] % 16 == np.arange(16)[:, None]
                   ).astype(np.float32)
    return st


STATIC_SHAPES = {
    "ident32": ([128, 128], F32), "ident16": ([128, 128], BF16),
    "ones16": ([128, 128], BF16), "ones32": ([128, 128], F32),
    "tq1": ([64, L], BF16), "tq2": ([64, L], BF16), "trigb": ([64, L], BF16),
    "trigb0": ([64, 192], BF16), "trigk": ([64, 32], BF16),
    "c2": ([64, 64], BF16), "dupr": ([128, 64], BF16),
    "fut": ([128, L], F32), "pert": ([128, L], F32),
    **{f"bm{v}": ([128, 192], F32) for v in range(8)},
    "mst96": ([128, 192], F32), "diagmask": ([128, 128], F32),
    "constv": ([128, 8], F32), "rep16": ([16, 128], F32),
}

WEIGHT_SHAPES = {
    "W_dkv": (D, DCKV), "W_uk": (DCKV, 512), "W_uv": (DCKV, 512),
    "W_dq": (D, DCQ), "W_uq": (DCQ, 512), "W_qr": (DCQ, 256),
    "W_kr": (D, DR), "W_out": (512, D), "idx_wq": (D, 256),
    "idx_wk": (D, DH), "idx_w": (1, NIH), "raw_delta": (1, HALF),
    "w_gate": (4096, D), "w_up": (4096, D), "w_down": (D, 4096),
}


class BB:

    NSLOT = 8

    def __init__(self, nc, stack):
        self.nc = nc
        self.cnt = {e: 0 for e in ENGS}
        self.water = {e: {p: 0 for p in ENGS} for e in ENGS}
        self.sem = {e: stack.enter_context(nc.semaphore(f"es_{e}")) for e in ENGS}
        self.dq = {}
        self.dwater = {e: {} for e in ENGS}
        self.stack = stack
        self.buf = {}

    def eng(self, n):
        return getattr(self.nc, n)

    def _drec(self, q):
        if q not in self.dq:
            sems = [self.stack.enter_context(self.nc.semaphore(f"ds_{q}{i}"))
                    for i in range(self.NSLOT)]
            self.dq[q] = {"sems": sems, "uses": [0] * self.NSLOT, "idx": 0}
        return self.dq[q]

    def _dma_slot(self, eng, q):
        rec = self._drec(q)
        i = rec["idx"]
        rec["idx"] = (i + 1) % self.NSLOT
        if rec["uses"][i] > 0:
            self.wait(eng, ("D", (q, i), 16 * rec["uses"][i]))
        rec["uses"][i] += 1
        return rec["sems"][i], ("D", (q, i), 16 * rec["uses"][i])

    def _rec(self, h):
        k = id(h)
        if k not in self.buf:
            self.buf[k] = {"w": None, "r": []}
        return self.buf[k]

    def _pre(self, eng, reads, writes, extra):
        toks = list(extra)
        for h in reads:
            rr = self._rec(h)
            if rr["w"] is not None:
                toks.append(rr["w"])
        for h in writes:
            rr = self._rec(h)
            if rr["w"] is not None:
                toks.append(rr["w"])
            toks.extend(rr["r"])
        self.wait(eng, *toks)

    def _post(self, tok, reads, writes):
        for h in writes:
            rr = self._rec(h)
            rr["w"] = tok
            rr["r"] = []
        for h in reads:
            rr = self._rec(h)
            if tok not in rr["r"]:
                rr["r"].append(tok)

    def op(self, eng, method, *args, r=(), w=(), after=(), **kwargs):
        self._pre(eng, r, w, after)
        inst = getattr(self.eng(eng), method)(*args, **kwargs)
        self.cnt[eng] += 1
        inst.then_inc(self.sem[eng], 1)
        tok = (eng, self.cnt[eng])
        self._post(tok, r, w)
        return tok

    def mm(self, chain, r=(), w=(), after=(), tile_position=None):
        self._pre("tensor", r, w, after)
        for i, (o, lh, rh, s0, s1) in enumerate(chain):
            inst = self.nc.tensor.matmul(o, lh, rh, start=s0, stop=s1,
                                         tile_position=tile_position)
            if i == len(chain) - 1:
                self.cnt["tensor"] += 1
                inst.then_inc(self.sem["tensor"], 1)
        tok = ("tensor", self.cnt["tensor"])
        self._post(tok, r, w)
        return tok

    def tr(self, out, in_, ident, r=(), w=(), after=()):
        self._pre("tensor", r, w, after)
        inst = self.nc.tensor.transpose(out, in_, ident)
        self.cnt["tensor"] += 1
        inst.then_inc(self.sem["tensor"], 1)
        tok = ("tensor", self.cnt["tensor"])
        self._post(tok, r, w)
        return tok

    def dma(self, eng, q, out, in_, r=(), w=(), after=(), transpose=False, **kw):
        self._pre(eng, r, w, after)
        sem, tok = self._dma_slot(eng, q)
        if transpose:
            inst = self.eng(eng).dma_start_transpose(out, in_)
        else:
            inst = self.eng(eng).dma_start(out=out, in_=in_, **kw)
        inst.then_inc(sem, 16)
        self._post(tok, r, w)
        return tok

    def gather(self, q, r=(), w=(), after=(), **kwargs):
        self._pre("gpsimd", r, w, after)
        sem, tok = self._dma_slot("gpsimd", q)
        inst = self.nc.gpsimd.dma_gather(**kwargs)
        inst.then_inc(sem, 16)
        self._post(tok, r, w)
        return tok

    def wait(self, eng, *tokens):
        e = self.eng(eng)
        dmax, emax = {}, {}
        for t in tokens:
            if t is None:
                continue
            if t[0] == "D":
                _, nm, v = t
                dmax[nm] = max(dmax.get(nm, 0), v)
            else:
                p, v = t
                emax[p] = max(emax.get(p, 0), v)
        for nm, v in dmax.items():
            wt = self.dwater[eng]
            if wt.get(nm, 0) < v:
                q, i = nm
                e.wait_ge(self.dq[q]["sems"][i], v)
                wt[nm] = v
        for p, v in emax.items():
            if self.water[eng][p] < v:
                e.wait_ge(self.sem[p], v)
                self.water[eng][p] = v

    def barrier(self):
        toks = [(e, self.cnt[e]) for e in ENGS]
        dtoks = [("D", (q, i), 16 * rec["uses"][i])
                 for q, rec in self.dq.items()
                 for i in range(self.NSLOT) if rec["uses"][i] > 0]
        for e in ENGS:
            self.wait(e, *toks, *dtoks)

    def fence(self, queues=()):
        toks = [(e, self.cnt[e]) for e in ENGS]
        dtoks = [("D", (q, i), 16 * self.dq[q]["uses"][i])
                 for q in queues if q in self.dq
                 for i in range(self.NSLOT) if self.dq[q]["uses"][i] > 0]
        for e in ENGS:
            self.wait(e, *toks, *dtoks)

    def qtokens(self, q):
        if q not in self.dq:
            return []
        rec = self.dq[q]
        return [("D", (q, i), 16 * rec["uses"][i])
                for i in range(self.NSLOT) if rec["uses"][i] > 0]


def band_params(ti, g):
    q0 = 128 * ti + 16 * g
    if ti == 0 and g < 4:
        return 0, 192, "mst96", 1
    a = (q0 - 64) // 128
    width = min(192, L - 128 * a)
    v = (q0 - 128 * a - 64) // 16
    return a, width, f"bm{v}", 2


def build_program(debug=()):
    nc = bass.Bass("TRN2", target_bir_lowering=False)

    dX = nc.dram_tensor("x", [L, D], F32, kind="ExternalInput").ap()
    dOut = nc.dram_tensor("out", [L, D], F32, kind="ExternalOutput").ap()
    dW = {nm: nc.dram_tensor(nm, list(sh), F32, kind="ExternalInput").ap()
          for nm, sh in WEIGHT_SHAPES.items()}
    dS = {nm: nc.dram_tensor("st_" + nm, list(sh), dt_, kind="ExternalInput").ap()
          for nm, (sh, dt_) in STATIC_SHAPES.items()}
    dH2U = nc.dram_tensor("h2u", [L, 4096], BF16, kind="Internal").ap()
    dCKV = nc.dram_tensor("ckvscr", [L, DCKV], BF16, kind="Internal").ap()
    dFFN = nc.dram_tensor("ffnout", [L, D], F32, kind="Internal").ap()
    dQT = nc.dram_tensor("qtscr", [2, 128, NH * L], BF16, kind="Internal").ap()
    dMUQ = nc.dram_tensor("muqscr", [2, 64, NH * L], BF16, kind="Internal").ap()
    dDbg = {nm: nc.dram_tensor("dbg_" + nm, list(shp), dt_, kind="ExternalOutput").ap()
            for nm, shp, dt_ in debug}

    est = ExitStack()
    bb = BB(nc, est)
    A = nc.alloc_sbuf_tensor

    sSt = {nm: A("s_" + nm, list(STATIC_SHAPES[nm][0]), STATIC_SHAPES[nm][1])
           for nm in dS if nm not in ("tq1", "tq2", "trigb", "fut", "pert")}
    sCkvT = [A(f"ckvT{g}", [128, L], BF16) for g in range(2)]
    sCkv = [A(f"ckv{t}", [128, DCKV], BF16) for t in range(NT)]
    sScr = A("scr", [128, 16 * 384], BF16)
    sKiT = A("kiT", [128, L], BF16)
    sQiT = [A(f"qiT{g}", [128, L], BF16) for g in range(2)]
    sWuv = [A(f"wuv{g}", [128, 512], BF16) for g in range(2)]
    sWout8 = [A(f"wout{h}", [64, D], BF16) for h in range(NH)]
    sBK = A("bk", [64, L], BF16)
    sBK0 = A("bk0", [64, 192], BF16)
    sStats = A("stats", [128, 2 * NT], F32)
    sIdxw = A("idxw", [128, NIH], F32)
    sSw = A("swv", [128, 8], F32)
    sRm = A("rmaxs", [128, NT], F32)
    sMu = A("mus", [128, NT], F32)
    sQs = A("qss", [128, NT], F32)

    with est, nc.Block():
        nc.gpsimd.load_library(_MLP_LIB)
        for nm in dS:
            if nm in ("tq1", "tq2", "trigb", "fut", "pert"):
                continue
            bb.dma("sync", "ld", sSt[nm][:], dS[nm][:], w=(sSt[nm],))
        for ci, cv in enumerate(_CONSTS):
            nc.const_aps.register(F32, float(cv), sSt["constv"][:, ci:ci + 1])
        bb.dma("sync", "ld", sIdxw[0:1, :], dW["idx_w"][:], w=(sIdxw,))
        bb.op("gpsimd", "partition_broadcast", sIdxw[:], sIdxw[0:1, :], 128,
              r=(sIdxw,), w=(sIdxw,))
        bb.barrier()
        bb.op("vector", "memset", sStats[:], 0.0, w=(sStats,))
        bb.op("vector", "memset", sSw[:], 0.0, w=(sSw,))
        bb.op("vector", "memset", sRm[:], 0.0, w=(sRm,))

        bb.op("vector", "memset", sScr[:], 0.0, w=(sScr,))
        with nc.sbuf_tensor("xnT", [128, 8 * L], BF16) as xnT:
            mest = ExitStack()
            with mest:
                _phase2_means(nc, bb, dW, sSw, sSt, dDbg, mest)
                _phase1_ln(nc, bb, dX, xnT, sStats, sMu, sQs, sSt, dDbg)
                bb.fence(queues=("ld",))
                _phase3_proj(nc, bb, dW, dS, xnT, sCkvT, sCkv, sScr, sKiT,
                             sQiT, sWuv, sWout8, sBK, sBK0, sSt, dCKV, dQT,
                             dMUQ, dDbg)
            with nc.sbuf_tensor("xqT", [128, 8 * L], BF16) as xqT:
                _phase1_xq(nc, bb, dX, xqT, sMu, sQs, sSt, dDbg)
                _phase2_gateup(nc, bb, dW, dH2U, xqT, sStats, sSw, sRm, sSt,
                               dDbg)
        _phase2_down(nc, bb, dW, dH2U, dFFN, sStats, sSw, sRm, sSt, dDbg)
        _phase4_attn(nc, bb, dX, dOut, dFFN, dCKV, dQT, dMUQ, sCkvT, sCkv,
                     sScr, sKiT, sQiT, sWuv, sWout8, sBK, sBK0, sSt, dS,
                     sStats, sIdxw, dDbg)
    mybir.codegen_inst_isa_subclasses(nc)
    return nc, [nm for nm, _, _ in debug]


def _phase1_ln(nc, bb, dX, xnT, sStats, sMu, sQs, sSt, dDbg):
    ident16 = sSt["ident16"]
    with (
        nc.sbuf_tensor("xtmA", [128, D], F32) as xtmA,
        nc.sbuf_tensor("xtmB", [128, D], F32) as xtmB,
        nc.sbuf_tensor("xctm", [128, D], F32) as xctm,
        nc.sbuf_tensor("sqtm", [128, D], F32) as sqtm,
        nc.sbuf_tensor("xn16A", [128, D], BF16) as xn16A,
        nc.sbuf_tensor("xn16B", [128, D], BF16) as xn16B,
        nc.sbuf_tensor("c1", [128, 1], F32) as c1,
        nc.sbuf_tensor("c2c", [128, 1], F32) as c2c,
        nc.sbuf_tensor("c3", [128, 1], F32) as c3,
        nc.psum_tensor("pTa", [128, 64], F32) as pTa,
        nc.psum_tensor("pTb", [128, 64], F32) as pTb,
    ):
        pT = (pTa, pTb)
        xnT3 = xnT[:].rearrange("p (a t) -> p a t", a=8, t=L)
        for ti in range(NT):
            r0 = ti * 128
            xtm = xtmA if ti % 2 == 0 else xtmB
            xn16 = xn16A if ti % 2 == 0 else xn16B
            bb.dma("sync", "ld", xtm[:], dX[r0:r0 + 128, :], w=(xtm,))
            bb.op("vector", "tensor_reduce", c1[:], xtm[:], AX.X, ALU.add,
                  r=(xtm,), w=(c1,))
            bb.op("vector", "tensor_scalar", c1[:], c1[:], 1.0 / D, None, ALU.mult,
                  r=(c1,), w=(c1,))
            bb.op("vector", "tensor_copy", sMu[:, ti:ti + 1], c1[:], r=(c1,),
                  w=(sMu,))
            bb.op("vector", "tensor_scalar", xctm[:], xtm[:], c1[:], None,
                  ALU.subtract, r=(xtm, c1), w=(xctm,))
            bb.op("scalar", "activation", sqtm[:], xctm[:], AF.Square,
                  r=(xctm,), w=(sqtm,))
            bb.op("vector", "tensor_reduce", c2c[:], sqtm[:], AX.X, ALU.add,
                  r=(sqtm,), w=(c2c,))
            bb.op("scalar", "activation", c2c[:], c2c[:], AF.Sqrt, bias=1e-5,
                  scale=1.0 / D, r=(c2c,), w=(c2c,))
            bb.op("vector", "reciprocal", c2c[:], c2c[:], r=(c2c,), w=(c2c,))
            bb.op("vector", "tensor_reduce", c3[:], xctm[:], AX.X, ALU.max,
                  apply_absolute_value=True, r=(xctm,), w=(c3,))
            bb.op("vector", "reciprocal", sQs[:, ti:ti + 1], c3[:], r=(c3,),
                  w=(sQs,))
            bb.op("vector", "tensor_scalar", sQs[:, ti:ti + 1],
                  sQs[:, ti:ti + 1], 127.0, None, ALU.mult, r=(sQs,), w=(sQs,))
            bb.op("vector", "tensor_scalar", xn16[:], xctm[:], c2c[:], None,
                  ALU.mult, r=(xctm, c2c), w=(xn16,))
            bb.op("vector", "tensor_scalar", sStats[:, ti:ti + 1], c3[:], c2c[:],
                  1.0 / 127.0, ALU.mult, ALU.mult, r=(c3, c2c), w=(sStats,))
            for k in range(8):
                pt = pT[k % 2]
                bb.tr(pt[:].bitcast(BF16)[:, 0:128],
                      xn16[:, k * 128:(k + 1) * 128], ident16[:],
                      r=(xn16, ident16), w=(pt,))
                bb.op("scalar", "activation", xnT3[:, k, r0:r0 + 128],
                      pt[:].bitcast(BF16)[:, 0:128], AF.Copy, r=(pt,), w=(xnT,))
    if "xnT" in dDbg:
        bb.dma("sync", "dbg", dDbg["xnT"][:], xnT[:], r=(xnT,))


def _phase1_xq(nc, bb, dX, xqT, sMu, sQs, sSt, dDbg):
    ident16 = sSt["ident16"]
    with (
        nc.sbuf_tensor("qxtA", [128, D], F32) as qxtA,
        nc.sbuf_tensor("qxtB", [128, D], F32) as qxtB,
        nc.sbuf_tensor("qsq", [128, D], F32) as qsq,
        nc.sbuf_tensor("qx32", [128, D], I32) as qx32,
        nc.sbuf_tensor("qxgt", [128, D], BF16) as qxgt,
        nc.sbuf_tensor("xq16A", [128, D], BF16) as xq16A,
        nc.sbuf_tensor("xq16B", [128, D], BF16) as xq16B,
        nc.psum_tensor("pTg", [128, 64], F32) as pTg,
        nc.psum_tensor("pTh", [128, 64], F32) as pTh,
    ):
        pT = (pTg, pTh)
        xqT3 = xqT[:].rearrange("p (a t) -> p a t", a=8, t=L)
        for ti in range(NT):
            r0 = ti * 128
            xtm = qxtA if ti % 2 == 0 else qxtB
            xq16 = xq16A if ti % 2 == 0 else xq16B
            bb.dma("sync", "ld", xtm[:], dX[r0:r0 + 128, :], w=(xtm,))
            bb.op("vector", "tensor_scalar", xtm[:], xtm[:], sMu[:, ti:ti + 1],
                  None, ALU.subtract, r=(xtm, sMu), w=(xtm,))
            bb.op("vector", "tensor_scalar", qsq[:], xtm[:], sQs[:, ti:ti + 1],
                  0.5, ALU.mult, ALU.add, r=(xtm, sQs), w=(qsq,))
            bb.op("scalar", "activation", qx32[:], qsq[:], AF.Copy,
                  r=(qsq,), w=(qx32,))
            bb.op("vector", "tensor_tensor", qxgt[:], qx32[:], qsq[:],
                  ALU.is_gt, r=(qx32, qsq), w=(qxgt,))
            bb.op("vector", "tensor_tensor", xq16[:], qx32[:], qxgt[:],
                  ALU.subtract, r=(qx32, qxgt), w=(xq16,))
            for k in range(8):
                pt = pT[k % 2]
                bb.tr(pt[:].bitcast(BF16)[:, 0:128],
                      xq16[:, k * 128:(k + 1) * 128], ident16[:],
                      r=(xq16, ident16), w=(pt,))
                bb.op("vector", "tensor_copy", xqT3[:, k, r0:r0 + 128],
                      pt[:].bitcast(BF16)[:, 0:128], r=(pt,), w=(xqT,))
    if "xqT" in dDbg:
        bb.dma("sync", "dbg", dDbg["xqT"][:], xqT[:], r=(xqT,))


def _quant_tern(nc, bb, out16, wtile, recip_ap, qf32, qi32, qg, sSw):
    bb.op("vector", "tensor_scalar", qf32[:], wtile[:], recip_ap, 1.4999,
          ALU.mult, ALU.min, r=(wtile, sSw), w=(qf32,))
    bb.op("vector", "tensor_scalar", qf32[:], qf32[:], -1.4999, 0.5,
          ALU.max, ALU.add, r=(qf32,), w=(qf32,))
    bb.op("scalar", "activation", qi32[:], qf32[:], AF.Copy,
          r=(qf32,), w=(qi32,))
    bb.op("vector", "tensor_tensor", qg[:], qi32[:], qf32[:], ALU.is_gt,
          r=(qi32, qf32), w=(qg,))
    bb.op("vector", "tensor_tensor", out16[:], qi32[:], qg[:], ALU.subtract,
          r=(qi32, qg), w=(out16,))


def _phase2_means(nc, bb, dW, sSw, sSt, dDbg, est):
    sOnes = sSt["ones16"]
    _c = est.enter_context
    pM = _c(nc.psum_tensor("pM", [128, 8], F32))
    mts = [_c(nc.sbuf_tensor(f"mt{i}", [128, 512], F32)) for i in range(4)]
    macc = _c(nc.sbuf_tensor("macc", [128, 4], F32))
    msum = _c(nc.sbuf_tensor("msum", [128, 1], F32))
    bb.op("vector", "memset", macc[:], 0.0, w=(macc,))
    for wi, wn in enumerate(("w_gate", "w_up", "w_down")):
        rows, cols = WEIGHT_SHAPES[wn]
        blocks = [(k * 128, cc * 512) for k in range(rows // 128)
                  for cc in range(cols // 512)]
        for bi, (rr, cc) in enumerate(blocks):
            wt = mts[bi % 4]
            bb.dma("sync", f"wm{bi % 4}", wt[:],
                   dW[wn][rr:rr + 128, cc:cc + 512], w=(wt,))
            bb.op("vector", "tensor_reduce", msum[:], wt[:], AX.X, ALU.add,
                  apply_absolute_value=True, r=(wt,), w=(msum,))
            if bi == 0:
                bb.op("vector", "tensor_copy", macc[:, wi:wi + 1], msum[:],
                      r=(msum,), w=(macc,))
            else:
                bb.op("vector", "tensor_tensor", macc[:, wi:wi + 1],
                      macc[:, wi:wi + 1], msum[:], ALU.add, r=(macc, msum),
                      w=(macc,))
    bb.mm([(pM[:, 0:4], sSt["ones32"][:], macc[:], True, True)],
          r=(sSt["ones32"], macc), w=(pM,))
    for wi, wn in enumerate(("w_gate", "w_up", "w_down")):
        rows, cols = WEIGHT_SHAPES[wn]
        bb.op("vector", "tensor_scalar", sSw[:, wi:wi + 1], pM[:, wi:wi + 1],
              1.0 / (rows * cols), 1e-5, ALU.mult, ALU.max, r=(pM,), w=(sSw,))
        bb.op("vector", "reciprocal", sSw[:, 3 + wi:4 + wi],
              sSw[:, wi:wi + 1], r=(sSw,), w=(sSw,))
    if "sw" in dDbg:
        bb.dma("sync", "dbg", dDbg["sw"][:], sSw[:], r=(sSw,))


def _phase2_gateup(nc, bb, dW, dH2U, xqT, sStats, sSw, sRm, sSt, dDbg):
    ident16 = sSt["ident16"]
    xqT3 = xqT[:].rearrange("p (a t) -> p a t", a=8, t=L)
    bb.barrier()
    _e = ExitStack()
    _c = _e.enter_context
    pGs = [_c(nc.psum_tensor(f"pG{i}", [128, 512], F32)) for i in range(2)]
    pUs = [_c(nc.psum_tensor(f"pU{i}", [128, 512], F32)) for i in range(2)]
    pTa = _c(nc.psum_tensor("pTe", [128, 64], F32))
    pTb = _c(nc.psum_tensor("pTf", [128, 64], F32))
    ternTg = _c(nc.sbuf_tensor("ternTg", [128, 8 * 512], BF16))
    ternTu = _c(nc.sbuf_tensor("ternTu", [128, 8 * 512], BF16))
    wtA = _c(nc.sbuf_tensor("wtA", [128, 1024], F32))
    wtB = _c(nc.sbuf_tensor("wtB", [128, 1024], F32))
    qf32s = [_c(nc.sbuf_tensor(f"qf32{i}", [128, 1024], F32)) for i in range(2)]
    qi32s = [_c(nc.sbuf_tensor(f"qi32{i}", [128, 1024], I32)) for i in range(2)]
    qgs = [_c(nc.sbuf_tensor(f"qg{i}", [128, 1024], BF16)) for i in range(2)]
    wq3A = _c(nc.sbuf_tensor("wq3A", [128, 1024], BF16))
    wq3B = _c(nc.sbuf_tensor("wq3B", [128, 1024], BF16))
    gtA = _c(nc.sbuf_tensor("gtA", [128, 512], BF16))
    gtB = _c(nc.sbuf_tensor("gtB", [128, 512], BF16))
    utA = _c(nc.sbuf_tensor("utA", [128, 512], BF16))
    utB = _c(nc.sbuf_tensor("utB", [128, 512], BF16))
    h2sA = _c(nc.sbuf_tensor("h2sA", [128, 512], BF16))
    h2sB = _c(nc.sbuf_tensor("h2sB", [128, 512], BF16))
    gsc = _c(nc.sbuf_tensor("gsc", [128, 1], F32))
    cmx = _c(nc.sbuf_tensor("cmx", [128, 1], F32))
    with _e:
        tg3 = ternTg[:].rearrange("p (a o) -> p a o", a=8, o=512)
        tu3 = ternTu[:].rearrange("p (a o) -> p a o", a=8, o=512)
        pT = (pTa, pTb)
        for sl in range(8):
            for wi, wn, tT3, tTh in ((0, "w_gate", tg3, ternTg),
                                     (1, "w_up", tu3, ternTu)):
                for j in range(4):
                    par = (wi * 4 + j) % 2
                    wt = wtA if par == 0 else wtB
                    wq3 = wq3A if par == 0 else wq3B
                    o0 = sl * 512 + j * 128
                    bb.dma("sync", f"wm{par}", wt[:], dW[wn][o0:o0 + 128, :],
                           w=(wt,))
                    _quant_tern(nc, bb, wq3, wt, sSw[:, 3 + wi:4 + wi],
                                qf32s[par], qi32s[par], qgs[par], sSw)
                    for k in range(8):
                        pt = pT[k % 2]
                        bb.tr(pt[:].bitcast(BF16)[:, 0:128],
                              wq3[:, k * 128:(k + 1) * 128], ident16[:],
                              r=(wq3, ident16), w=(pt,))
                        bb.op("scalar", "activation",
                              tT3[:, k, j * 128:(j + 1) * 128],
                              pt[:].bitcast(BF16)[:, 0:128], AF.Copy,
                              r=(pt,), w=(tTh,))
            for ti in range(NT):
                t0 = ti * 128
                par = ti % 2
                gt = gtA if par == 0 else gtB
                ut = utA if par == 0 else utB
                h2s = h2sA if par == 0 else h2sB
                pG = pGs[ti % 2]
                pU = pUs[ti % 2]
                bb.op("vector", "tensor_scalar", gsc[:], sStats[:, ti:ti + 1],
                      sSw[:, 0:1], None, ALU.mult, r=(sStats, sSw), w=(gsc,))
                bb.mm([(pG[:], xqT3[:, k, t0:t0 + 128], tg3[:, k, :], k == 0,
                        k == 7) for k in range(8)], r=(xqT, ternTg), w=(pG,))
                bb.mm([(pU[:], xqT3[:, k, t0:t0 + 128], tu3[:, k, :], k == 0,
                        k == 7) for k in range(8)], r=(xqT, ternTu), w=(pU,))
                bb.op("scalar", "activation", gt[:], pG[:], AF.Silu,
                      scale=gsc[:], r=(pG, gsc), w=(gt,))
                bb.op("scalar", "activation", ut[:], pU[:], AF.Copy,
                      r=(pU,), w=(ut,))
                bb.op("vector", "tensor_tensor", h2s[:], gt[:], ut[:],
                      ALU.mult, r=(gt, ut), w=(h2s,))
                if sl == 0:
                    bb.op("vector", "tensor_reduce", sRm[:, ti:ti + 1], h2s[:],
                          AX.X, ALU.max, apply_absolute_value=True,
                          r=(h2s,), w=(sRm,))
                else:
                    bb.op("vector", "tensor_reduce", cmx[:], h2s[:], AX.X,
                          ALU.max, apply_absolute_value=True, r=(h2s,),
                          w=(cmx,))
                    bb.op("vector", "tensor_tensor", sRm[:, ti:ti + 1],
                          sRm[:, ti:ti + 1], cmx[:], ALU.max, r=(sRm, cmx),
                          w=(sRm,))
                bb.dma("sync", "h2st",
                       dH2U[t0:t0 + 128, sl * 512:(sl + 1) * 512], h2s[:],
                       r=(h2s,))
    bb.barrier()


def _phase2_down(nc, bb, dW, dH2U, dFFN, sStats, sSw, sRm, sSt, dDbg):
    ident16 = sSt["ident16"]
    bb.barrier()
    _e = ExitStack()
    _c = _e.enter_context
    pD0 = _c(nc.psum_tensor("pD0", [128, 1024], F32))
    pD1 = _c(nc.psum_tensor("pD1", [128, 1024], F32))
    pTc = _c(nc.psum_tensor("pTc", [128, 64], F32))
    pTd = _c(nc.psum_tensor("pTd", [128, 64], F32))
    ternDT = _c(nc.sbuf_tensor("ternDT", [128, 32 * 1024], BF16))
    h2ts = [_c(nc.sbuf_tensor(f"h2t{i}", [128, 4096], BF16)) for i in range(2)]
    h2qTs = [_c(nc.sbuf_tensor(f"h2qT{i}", [128, 32 * 128], BF16))
             for i in range(2)]
    qf32s = [_c(nc.sbuf_tensor(f"dqf32{i}", [128, 1024], F32)) for i in range(2)]
    qi32s = [_c(nc.sbuf_tensor(f"dqi32{i}", [128, 1024], I32)) for i in range(2)]
    qgs = [_c(nc.sbuf_tensor(f"dqg{i}", [128, 1024], BF16)) for i in range(2)]
    qbs = [_c(nc.sbuf_tensor(f"dqb{i}", [128, 1024], BF16)) for i in range(2)]
    wtA = _c(nc.sbuf_tensor("dwtA", [128, 1024], F32))
    wtB = _c(nc.sbuf_tensor("dwtB", [128, 1024], F32))
    fotA = _c(nc.sbuf_tensor("fotA", [128, 1024], F32))
    s2c = _c(nc.sbuf_tensor("s2c", [128, 1], F32))
    duc = _c(nc.sbuf_tensor("duc", [128, 1], F32))
    with _e:
        td3 = ternDT[:].rearrange("p (a d) -> p a d", a=32, d=1024)
        pT = (pTc, pTd)
        for dtile in range(8):
            d0 = dtile * 128
            for cc in range(4):
                par = (dtile * 4 + cc) % 2
                wt = wtA if par == 0 else wtB
                bb.dma("sync", f"wm{par}", wt[:],
                       dW["w_down"][d0:d0 + 128, cc * 1024:(cc + 1) * 1024],
                       w=(wt,))
                _quant_tern(nc, bb, qbs[par], wt, sSw[:, 5:6], qf32s[par],
                            qi32s[par], qgs[par], sSw)
                for k in range(8):
                    hc = cc * 8 + k
                    pt = pT[k % 2]
                    bb.tr(pt[:].bitcast(BF16)[:, 0:128],
                          qbs[par][:, k * 128:(k + 1) * 128], ident16[:],
                          r=(qbs[par], ident16), w=(pt,))
                    bb.op("scalar", "activation", td3[:, hc, d0:d0 + 128],
                          pt[:].bitcast(BF16)[:, 0:128], AF.Copy,
                          r=(pt,), w=(ternDT,))
        for ti in range(NT):
            t0 = ti * 128
            pD = pD0 if ti % 2 == 0 else pD1
            fot = fotA
            h2t = h2ts[ti % 2]
            h2qT = h2qTs[ti % 2]
            bb.dma("sync", "wm0", h2t[:], dH2U[t0:t0 + 128, :], w=(h2t,),
                   after=tuple(bb.qtokens("h2st")) if ti == 0 else ())
            bb.op("vector", "tensor_scalar", s2c[:], sRm[:, ti:ti + 1], 1e-5,
                  None, ALU.max, r=(sRm,), w=(s2c,))
            bb.op("vector", "tensor_scalar", duc[:], sStats[:, ti:ti + 1],
                  sSw[:, 1:2], None, ALU.mult, r=(sStats, sSw), w=(duc,))
            bb.op("vector", "tensor_scalar", duc[:], duc[:], sSw[:, 2:3],
                  1.0 / 127.0, ALU.mult, ALU.mult, r=(duc, sSw), w=(duc,))
            bb.op("vector", "tensor_tensor", sStats[:, NT + ti:NT + ti + 1],
                  duc[:], s2c[:], ALU.mult, r=(duc, s2c), w=(sStats,))
            bb.op("vector", "reciprocal", s2c[:], s2c[:], r=(s2c,), w=(s2c,))
            bb.op("vector", "tensor_scalar", s2c[:], s2c[:], 127.0, None,
                  ALU.mult, r=(s2c,), w=(s2c,))
            for qc in range(4):
                qf32, qi32, qg, qb = (qf32s[qc % 2], qi32s[qc % 2],
                                      qgs[qc % 2], qbs[qc % 2])
                bb.op("vector", "tensor_scalar", qf32[:],
                      h2t[:, qc * 1024:(qc + 1) * 1024], s2c[:], 0.5,
                      ALU.mult, ALU.add, r=(h2t, s2c), w=(qf32,))
                bb.op("scalar", "activation", qi32[:], qf32[:], AF.Copy,
                      r=(qf32,), w=(qi32,))
                bb.op("vector", "tensor_tensor", qg[:], qi32[:], qf32[:],
                      ALU.is_gt, r=(qi32, qf32), w=(qg,))
                bb.op("vector", "tensor_tensor", qb[:], qi32[:], qg[:],
                      ALU.subtract, r=(qi32, qg), w=(qb,))
                for k in range(8):
                    hc = qc * 8 + k
                    pt = pT[hc % 2]
                    bb.tr(pt[:].bitcast(BF16)[:, 0:128],
                          qb[:, k * 128:(k + 1) * 128], ident16[:],
                          r=(qb, ident16), w=(pt,))
                    bb.op("vector", "tensor_copy",
                          h2qT[:, hc * 128:(hc + 1) * 128],
                          pt[:].bitcast(BF16)[:, 0:128], r=(pt,), w=(h2qT,))
            ch = []
            for hc in range(32):
                for nch in range(2):
                    ch.append((pD[:, nch * 512:(nch + 1) * 512],
                               h2qT[:, hc * 128:(hc + 1) * 128],
                               td3[:, hc, nch * 512:(nch + 1) * 512],
                               hc == 0, hc == 31))
            bb.mm(ch, r=(h2qT, ternDT), w=(pD,))
            bb.op("scalar", "activation", fot[:], pD[:], AF.Copy,
                  scale=sStats[:, NT + ti:NT + ti + 1], r=(pD, sStats),
                  w=(fot,))
            bb.dma("sync", "st", dFFN[t0:t0 + 128, :], fot[:], r=(fot,))
    bb.barrier()
    if "stats" in dDbg:
        bb.dma("sync", "dbg", dDbg["stats"][:], sStats[:], r=(sStats,))
    if "h2u" in dDbg:
        _dump_dram(nc, bb, dDbg["h2u"], dH2U, BF16, 4096)
    if "ffnout" in dDbg:
        _dump_dram(nc, bb, dDbg["ffnout"], dFFN, F32, D)


_DUMPN = [0]


def _dump_dram(nc, bb, dbg_ap, src_ap, dt_, cols):
    rows = src_ap.shape[0]
    _DUMPN[0] += 1
    with nc.sbuf_tensor(f"dumptmp{_DUMPN[0]}", [128, cols], dt_) as tmp:
        for k in range(rows // 128):
            bb.dma("sync", "ld", tmp[:], src_ap[k * 128:(k + 1) * 128, :], w=(tmp,))
            bb.dma("sync", "dbg", dbg_ap[k * 128:(k + 1) * 128, :], tmp[:], r=(tmp,))
    bb.barrier()


def _phase3_proj(nc, bb, dW, dS, xnT, sCkvT, sCkv, sScr, sKiT, sQiT, sWuv,
                 sWout8, sBK, sBK0, sSt, dCKV, dQT, dMUQ, dDbg):
    xnT3 = xnT[:].rearrange("p (a t) -> p a t", a=8, t=L)
    ident16 = sSt["ident16"]
    ident32 = sSt["ident32"]
    _e = ExitStack()
    _c = _e.enter_context
    pA = _c(nc.psum_tensor("pA", [128, 512], F32))
    pB = _c(nc.psum_tensor("pB", [128, 512], F32))
    pC = _c(nc.psum_tensor("pC", [128, 512], F32))
    pD = _c(nc.psum_tensor("pD", [128, 512], F32))
    pE = _c(nc.psum_tensor("pE", [128, 512], F32))
    WBUF = _c(nc.sbuf_tensor("WBUF", [128, 8 * 512], BF16))
    WUK = _c(nc.sbuf_tensor("WUK", [128, 2 * 512], BF16))
    WUKT = _c(nc.sbuf_tensor("WUKT", [128, 8 * 128], BF16))
    QCT = _c(nc.sbuf_tensor("QCT", [128, 4 * L], BF16))
    CQT = _c(nc.sbuf_tensor("CQT", [128, 4 * L], BF16))
    MUQ32 = _c(nc.sbuf_tensor("MUQ32", [128, 2 * L], BF16))
    MKT = _c(nc.sbuf_tensor("MKT", [32, L], BF16))
    spA = _c(nc.sbuf_tensor("spA", [128, 512], F32))
    spB = _c(nc.sbuf_tensor("spB", [128, 512], F32))
    v64 = _c(nc.sbuf_tensor("v64", [64, 512], BF16))
    v64b = _c(nc.sbuf_tensor("v64b", [64, 512], BF16))
    qstage = _c(nc.sbuf_tensor("qstage", [128, 4096], BF16))
    tq1 = _c(nc.sbuf_tensor("s_tq1", [64, L], BF16))
    tq2 = _c(nc.sbuf_tensor("s_tq2", [64, L], BF16))
    trigb = _c(nc.sbuf_tensor("s_trigb", [64, L], BF16))
    for _nm, _h in (("tq1", tq1), ("tq2", tq2), ("trigb", trigb)):
        bb.dma("sync", "ld", _h[:], dS[_nm][:], w=(_h,))
    if True:
        def wload(name, tiles, cols):
            for k in range(tiles):
                bb.dma("gpsimd", "cast", WBUF[:, k * cols:(k + 1) * cols],
                       dW[name][k * 128:(k + 1) * 128, :], w=(WBUF,))
            return WBUF
        wload2 = wload
        for k in range(2):
            bb.dma("gpsimd", "cast", WUK[:, k * 512:(k + 1) * 512],
                   dW["W_uk"][k * 128:(k + 1) * 128, :], w=(WUK,))
        for g in range(2):
            bb.dma("gpsimd", "cast", sWuv[g][:],
                   dW["W_uv"][g * 128:(g + 1) * 128, :], w=(sWuv[g],))
        for h in range(NH):
            bb.dma("gpsimd", "cast", sWout8[h][:],
                   dW["W_out"][h * 64:(h + 1) * 64, :], w=(sWout8[h],))

        WDKV = wload("W_dkv", 8, 256)
        for g in range(2):
            for nk in range(4):
                px = pA if nk % 2 == 0 else pD
                bb.mm([(px[:], WDKV[:, k * 256 + g * 128:k * 256 + g * 128 + 128],
                        xnT3[:, k, nk * 512:(nk + 1) * 512], k == 0, k == 7)
                       for k in range(8)], r=(WDKV, xnT), w=(px,))
                bb.op("vector", "tensor_copy",
                      sCkvT[g][:, nk * 512:(nk + 1) * 512], px[:], r=(px,),
                      w=(sCkvT[g],))
        for ti in range(NT):
            r0 = ti * 128
            px = pB if ti % 2 == 0 else pE
            bb.mm([(px[:, 0:256], xnT3[:, k, r0:r0 + 128],
                    WDKV[:, k * 256:(k + 1) * 256], k == 0, k == 7)
                   for k in range(8)], r=(WDKV, xnT), w=(px,))
            bb.op("vector", "tensor_copy", sCkv[ti][:], px[:, 0:256], r=(px,),
                  w=(sCkv[ti],))
            bb.op("vector", "tensor_copy", sScr[:, ti * 384:ti * 384 + 256],
                  sCkv[ti][:], r=(sCkv[ti],), w=(sScr,))
            bb.dma("sync", "st", dCKV[r0:r0 + 128, :], sCkv[ti][:], r=(sCkv[ti],))
        WDQ = wload("W_dq", 8, 512)
        for mi in range(4):
            for nk in range(4):
                px = pA if nk % 2 == 0 else pD
                bb.mm([(px[:], WDQ[:, k * 512 + mi * 128:k * 512 + mi * 128 + 128],
                        xnT3[:, k, nk * 512:(nk + 1) * 512], k == 0, k == 7)
                       for k in range(8)], r=(WDQ, xnT), w=(px,))
                bb.op("vector", "tensor_copy",
                      CQT[:, mi * L + nk * 512:mi * L + (nk + 1) * 512], px[:],
                      r=(px,), w=(CQT,))
        WUQ = wload("W_uq", 4, 512)
        cqT3 = CQT[:].rearrange("p (a t) -> p a t", a=4, t=L)
        for mi in range(4):
            for nk in range(4):
                px = pA if nk % 2 == 0 else pD
                bb.mm([(px[:], WUQ[:, k * 512 + mi * 128:k * 512 + mi * 128 + 128],
                        cqT3[:, k, nk * 512:(nk + 1) * 512], k == 0, k == 3)
                       for k in range(4)], r=(WUQ, CQT), w=(px,))
                bb.op("vector", "tensor_copy",
                      QCT[:, mi * L + nk * 512:mi * L + (nk + 1) * 512], px[:],
                      r=(px,), w=(QCT,))
        for g in range(2):
            for hp in range(4):
                bb.tr(pC[:].bitcast(BF16)[:, 0:128],
                      WUK[:, g * 512 + hp * 128:g * 512 + hp * 128 + 128],
                      ident16[:], r=(WUK, ident16), w=(pC,))
                bb.op("vector", "tensor_copy",
                      WUKT[:, (g * 4 + hp) * 128:(g * 4 + hp + 1) * 128],
                      pC[:].bitcast(BF16)[:, 0:128], r=(pC,), w=(WUKT,))
        qst3 = qstage[:].rearrange("p (t h) -> p t h", t=512, h=8)
        for g in range(2):
            for nk in range(4):
                for h in range(NH):
                    base = (h % 2) * 64
                    lhs = WUKT[base:base + 64,
                               (g * 4 + h // 2) * 128:(g * 4 + h // 2 + 1) * 128]
                    rhs = QCT[base:base + 64,
                              (h // 2) * L + nk * 512:(h // 2) * L + (nk + 1) * 512]
                    px = pA if h % 2 == 0 else pD
                    bb.mm([(px[:], lhs, rhs, True, True)], r=(WUKT, QCT), w=(px,))
                    bb.op("scalar", "activation", qst3[:, :, h], px[:], AF.Copy,
                          scale=SCALE, r=(px,), w=(qstage,))
                bb.dma("sync", "st", dQT[g, :, nk * 4096:(nk + 1) * 4096],
                       qstage[:, 0:4096], r=(qstage,))
        WQR = wload("W_qr", 4, 256)
        for half in range(2):
            for nk in range(4):
                px = pA if nk % 2 == 0 else pD
                bb.mm([(px[:], WQR[:, k * 256 + half * 128:k * 256 + half * 128 + 128],
                        cqT3[:, k, nk * 512:(nk + 1) * 512], k == 0, k == 3)
                       for k in range(4)], r=(WQR, CQT), w=(px,))
                _softplus(nc, bb, MUQ32[:, half * L + nk * 512:half * L + (nk + 1) * 512],
                          px, 128, spA, spB, MUQ32)
        dupr = sSt["dupr"]
        c2t = sSt["c2"]
        for var, tq_h in ((0, tq1), (1, tq2)):
            tq = tq_h
            for nk in range(4):
                for h in range(NH):
                    pb = 32 * (h % 4)
                    rhs = MUQ32[pb:pb + 32,
                                (h // 4) * L + nk * 512:(h // 4) * L + (nk + 1) * 512]
                    px = pB if h % 2 == 0 else pE
                    py = pC if h % 2 == 0 else pD
                    vv = v64 if h % 2 == 0 else v64b
                    bb.mm([(px[0:64, :], dupr[pb:pb + 32, :], rhs, True, True)],
                          r=(dupr, MUQ32), w=(px,),
                          tile_position=(pb, 0) if pb == 96 else None)
                    bb.op("vector", "tensor_tensor", vv[:], px[0:64, :],
                          tq[:, nk * 512:(nk + 1) * 512], ALU.mult,
                          r=(px, tq), w=(vv,))
                    bb.mm([(py[0:64, :], c2t[:], vv[:], True, True)],
                          r=(c2t, vv), w=(py,))
                    bb.op("scalar", "activation", qst3[0:64, :, h], py[0:64, :],
                          AF.Copy, scale=SCALE, r=(py,), w=(qstage,))
                bb.dma("sync", "st", dMUQ[var, :, nk * 4096:(nk + 1) * 4096],
                       qstage[0:64, 0:4096], r=(qstage,))
        IWQ = wload("idx_wq", 8, 256)
        for g in range(2):
            for nk in range(4):
                bb.mm([(pA[:], IWQ[:, k * 256 + g * 128:k * 256 + g * 128 + 128],
                        xnT3[:, k, nk * 512:(nk + 1) * 512], k == 0, k == 7)
                       for k in range(8)], r=(IWQ, xnT), w=(pA,))
                bb.op("vector", "tensor_copy",
                      sQiT[g][:, nk * 512:(nk + 1) * 512], pA[:], r=(pA,),
                      w=(sQiT[g],))
        IWK = wload("idx_wk", 8, 64)
        for hr in range(2):
            for nk in range(4):
                bb.mm([(pA[hr * 64:hr * 64 + 64, :], IWK[:, k * 64:(k + 1) * 64],
                        xnT3[:, k, nk * 512:(nk + 1) * 512], k == 0, k == 7)
                       for k in range(8)], r=(IWK, xnT), w=(pA,))
                bb.op("vector", "tensor_copy",
                      sKiT[hr * 64:hr * 64 + 64, nk * 512:(nk + 1) * 512],
                      pA[hr * 64:hr * 64 + 64, :], r=(pA,), w=(sKiT,))
        WKR = wload("W_kr", 8, 32)
        for nk in range(4):
            bb.mm([(pA[0:32, :], WKR[:, k * 32:(k + 1) * 32],
                    xnT3[:, k, nk * 512:(nk + 1) * 512], k == 0, k == 7)
                   for k in range(8)], r=(WKR, xnT), w=(pA,))
            _softplus(nc, bb, MKT[:, nk * 512:(nk + 1) * 512], pA, 32, spA, spB, MKT)
        for nk in range(4):
            bb.mm([(pB[0:64, :], dupr[0:32, :], MKT[:, nk * 512:(nk + 1) * 512],
                    True, True)], r=(dupr, MKT), w=(pB,))
            bb.op("vector", "tensor_tensor", sBK[:, nk * 512:(nk + 1) * 512],
                  pB[0:64, :], trigb[:, nk * 512:(nk + 1) * 512], ALU.mult,
                  r=(pB, trigb), w=(sBK,))
        bb.mm([(pB[0:64, 0:192], dupr[0:32, :], MKT[:, 0:192], True, True)],
              r=(dupr, MKT), w=(pB,))
        bb.op("vector", "tensor_tensor", sBK0[:], pB[0:64, 0:192],
              sSt["trigb0"][:], ALU.mult, r=(pB, sSt["trigb0"]), w=(sBK0,))
        for ti in range(NT):
            r0 = ti * 128
            bb.tr(pC[:].bitcast(BF16)[:, 0:32], MKT[:, r0:r0 + 128],
                  ident16[0:32, 0:32], r=(MKT, ident16), w=(pC,))
            mu = pC[:].bitcast(BF16)[:, 0:32]
            for dupi in range(2):
                dst = sScr[:, ti * 384 + 256 + dupi * 32:ti * 384 + 256 + dupi * 32 + 32]
                bb.op("vector", "tensor_copy",
                      dst.rearrange("p (a b) -> p a b", a=2, b=16),
                      mu[:, dupi * 16:dupi * 16 + 16]
                      .rearrange("p (a b) -> p a b", a=1, b=16)
                      .broadcast_to([128, 2, 16]),
                      r=(pC,), w=(sScr,))
    bb.barrier()
    _e.close()
    for nm, src in (("ckvT0", sCkvT[0]), ("kiT", sKiT), ("scr", sScr),
                    ("bk", sBK), ("qiT0", sQiT[0])):
        if nm in dDbg:
            bb.dma("sync", "dbg", dDbg[nm][:], src[:], r=(src,))
    bb.barrier()


def _softplus(nc, bb, out_ap, psum, rows, spA, spB, outbuf):
    src = psum[0:rows, :]
    a = spA[0:rows, :]
    b = spB[0:rows, :]
    bb.op("scalar", "activation", a, src, AF.Abs, r=(psum,), w=(spA,))
    bb.op("scalar", "activation", b, a, AF.Exp, scale=-1.0, r=(spA,), w=(spB,))
    bb.op("scalar", "activation", b, b, AF.Ln, bias=1.0, r=(spB,), w=(spB,))
    bb.op("vector", "scalar_tensor_tensor", out_ap, src, 0.0, b, ALU.max,
          ALU.add, r=(psum, spB), w=(outbuf,))


def _phase4_attn(nc, bb, dX, dOut, dFFN, dCKV, dQT, dMUQ, sCkvT, sCkv, sScr,
                 sKiT, sQiT, sWuv, sWout8, sBK, sBK0, sSt, dS, sStats, sIdxw,
                 dDbg):
    P4T = int(os.environ.get("MK_P4TILES", "16"))
    P4S = int(os.environ.get("MK_P4STEP", "99"))
    ident16 = sSt["ident16"]
    ident32 = sSt["ident32"]
    _e = ExitStack()
    _c = _e.enter_context
    pI = _c(nc.psum_tensor("pI", [128, 512], F32))
    pBand = _c(nc.psum_tensor("pBand", [128, 192], F32))
    pGath = _c(nc.psum_tensor("pGath", [128, 512], F32))
    pT32 = _c(nc.psum_tensor("pT32", [128, 128], F32))
    pT16 = _c(nc.psum_tensor("pT16", [128, 256], BF16))
    pLat = _c(nc.psum_tensor("pLat", [128, 256], F32))
    pOut = _c(nc.psum_tensor("pOut", [128, 1024], F32))
    sFut = _c(nc.sbuf_tensor("s4_fut", [128, L], F32))
    sPert = _c(nc.sbuf_tensor("s4_pert", [128, L], F32))
    sSt = dict(sSt, fut=sFut, pert=sPert)
    bb.dma("sync", "ld", sFut[:], dS["fut"][:], w=(sFut,))
    bb.dma("sync", "ld", sPert[:], dS["pert"][:], w=(sPert,))
    QTS = _c(nc.sbuf_tensor("QTS", [128, 2 * NH * 128], BF16))
    QTS2 = _c(nc.sbuf_tensor("QTS2", [128, 2 * NH * 128], BF16))
    MQS = _c(nc.sbuf_tensor("MQS", [128, 2 * NH * 128], BF16))
    MQS2 = _c(nc.sbuf_tensor("MQS2", [128, 2 * NH * 128], BF16))
    iacc = _c(nc.sbuf_tensor("iacc", [128, L], F32))
    tmp16 = _c(nc.sbuf_tensor("tmp16", [128, L], BF16))
    msk = _c(nc.sbuf_tensor("msk", [128, L], F32))
    scrA = _c(nc.sbuf_tensor("scrA", [128, 8 * GW], F32))
    ppan = _c(nc.sbuf_tensor("ppan", [128, 8 * GW], BF16))
    gT = _c(nc.sbuf_tensor("gT", [128, 3 * 4096], BF16))
    gV = _c(nc.sbuf_tensor("gV", [128, 32 * 256], BF16))
    vals = _c(nc.sbuf_tensor("vals", [128, 32], F32))
    idxu = _c(nc.sbuf_tensor("idxu", [128, 32], U32))
    idxf = _c(nc.sbuf_tensor("idxf", [128, 32], F32))
    idxS = _c(nc.sbuf_tensor("idxS", [16, 256], F32))
    idxA = _c(nc.sbuf_tensor("idxA", [128, 256], I16))
    idxB = _c(nc.sbuf_tensor("idxB", [128, 256], I16))
    idxAc = [_c(nc.sbuf_tensor(f"idxAc{c}", [128, 32], I16)) for c in range(8)]
    idxBc = [_c(nc.sbuf_tensor(f"idxBc{c}", [128, 64], I16)) for c in range(4)]
    pbTA = _c(nc.sbuf_tensor("pbTA", [128, 128], BF16))
    pbTB = _c(nc.sbuf_tensor("pbTB", [64, 128], BF16))
    pgT = _c(nc.sbuf_tensor("pgT", [128, 128], BF16))
    olt0 = _c(nc.sbuf_tensor("olt0", [128, NH * 128], BF16))
    olt1 = _c(nc.sbuf_tensor("olt1", [128, NH * 128], BF16))
    oat = _c(nc.sbuf_tensor("oat", [64, NH * 128], BF16))
    xt = _c(nc.sbuf_tensor("xt", [128, D], F32))
    ft = _c(nc.sbuf_tensor("ft", [128, D], F32))
    smax = _c(nc.sbuf_tensor("smax", [128, 8], F32))
    ssum = _c(nc.sbuf_tensor("ssum", [128, 8], F32))
    if True:
        olt = [olt0, olt1]
        bb.op("vector", "memset", iacc[:], 0.0, w=(iacc,))
        bb.op("vector", "memset", msk[:], 0.0, w=(msk,))
        nidx_reg = nc.gpsimd.alloc_register("nidx")
        nc.gpsimd.reg_mov(nidx_reg, 1024)
        nidx512 = nc.gpsimd.alloc_register("nidx512")
        nc.gpsimd.reg_mov(nidx512, 512)
        bb.op("vector", "memset", idxA[:], 0, w=(idxA,))
        bb.op("vector", "memset", idxB[:], 0, w=(idxB,))
        gT6 = gT[:].rearrange("p (rh c2 j q rl) -> p rh c2 j q rl",
                              rh=2, c2=4, j=3, q=32, rl=16)
        gV3 = gV[:].rearrange("p (c e) -> p c e", c=32, e=256)
        trigk_b = (sSt["trigk"][:].rearrange("p (rh rl) -> p rh rl", rh=2, rl=16)
                   .unsqueeze(2).broadcast_to([64, 2, 128, 16]))
        QTSb = (QTS, QTS2)
        MQSb = (MQS, MQS2)
        st8 = {"prev": None, "g1": [], "g2": []}

        def emit_A1(ti):
            t0 = ti * 128
            QTSl, MQSl = QTSb[ti % 2], MQSb[ti % 2]
            for g in range(2):
                bb.dma("sync", "ld", QTSl[:, g * 1024:(g + 1) * 1024],
                       dQT[g, :, t0 * 8:t0 * 8 + 1024], w=(QTSl,))
            for var in range(2):
                bb.dma("sync", "ld", MQSl[0:64, var * 1024:(var + 1) * 1024],
                       dMUQ[var, :, t0 * 8:t0 * 8 + 1024], w=(MQSl,))
            Wm = 128 if ti == 0 else t0 + 64
            nI = (Wm + 511) // 512
            for h in range(NIH):
                base = (h % 2) * 64
                lhs = sQiT[h // 2][base:base + 64, t0:t0 + 128]
                for nk in range(nI):
                    wseg = min(512, Wm - nk * 512)
                    pix = pI if (h * nI + nk) % 2 == 0 else pGath
                    bb.mm([(pix[:, 0:wseg], lhs,
                            sKiT[base:base + 64, nk * 512:nk * 512 + wseg],
                            True, True)], r=(sQiT[h // 2], sKiT), w=(pix,))
                    bb.op("scalar", "activation", tmp16[:, nk * 512:nk * 512 + wseg],
                          pix[:, 0:wseg], AF.Relu, r=(pix,), w=(tmp16,))
                if h == 0:
                    bb.op("vector", "scalar_tensor_tensor", iacc[:, 0:Wm],
                          tmp16[:, 0:Wm], sIdxw[:, 0:1], sSt["pert"][:, 0:Wm],
                          ALU.mult, ALU.add, r=(tmp16, sIdxw, sSt["pert"]),
                          w=(iacc,))
                else:
                    bb.op("vector", "scalar_tensor_tensor", iacc[:, 0:Wm],
                          tmp16[:, 0:Wm], sIdxw[:, h:h + 1], iacc[:, 0:Wm],
                          ALU.mult, ALU.add, r=(tmp16, sIdxw, iacc), w=(iacc,))
            if ti == 1 and "iacc1" in dDbg:
                bb.dma("sync", "dbg", dDbg["iacc1"][:], iacc[:], r=(iacc,))
            bb.op("gpsimd", "affine_select", msk[:, 0:Wm], iacc[:, 0:Wm],
                  pattern=[[-1, Wm]], compare_op=ALU.is_ge, fill=-2.0e8,
                  base=t0 - 64, channel_multiplier=1, r=(iacc,), w=(msk,))
            bb.op("gpsimd", "affine_select", iacc[:, 0:Wm], sSt["fut"][:, 0:Wm],
                  pattern=[[1, Wm]], compare_op=ALU.is_ge, fill=0.0,
                  base=-t0 - 1, channel_multiplier=-1, r=(sSt["fut"],),
                  w=(iacc,))
            bb.op("vector", "tensor_tensor", msk[:, 0:Wm], msk[:, 0:Wm],
                  iacc[:, 0:Wm], ALU.add, r=(msk, iacc), w=(msk,))
            for it in range(4):
                bb.op("vector", "max", out=vals[:, it * 8:(it + 1) * 8],
                      in_=msk[:, 0:Wm], r=(msk,), w=(vals,))
                bb.op("vector", "max_index", out=idxu[:, it * 8:(it + 1) * 8],
                      in_max=vals[:, it * 8:(it + 1) * 8], in_values=msk[:, 0:Wm],
                      r=(msk, vals), w=(idxu,))
                if it < 3:
                    bb.op("vector", "match_replace", out=msk[:, 0:Wm],
                          in_to_replace=vals[:, it * 8:(it + 1) * 8],
                          in_values=msk[:, 0:Wm], imm_value=REPL, r=(msk, vals),
                          w=(msk,))
            if ti == 1 and "idx1" in dDbg:
                bb.dma("sync", "dbg", dDbg["idx1"][:], idxu[:], r=(idxu,))

        def emit_A1b(ti):
            bb.op("vector", "tensor_copy", idxf[:], idxu[:], r=(idxu,), w=(idxf,))
            rep = sSt["rep16"]
            for hr in range(2):
                bb.tr(pT32[0:16, 0:128], idxf[:, hr * 16:(hr + 1) * 16],
                      ident32[:], r=(idxf, ident32), w=(pT32,))
                bb.op("vector", "tensor_copy",
                      idxS[0:16, hr * 128:(hr + 1) * 128], pT32[0:16, 0:128],
                      r=(pT32,), w=(idxS,))
                bb.mm([(pT32[:, 0:128], rep[:],
                        idxS[0:16, hr * 128:(hr + 1) * 128], True, True)],
                      r=(sSt["rep16"], idxS), w=(pT32,))
                bb.op("vector", "tensor_copy",
                      idxA[:, hr * 128:(hr + 1) * 128], pT32[:, 0:128],
                      r=(pT32,), w=(idxA,))
            bb.op("vector", "tensor_copy",
                  idxB[:].rearrange("p (q r) -> p q r", q=128, r=2),
                  idxA[:].rearrange("p (r q) -> p r q", r=2, q=128)
                  .transpose([0, 2, 1]), r=(idxA,), w=(idxB,))
            for c in range(8):
                bb.op("vector", "tensor_copy", idxAc[c][:],
                      idxA[:, c * 32:(c + 1) * 32], r=(idxA,), w=(idxAc[c],))
            for c in range(4):
                bb.op("vector", "tensor_copy", idxBc[c][:],
                      idxB[:, c * 64:(c + 1) * 64], r=(idxB,), w=(idxBc[c],))

        def emit_G1(ti):
            if st8["prev"] is not None:
                bb.wait("gpsimd", st8["prev"])
            st8["g1"] = []
            for c in range(8):
                tok = bb.gather("gq", r=(sScr, idxAc[c]), w=(gT,),
                                out_ap=gT[:, c * 1536:(c + 1) * 1536]
                                .rearrange("p (j n) -> p j n", j=3, n=512),
                                in_ap=sScr[:], idxs_ap=idxAc[c][:],
                                num_idxs=512, num_idxs_reg=nidx512, elem_size=384,
                                transpose=True, sbuf_tokens_per_rank=128,
                                sbuf_free_dim_per_rank=768,
                                sbuf_free_dim_pad_per_rank=0, sbuf_byte_offset=0)
                st8["g1"].append(tok)

        def emit_G2(ti):
            for c in range(8):
                mu_sl = gT[0:64, c * 1536 + 1024:c * 1536 + 1536].rearrange(
                    "p (q rl) -> p q rl", q=32, rl=16)
                tk_sl = (sSt["trigk"][0:64, (c // 4) * 16:(c // 4) * 16 + 16]
                         .unsqueeze(1).broadcast_to([64, 32, 16]))
                bb.op("vector", "tensor_tensor", mu_sl, mu_sl, tk_sl, ALU.mult,
                      r=(gT, sSt["trigk"]), w=(gT,), after=tuple(st8["g1"]))
            st8["g2"] = []
            for c in range(4):
                tok = bb.gather(
                    "gq", r=(idxBc[c],), w=(gV,),
                    out_ap=gV3[:, c * 8:(c + 1) * 8, :], in_ap=dCKV[:],
                    idxs_ap=idxBc[c][:], num_idxs=1024,
                    num_idxs_reg=nidx_reg, elem_size=256)
                st8["g2"].append(tok)
            st8["prev"] = tok

        def emit_B1(ti):
            t0 = ti * 128
            QTSl, MQSl = QTSb[ti % 2], MQSb[ti % 2]
            g1toks = tuple(st8["g1"])
            for g in range(8):
                a, width, mname, qside = band_params(ti, g)
                off = g * 16
                lq = [QTSl[:, gg * 1024 + off * 8:gg * 1024 + off * 8 + 128]
                      for gg in range(2)]
                lm = [MQSl[0:64, v * 1024 + off * 8:v * 1024 + off * 8 + 128]
                      for v in range(2)]
                w0 = 128 * a
                if qside == 1:
                    bk_ap = sBK0[:, 0:width]
                else:
                    bk_ap = sBK[:, w0:w0 + width]
                bb.mm([(pBand[:, 0:width], lq[0], sCkvT[0][:, w0:w0 + width],
                        True, False),
                       (pBand[:, 0:width], lq[1], sCkvT[1][:, w0:w0 + width],
                        False, False),
                       (pBand[:, 0:width], lm[qside - 1], bk_ap, False, True)],
                      r=(QTSl, MQSl, sCkvT[0], sCkvT[1],
                         sBK0 if qside == 1 else sBK),
                      w=(pBand,))
                mask = sSt[mname]
                bb.op("vector", "scalar_tensor_tensor",
                      scrA[:, g * GW:g * GW + width], pBand[:, 0:width], 1.0,
                      mask[:, 0:width], ALU.mult, ALU.add, r=(pBand, mask),
                      w=(scrA,))
                if width < 192:
                    bb.op("vector", "memset", scrA[:, g * GW + width:g * GW + 192],
                          BIGNEG, w=(scrA,))
                if qside == 1:
                    bb.op("vector", "memset", scrA[:, g * GW + 192:(g + 1) * GW],
                          BIGNEG, w=(scrA,))
                else:
                    nap = [gT6[:, :, g // 2, jj,
                               (g % 2) * 16:(g % 2) * 16 + 16, :]
                           .transpose([0, 2, 1, 3]) for jj in range(3)]
                    pg = pGath if g % 2 == 0 else pI
                    bb.mm([(pg[:], lq[0], nap[0], True, False),
                           (pg[:], lq[1], nap[1], False, False),
                           (pg[:], lm[0], nap[2][0:64], False, True)],
                          r=(QTSl, MQSl, gT), w=(pg,), after=g1toks)
                    dg = sSt["diagmask"]
                    for Bq in range(4):
                        bb.op("vector", "scalar_tensor_tensor",
                              scrA[Bq * 32:(Bq + 1) * 32,
                                   g * GW + 192:(g + 1) * GW],
                              pg[Bq * 32:(Bq + 1) * 32, Bq * 128:(Bq + 1) * 128],
                              1.0, dg[Bq * 32:(Bq + 1) * 32, :], ALU.mult,
                              ALU.add, r=(pg, dg), w=(scrA,))
            if ti == 1 and "gt1" in dDbg:
                bb.dma("sync", "dbg", dDbg["gt1"][:], gT[:], r=(gT,))

        def emit_B2(ti):
            sc3 = scrA[:].rearrange("p (g n) -> p g n", g=8, n=GW)
            bb.op("vector", "tensor_reduce", smax[:], sc3, AX.X, ALU.max,
                  r=(scrA,), w=(smax,))
            bb.op("vector", "tensor_tensor", sc3, sc3,
                  smax[:].unsqueeze(2).broadcast_to([128, 8, GW]), ALU.subtract,
                  r=(scrA, smax), w=(scrA,))
            bb.op("scalar", "activation", ppan[:], scrA[:], AF.Exp, r=(scrA,),
                  w=(ppan,))
            pp3 = ppan[:].rearrange("p (g n) -> p g n", g=8, n=GW)
            bb.op("vector", "tensor_reduce", ssum[:], pp3, AX.X, ALU.add,
                  r=(ppan,), w=(ssum,))
            bb.op("vector", "reciprocal", ssum[:], ssum[:], r=(ssum,), w=(ssum,))
            bb.op("vector", "tensor_tensor", pp3, pp3,
                  ssum[:].unsqueeze(2).broadcast_to([128, 8, GW]), ALU.mult,
                  r=(ppan, ssum), w=(ppan,))
            if ti == 1:
                for nm, src in (("scr1", scrA), ("ppan1", ppan)):
                    if nm in dDbg:
                        bb.dma("sync", "dbg", dDbg[nm][:], src[:], r=(src,))

        def emit_B3(ti):
            g2toks = tuple(st8["g2"])
            for g in range(8):
                a, width, mname, qside = band_params(ti, g)
                gb = g * GW
                bb.tr(pT16[0:128, 0:128], ppan[:, gb:gb + 128], ident16[:],
                      r=(ppan, ident16), w=(pT16,))
                bb.op("scalar", "activation", pbTA[:], pT16[:, 0:128],
                      AF.Copy, r=(pT16,), w=(pbTA,))
                if width == 192:
                    bb.tr(pT16[0:64, 128:256], ppan[:, gb + 128:gb + 192],
                          ident16[:], r=(ppan, ident16), w=(pT16,))
                    bb.op("scalar", "activation", pbTB[:], pT16[0:64, 128:256],
                          AF.Copy, r=(pT16,), w=(pbTB,))
                if qside == 2:
                    bb.tr(pT16[0:128, 128:256], ppan[:, gb + 192:gb + GW],
                          ident16[:], r=(ppan, ident16), w=(pT16,))
                    bb.op("scalar", "activation", pgT[:], pT16[:, 128:256],
                          AF.Copy, r=(pT16,), w=(pgT,))
                for gam in range(2):
                    pl = pLat[:, gam * 128:(gam + 1) * 128]
                    ch = [(pl, sCkv[a][:, gam * 128:(gam + 1) * 128], pbTA[:],
                           True, False)]
                    if width == 192:
                        ch.append((pl, sCkv[a + 1][0:64, gam * 128:(gam + 1) * 128],
                                   pbTB[:], False, False))
                    ch[-1] = (*ch[-1][:4], qside == 1)
                    bb.mm(ch, r=(pbTA, pbTB, sCkv[a],
                                 sCkv[min(a + 1, NT - 1)]), w=(pLat,))
                    if qside == 2:
                        for C in range(4):
                            bb.mm([(pLat[:, gam * 128 + C * 32:gam * 128 + (C + 1) * 32],
                                    gV[:, (g * 4 + C) * 256 + gam * 128:
                                       (g * 4 + C) * 256 + (gam + 1) * 128],
                                    pgT[:, C * 32:(C + 1) * 32], False, C == 3)],
                                  r=(gV, pgT), w=(pLat,), after=g2toks)
                    bb.op("scalar", "activation",
                          olt[gam][:].rearrange("p (h t) -> p h t", h=NH, t=128)
                          [:, :, g * 16:(g + 1) * 16].transpose([0, 2, 1]),
                          pLat[:, gam * 128:(gam + 1) * 128]
                          .rearrange("p (dq h) -> p dq h", dq=16, h=NH),
                          AF.Copy, r=(pLat,), w=(olt[gam],))
            if ti == 1 and "gv1" in dDbg:
                bb.dma("sync", "dbg", dDbg["gv1"][:], gV[:], r=(gV,))
            if ti == 1 and "olt1" in dDbg:
                bb.dma("sync", "dbg", dDbg["olt1"][:], olt[0][:], r=(olt[0],))

        def emit_B4(ti):
            t0 = ti * 128
            for h in range(NH):
                pv = pLat[0:64, (h % 2) * 128:(h % 2) * 128 + 128]
                bb.mm([(pv, sWuv[gam][:, h * 64:(h + 1) * 64],
                        olt[gam][:, h * 128:(h + 1) * 128], gam == 0, gam == 1)
                       for gam in range(2)],
                      r=(sWuv[0], sWuv[1], olt[0], olt[1]), w=(pLat,))
                bb.op("scalar", "activation", oat[:, h * 128:(h + 1) * 128],
                      pv, AF.Copy, r=(pLat,), w=(oat,))
            bb.mm([(pOut[:, nch * 512:(nch + 1) * 512],
                    oat[0:64, h * 128:(h + 1) * 128],
                    sWout8[h][:, nch * 512:(nch + 1) * 512], h == 0, h == NH - 1)
                   for h in range(NH) for nch in range(2)],
                  r=(oat,) + tuple(sWout8), w=(pOut,))
            bb.dma("sync", "ld", xt[:], dX[t0:t0 + 128, :], w=(xt,))
            bb.dma("sync", "ld", ft[:], dFFN[t0:t0 + 128, :], w=(ft,))
            bb.op("vector", "scalar_tensor_tensor", ft[:], pOut[:], 1.0, ft[:],
                  ALU.mult, ALU.add, r=(pOut, ft), w=(ft,))
            bb.op("vector", "tensor_tensor", xt[:], xt[:], ft[:], ALU.add,
                  r=(xt, ft), w=(xt,))
            bb.dma("sync", "outq", dOut[t0:t0 + 128, :], xt[:], r=(xt,))
            if ti == 1:
                if "qts1" in dDbg:
                    bb.dma("sync", "dbg", dDbg["qts1"][:], QTSb[1][:],
                           r=(QTSb[1],))
                if "mqs1" in dDbg:
                    bb.dma("sync", "dbg", dDbg["mqs1"][:], MQSb[1][0:64, :],
                           r=(MQSb[1],))
                if "sca1" in dDbg:
                    bb.dma("sync", "dbg", dDbg["sca1"][:], scrA[:], r=(scrA,))

        emit_A1(0)
        emit_A1b(0)
        emit_G1(0)
        emit_G2(0)
        for ti in range(NT):
            emit_B1(ti)
            if ti + 1 < NT:
                emit_A1(ti + 1)
                emit_A1b(ti + 1)
                emit_G1(ti + 1)
            emit_B2(ti)
            emit_B3(ti)
            if ti + 1 < NT:
                emit_G2(ti + 1)
            emit_B4(ti)
        bb.barrier()
    _e.close()


def kernel(**inputs):
    statics = build_statics()
    nc, _dbg = build_program()
    x = np.ascontiguousarray(np.asarray(inputs["L0_attn_x"], dtype=np.float32))
    base = {}
    for nm, sh in WEIGHT_SHAPES.items():
        key = ("L0_ffn_" + nm) if nm.startswith("w_") else ("L0_attn_" + nm)
        base[nm] = np.ascontiguousarray(
            np.asarray(inputs[key], dtype=np.float32).reshape(sh))
    for nm, arr in statics.items():
        base["st_" + nm] = arr
    in_maps = [dict(base, x=x[b]) for b in range(8)]
    res = run_bass_kernel_spmd(nc, in_maps, core_ids=list(range(8)))
    out = np.stack([np.asarray(res.results[b]["out"]) for b in range(8)], axis=0)
    return out.astype(np.float32)


# version 17
# speedup vs baseline: 1.0489x; 1.0489x over previous
import os
from contextlib import ExitStack

import numpy as np
import ml_dtypes

import concourse.bass as bass
import concourse.mybir as mybir
from concourse.bass_utils import run_bass_kernel_spmd
from concourse.library_config import mlp as _MLP_LIB

F32 = mybir.dt.float32
BF16 = mybir.dt.bfloat16
I16 = mybir.dt.int16
I32 = mybir.dt.int32
U32 = mybir.dt.uint32
AF = mybir.ActivationFunctionType
ALU = mybir.AluOpType
AX = mybir.AxisListType

L, D, DCKV, DCQ = 2048, 1024, 256, 512
NH, DH, DR, HALF, NIH = 8, 64, 32, 16, 4
NT = L // 128
SCALE = float((DH + DR) ** -0.5)
ROPE_BASE = 10000.0
BIGNEG = -1.0e9
REPL = -1.0e30
GW = 320
_CONSTS = (1e-5, 1.0, -1.0, 1.0 / 1024.0, float((64 + 32) ** -0.5), 0.0, 127.0, 128.5)

ENGS = ("tensor", "vector", "scalar", "gpsimd", "sync")


def _trig4(angles):
    c, s = np.cos(angles), np.sin(angles)
    return np.concatenate([c, s, c, s], axis=0)


def build_statics():
    th = 1.0 / ROPE_BASE ** (2.0 * np.arange(HALF, dtype=np.float64) / DR)
    t = np.arange(L, dtype=np.float64)
    bf = ml_dtypes.bfloat16
    st = {}
    st["ident32"] = np.eye(128, dtype=np.float32)
    st["ident16"] = np.eye(128).astype(bf)
    st["ones16"] = np.ones((128, 128), dtype=bf)
    st["ones32"] = np.ones((128, 128), dtype=np.float32)
    st["tq1"] = _trig4(th[:, None] * t[None, :]).astype(bf)
    st["tq2"] = _trig4(2.0 * th[:, None] * t[None, :]).astype(bf)
    st["trigb"] = _trig4(th[:, None] * (t[None, :] + 63.0)).astype(bf)
    st["trigb0"] = _trig4(th[:, None] * t[None, :192]).astype(bf)
    st["trigk"] = _trig4(th[:, None] * (64.0 + np.arange(32.0)[None, :])).astype(bf)
    c2 = np.zeros((64, 64), dtype=np.float32)
    for m in range(16):
        c2[m, m] = 1.0
        c2[48 + m, m] = -1.0
    for m in range(16, 48):
        i = (m - 16) % 16
        c2[16 + i, m] = 1.0
        c2[32 + i, m] = 1.0
    for m in range(48, 64):
        i = m - 48
        c2[i, m] = -1.0
        c2[48 + i, m] = 1.0
    st["c2"] = c2.astype(bf)
    dup = np.zeros((32, 64), dtype=np.float32)
    for m in range(64):
        dup[(m % 16) if m < 32 else 16 + (m % 16), m] = 1.0
    st["dupr"] = np.tile(dup, (4, 1)).astype(bf)
    s = np.arange(L, dtype=np.float64)
    st["fut"] = np.tile((1.0e8 - 16.0 * s).astype(np.float32), (128, 1))
    st["pert"] = np.tile((-1.0e-20 * (s + 1.0)).astype(np.float32), (128, 1))
    p = np.arange(128)
    dq = p // 8
    col = np.arange(192)
    for v in range(8):
        lo = dq[:, None] + 1 + 16 * v
        hi = dq[:, None] + 64 + 16 * v
        st[f"bm{v}"] = np.where((col[None, :] >= lo) & (col[None, :] <= hi),
                                0.0, BIGNEG).astype(np.float32)
    st["mst96"] = np.tile(np.where(col < 96, 0.0, BIGNEG).astype(np.float32),
                          (128, 1))
    colg = np.arange(128)
    st["diagmask"] = np.where((colg // 32)[None, :] == (dq % 4)[:, None],
                              0.0, BIGNEG).astype(np.float32)
    st["constv"] = np.tile(np.array(_CONSTS, dtype=np.float32), (128, 1))
    st["rep16"] = (np.arange(128)[None, :] % 16 == np.arange(16)[:, None]
                   ).astype(np.float32)
    return st


STATIC_SHAPES = {
    "ident32": ([128, 128], F32), "ident16": ([128, 128], BF16),
    "ones16": ([128, 128], BF16), "ones32": ([128, 128], F32),
    "tq1": ([64, L], BF16), "tq2": ([64, L], BF16), "trigb": ([64, L], BF16),
    "trigb0": ([64, 192], BF16), "trigk": ([64, 32], BF16),
    "c2": ([64, 64], BF16), "dupr": ([128, 64], BF16),
    "fut": ([128, L], F32), "pert": ([128, L], F32),
    **{f"bm{v}": ([128, 192], F32) for v in range(8)},
    "mst96": ([128, 192], F32), "diagmask": ([128, 128], F32),
    "constv": ([128, 8], F32), "rep16": ([16, 128], F32),
}

WEIGHT_SHAPES = {
    "W_dkv": (D, DCKV), "W_uk": (DCKV, 512), "W_uv": (DCKV, 512),
    "W_dq": (D, DCQ), "W_uq": (DCQ, 512), "W_qr": (DCQ, 256),
    "W_kr": (D, DR), "W_out": (512, D), "idx_wq": (D, 256),
    "idx_wk": (D, DH), "idx_w": (1, NIH), "raw_delta": (1, HALF),
    "w_gate": (4096, D), "w_up": (4096, D), "w_down": (D, 4096),
}


class BB:

    NSLOT = 8

    def __init__(self, nc, stack):
        self.nc = nc
        self.cnt = {e: 0 for e in ENGS}
        self.water = {e: {p: 0 for p in ENGS} for e in ENGS}
        self.sem = {e: stack.enter_context(nc.semaphore(f"es_{e}")) for e in ENGS}
        self.dq = {}
        self.dwater = {e: {} for e in ENGS}
        self.stack = stack
        self.buf = {}

    def eng(self, n):
        return getattr(self.nc, n)

    def _drec(self, q):
        if q not in self.dq:
            sems = [self.stack.enter_context(self.nc.semaphore(f"ds_{q}{i}"))
                    for i in range(self.NSLOT)]
            self.dq[q] = {"sems": sems, "uses": [0] * self.NSLOT, "idx": 0}
        return self.dq[q]

    def _dma_slot(self, eng, q):
        rec = self._drec(q)
        i = rec["idx"]
        rec["idx"] = (i + 1) % self.NSLOT
        if rec["uses"][i] > 0:
            self.wait(eng, ("D", (q, i), 16 * rec["uses"][i]))
        rec["uses"][i] += 1
        return rec["sems"][i], ("D", (q, i), 16 * rec["uses"][i])

    def _rec(self, h):
        k = id(h)
        if k not in self.buf:
            self.buf[k] = {"w": None, "r": []}
        return self.buf[k]

    def _pre(self, eng, reads, writes, extra):
        toks = list(extra)
        for h in reads:
            rr = self._rec(h)
            if rr["w"] is not None:
                toks.append(rr["w"])
        for h in writes:
            rr = self._rec(h)
            if rr["w"] is not None:
                toks.append(rr["w"])
            toks.extend(rr["r"])
        self.wait(eng, *toks)

    def _post(self, tok, reads, writes):
        for h in writes:
            rr = self._rec(h)
            rr["w"] = tok
            rr["r"] = []
        for h in reads:
            rr = self._rec(h)
            if tok not in rr["r"]:
                rr["r"].append(tok)

    def op(self, eng, method, *args, r=(), w=(), after=(), **kwargs):
        self._pre(eng, r, w, after)
        inst = getattr(self.eng(eng), method)(*args, **kwargs)
        self.cnt[eng] += 1
        inst.then_inc(self.sem[eng], 1)
        tok = (eng, self.cnt[eng])
        self._post(tok, r, w)
        return tok

    def mm(self, chain, r=(), w=(), after=(), tile_position=None):
        self._pre("tensor", r, w, after)
        for i, (o, lh, rh, s0, s1) in enumerate(chain):
            inst = self.nc.tensor.matmul(o, lh, rh, start=s0, stop=s1,
                                         tile_position=tile_position)
            if i == len(chain) - 1:
                self.cnt["tensor"] += 1
                inst.then_inc(self.sem["tensor"], 1)
        tok = ("tensor", self.cnt["tensor"])
        self._post(tok, r, w)
        return tok

    def tr(self, out, in_, ident, r=(), w=(), after=()):
        self._pre("tensor", r, w, after)
        inst = self.nc.tensor.transpose(out, in_, ident)
        self.cnt["tensor"] += 1
        inst.then_inc(self.sem["tensor"], 1)
        tok = ("tensor", self.cnt["tensor"])
        self._post(tok, r, w)
        return tok

    def dma(self, eng, q, out, in_, r=(), w=(), after=(), transpose=False, **kw):
        self._pre(eng, r, w, after)
        sem, tok = self._dma_slot(eng, q)
        if transpose:
            inst = self.eng(eng).dma_start_transpose(out, in_)
        else:
            inst = self.eng(eng).dma_start(out=out, in_=in_, **kw)
        inst.then_inc(sem, 16)
        self._post(tok, r, w)
        return tok

    def gather(self, q, r=(), w=(), after=(), **kwargs):
        self._pre("gpsimd", r, w, after)
        sem, tok = self._dma_slot("gpsimd", q)
        inst = self.nc.gpsimd.dma_gather(**kwargs)
        inst.then_inc(sem, 16)
        self._post(tok, r, w)
        return tok

    def wait(self, eng, *tokens):
        e = self.eng(eng)
        dmax, emax = {}, {}
        for t in tokens:
            if t is None:
                continue
            if t[0] == "D":
                _, nm, v = t
                dmax[nm] = max(dmax.get(nm, 0), v)
            else:
                p, v = t
                emax[p] = max(emax.get(p, 0), v)
        for nm, v in dmax.items():
            wt = self.dwater[eng]
            if wt.get(nm, 0) < v:
                q, i = nm
                e.wait_ge(self.dq[q]["sems"][i], v)
                wt[nm] = v
        for p, v in emax.items():
            if self.water[eng][p] < v:
                e.wait_ge(self.sem[p], v)
                self.water[eng][p] = v

    def barrier(self):
        toks = [(e, self.cnt[e]) for e in ENGS]
        dtoks = [("D", (q, i), 16 * rec["uses"][i])
                 for q, rec in self.dq.items()
                 for i in range(self.NSLOT) if rec["uses"][i] > 0]
        for e in ENGS:
            self.wait(e, *toks, *dtoks)

    def fence(self, queues=()):
        toks = [(e, self.cnt[e]) for e in ENGS]
        dtoks = [("D", (q, i), 16 * self.dq[q]["uses"][i])
                 for q in queues if q in self.dq
                 for i in range(self.NSLOT) if self.dq[q]["uses"][i] > 0]
        for e in ENGS:
            self.wait(e, *toks, *dtoks)

    def qtokens(self, q):
        if q not in self.dq:
            return []
        rec = self.dq[q]
        return [("D", (q, i), 16 * rec["uses"][i])
                for i in range(self.NSLOT) if rec["uses"][i] > 0]


def band_params(ti, g):
    q0 = 128 * ti + 16 * g
    if ti == 0 and g < 4:
        return 0, 192, "mst96", 1
    a = (q0 - 64) // 128
    width = min(192, L - 128 * a)
    v = (q0 - 128 * a - 64) // 16
    return a, width, f"bm{v}", 2


def build_program(debug=()):
    nc = bass.Bass("TRN2", target_bir_lowering=False)

    dX = nc.dram_tensor("x", [L, D], F32, kind="ExternalInput").ap()
    dOut = nc.dram_tensor("out", [L, D], F32, kind="ExternalOutput").ap()
    dW = {nm: nc.dram_tensor(nm, list(sh), F32, kind="ExternalInput").ap()
          for nm, sh in WEIGHT_SHAPES.items()}
    dS = {nm: nc.dram_tensor("st_" + nm, list(sh), dt_, kind="ExternalInput").ap()
          for nm, (sh, dt_) in STATIC_SHAPES.items()}
    dH2U = nc.dram_tensor("h2u", [L, 4096], BF16, kind="Internal").ap()
    dCKV = nc.dram_tensor("ckvscr", [L, DCKV], BF16, kind="Internal").ap()
    dFFN = nc.dram_tensor("ffnout", [L, D], F32, kind="Internal").ap()
    dQT = nc.dram_tensor("qtscr", [2, 128, NH * L], BF16, kind="Internal").ap()
    dMUQ = nc.dram_tensor("muqscr", [2, 64, NH * L], BF16, kind="Internal").ap()
    dDbg = {nm: nc.dram_tensor("dbg_" + nm, list(shp), dt_, kind="ExternalOutput").ap()
            for nm, shp, dt_ in debug}

    est = ExitStack()
    bb = BB(nc, est)
    A = nc.alloc_sbuf_tensor

    sSt = {nm: A("s_" + nm, list(STATIC_SHAPES[nm][0]), STATIC_SHAPES[nm][1])
           for nm in dS if nm not in ("tq1", "tq2", "trigb", "fut", "pert")}
    sCkvT = [A(f"ckvT{g}", [128, L], BF16) for g in range(2)]
    sCkv = [A(f"ckv{t}", [128, DCKV], BF16) for t in range(NT)]
    sScr = A("scr", [128, 16 * 384], BF16)
    sKiT = A("kiT", [128, L], BF16)
    sQiT = [A(f"qiT{g}", [128, L], BF16) for g in range(2)]
    sWuv = [A(f"wuv{g}", [128, 512], BF16) for g in range(2)]
    sWout8 = [A(f"wout{h}", [64, D], BF16) for h in range(NH)]
    sBK = A("bk", [64, L], BF16)
    sBK0 = A("bk0", [64, 192], BF16)
    sStats = A("stats", [128, 2 * NT], F32)
    sIdxw = A("idxw", [128, NIH], F32)
    sSw = A("swv", [128, 8], F32)
    sRm = A("rmaxs", [128, NT], F32)
    sMu = A("mus", [128, NT], F32)
    sQs = A("qss", [128, NT], F32)

    with est, nc.Block():
        nc.gpsimd.load_library(_MLP_LIB)
        for nm in dS:
            if nm in ("tq1", "tq2", "trigb", "fut", "pert"):
                continue
            bb.dma("sync", "ld", sSt[nm][:], dS[nm][:], w=(sSt[nm],))
        for ci, cv in enumerate(_CONSTS):
            nc.const_aps.register(F32, float(cv), sSt["constv"][:, ci:ci + 1])
        bb.dma("sync", "ld", sIdxw[0:1, :], dW["idx_w"][:], w=(sIdxw,))
        bb.op("gpsimd", "partition_broadcast", sIdxw[:], sIdxw[0:1, :], 128,
              r=(sIdxw,), w=(sIdxw,))
        bb.barrier()
        bb.op("vector", "memset", sStats[:], 0.0, w=(sStats,))
        bb.op("vector", "memset", sSw[:], 0.0, w=(sSw,))
        bb.op("vector", "memset", sRm[:], 0.0, w=(sRm,))

        bb.op("vector", "memset", sScr[:], 0.0, w=(sScr,))
        with nc.sbuf_tensor("xnT", [128, 8 * L], BF16) as xnT:
            mest = ExitStack()
            with mest:
                _phase2_means(nc, bb, dW, sSw, sSt, dDbg, mest)
                _phase1_ln(nc, bb, dX, xnT, sStats, sMu, sQs, sSt, dDbg)
                bb.fence(queues=("ld",))
                _phase3_proj(nc, bb, dW, dS, xnT, sCkvT, sCkv, sScr, sKiT,
                             sQiT, sWuv, sWout8, sBK, sBK0, sSt, dCKV, dQT,
                             dMUQ, dDbg)
            with nc.sbuf_tensor("xqT", [128, 8 * L], BF16) as xqT:
                _phase1_xq(nc, bb, dX, xqT, sMu, sQs, sSt, dDbg)
                _phase2_gateup(nc, bb, dW, dH2U, xqT, sStats, sSw, sRm, sSt,
                               dDbg)
        _phase2_down(nc, bb, dW, dH2U, dFFN, sStats, sSw, sRm, sSt, dDbg)
        _phase4_attn(nc, bb, dX, dOut, dFFN, dCKV, dQT, dMUQ, sCkvT, sCkv,
                     sScr, sKiT, sQiT, sWuv, sWout8, sBK, sBK0, sSt, dS,
                     sStats, sIdxw, dDbg)
    mybir.codegen_inst_isa_subclasses(nc)
    return nc, [nm for nm, _, _ in debug]


def _phase1_ln(nc, bb, dX, xnT, sStats, sMu, sQs, sSt, dDbg):
    ident16 = sSt["ident16"]
    with (
        nc.sbuf_tensor("xtmA", [128, D], F32) as xtmA,
        nc.sbuf_tensor("xtmB", [128, D], F32) as xtmB,
        nc.sbuf_tensor("xctm", [128, D], F32) as xctm,
        nc.sbuf_tensor("sqtm", [128, D], F32) as sqtm,
        nc.sbuf_tensor("xn16A", [128, D], BF16) as xn16A,
        nc.sbuf_tensor("xn16B", [128, D], BF16) as xn16B,
        nc.sbuf_tensor("c1", [128, 1], F32) as c1,
        nc.sbuf_tensor("c2c", [128, 1], F32) as c2c,
        nc.sbuf_tensor("c3", [128, 1], F32) as c3,
        nc.psum_tensor("pTa", [128, 64], F32) as pTa,
        nc.psum_tensor("pTb", [128, 64], F32) as pTb,
    ):
        pT = (pTa, pTb)
        xnT3 = xnT[:].rearrange("p (a t) -> p a t", a=8, t=L)
        for ti in range(NT):
            r0 = ti * 128
            xtm = xtmA if ti % 2 == 0 else xtmB
            xn16 = xn16A if ti % 2 == 0 else xn16B
            bb.dma("sync", "ld", xtm[:], dX[r0:r0 + 128, :], w=(xtm,))
            bb.op("vector", "tensor_reduce", c1[:], xtm[:], AX.X, ALU.add,
                  r=(xtm,), w=(c1,))
            bb.op("vector", "tensor_scalar", c1[:], c1[:], 1.0 / D, None, ALU.mult,
                  r=(c1,), w=(c1,))
            bb.op("vector", "tensor_copy", sMu[:, ti:ti + 1], c1[:], r=(c1,),
                  w=(sMu,))
            bb.op("vector", "tensor_scalar", xctm[:], xtm[:], c1[:], None,
                  ALU.subtract, r=(xtm, c1), w=(xctm,))
            bb.op("scalar", "activation", sqtm[:], xctm[:], AF.Square,
                  r=(xctm,), w=(sqtm,))
            bb.op("vector", "tensor_reduce", c2c[:], sqtm[:], AX.X, ALU.add,
                  r=(sqtm,), w=(c2c,))
            bb.op("scalar", "activation", c2c[:], c2c[:], AF.Sqrt, bias=1e-5,
                  scale=1.0 / D, r=(c2c,), w=(c2c,))
            bb.op("vector", "reciprocal", c2c[:], c2c[:], r=(c2c,), w=(c2c,))
            bb.op("vector", "tensor_reduce", c3[:], xctm[:], AX.X, ALU.max,
                  apply_absolute_value=True, r=(xctm,), w=(c3,))
            bb.op("vector", "reciprocal", sQs[:, ti:ti + 1], c3[:], r=(c3,),
                  w=(sQs,))
            bb.op("vector", "tensor_scalar", sQs[:, ti:ti + 1],
                  sQs[:, ti:ti + 1], 127.0, None, ALU.mult, r=(sQs,), w=(sQs,))
            bb.op("vector", "tensor_scalar", xn16[:], xctm[:], c2c[:], None,
                  ALU.mult, r=(xctm, c2c), w=(xn16,))
            bb.op("vector", "tensor_scalar", sStats[:, ti:ti + 1], c3[:], c2c[:],
                  1.0 / 127.0, ALU.mult, ALU.mult, r=(c3, c2c), w=(sStats,))
            for k in range(8):
                pt = pT[k % 2]
                bb.tr(pt[:].bitcast(BF16)[:, 0:128],
                      xn16[:, k * 128:(k + 1) * 128], ident16[:],
                      r=(xn16, ident16), w=(pt,))
                bb.op("scalar", "activation", xnT3[:, k, r0:r0 + 128],
                      pt[:].bitcast(BF16)[:, 0:128], AF.Copy, r=(pt,), w=(xnT,))
    if "xnT" in dDbg:
        bb.dma("sync", "dbg", dDbg["xnT"][:], xnT[:], r=(xnT,))


def _phase1_xq(nc, bb, dX, xqT, sMu, sQs, sSt, dDbg):
    ident16 = sSt["ident16"]
    with (
        nc.sbuf_tensor("qxtA", [128, D], F32) as qxtA,
        nc.sbuf_tensor("qxtB", [128, D], F32) as qxtB,
        nc.sbuf_tensor("qsq", [128, D], F32) as qsq,
        nc.sbuf_tensor("qx32", [128, D], I32) as qx32,
        nc.sbuf_tensor("qxgt", [128, D], BF16) as qxgt,
        nc.sbuf_tensor("xq16A", [128, D], BF16) as xq16A,
        nc.sbuf_tensor("xq16B", [128, D], BF16) as xq16B,
        nc.psum_tensor("pTg", [128, 64], F32) as pTg,
        nc.psum_tensor("pTh", [128, 64], F32) as pTh,
    ):
        pT = (pTg, pTh)
        xqT3 = xqT[:].rearrange("p (a t) -> p a t", a=8, t=L)
        for ti in range(NT):
            r0 = ti * 128
            xtm = qxtA if ti % 2 == 0 else qxtB
            xq16 = xq16A if ti % 2 == 0 else xq16B
            bb.dma("sync", "ld", xtm[:], dX[r0:r0 + 128, :], w=(xtm,))
            bb.op("vector", "tensor_scalar", xtm[:], xtm[:], sMu[:, ti:ti + 1],
                  None, ALU.subtract, r=(xtm, sMu), w=(xtm,))
            bb.op("vector", "tensor_scalar", qsq[:], xtm[:], sQs[:, ti:ti + 1],
                  0.5, ALU.mult, ALU.add, r=(xtm, sQs), w=(qsq,))
            bb.op("scalar", "activation", qx32[:], qsq[:], AF.Copy,
                  r=(qsq,), w=(qx32,))
            bb.op("vector", "tensor_tensor", qxgt[:], qx32[:], qsq[:],
                  ALU.is_gt, r=(qx32, qsq), w=(qxgt,))
            bb.op("vector", "tensor_tensor", xq16[:], qx32[:], qxgt[:],
                  ALU.subtract, r=(qx32, qxgt), w=(xq16,))
            for k in range(8):
                pt = pT[k % 2]
                bb.tr(pt[:].bitcast(BF16)[:, 0:128],
                      xq16[:, k * 128:(k + 1) * 128], ident16[:],
                      r=(xq16, ident16), w=(pt,))
                bb.op("vector", "tensor_copy", xqT3[:, k, r0:r0 + 128],
                      pt[:].bitcast(BF16)[:, 0:128], r=(pt,), w=(xqT,))
    if "xqT" in dDbg:
        bb.dma("sync", "dbg", dDbg["xqT"][:], xqT[:], r=(xqT,))


def _quant_tern(nc, bb, out16, wtile, recip_ap, qf32, qi32, qg, sSw):
    bb.op("vector", "tensor_scalar", qf32[:], wtile[:], recip_ap, 1.4999,
          ALU.mult, ALU.min, r=(wtile, sSw), w=(qf32,))
    bb.op("vector", "tensor_scalar", qf32[:], qf32[:], -1.4999, 0.5,
          ALU.max, ALU.add, r=(qf32,), w=(qf32,))
    bb.op("scalar", "activation", qi32[:], qf32[:], AF.Copy,
          r=(qf32,), w=(qi32,))
    bb.op("vector", "tensor_tensor", qg[:], qi32[:], qf32[:], ALU.is_gt,
          r=(qi32, qf32), w=(qg,))
    bb.op("vector", "tensor_tensor", out16[:], qi32[:], qg[:], ALU.subtract,
          r=(qi32, qg), w=(out16,))


def _phase2_means(nc, bb, dW, sSw, sSt, dDbg, est):
    sOnes = sSt["ones16"]
    _c = est.enter_context
    pM = _c(nc.psum_tensor("pM", [128, 8], F32))
    mts = [_c(nc.sbuf_tensor(f"mt{i}", [128, 512], F32)) for i in range(4)]
    macc = _c(nc.sbuf_tensor("macc", [128, 4], F32))
    msum = _c(nc.sbuf_tensor("msum", [128, 1], F32))
    bb.op("vector", "memset", macc[:], 0.0, w=(macc,))
    for wi, wn in enumerate(("w_gate", "w_up", "w_down")):
        rows, cols = WEIGHT_SHAPES[wn]
        blocks = [(k * 128, cc * 512) for k in range(rows // 128)
                  for cc in range(cols // 512)]
        for bi, (rr, cc) in enumerate(blocks):
            wt = mts[bi % 4]
            bb.dma("sync", f"wm{bi % 4}", wt[:],
                   dW[wn][rr:rr + 128, cc:cc + 512], w=(wt,))
            bb.op("vector", "tensor_reduce", msum[:], wt[:], AX.X, ALU.add,
                  apply_absolute_value=True, r=(wt,), w=(msum,))
            if bi == 0:
                bb.op("vector", "tensor_copy", macc[:, wi:wi + 1], msum[:],
                      r=(msum,), w=(macc,))
            else:
                bb.op("vector", "tensor_tensor", macc[:, wi:wi + 1],
                      macc[:, wi:wi + 1], msum[:], ALU.add, r=(macc, msum),
                      w=(macc,))
    bb.mm([(pM[:, 0:4], sSt["ones32"][:], macc[:], True, True)],
          r=(sSt["ones32"], macc), w=(pM,))
    for wi, wn in enumerate(("w_gate", "w_up", "w_down")):
        rows, cols = WEIGHT_SHAPES[wn]
        bb.op("vector", "tensor_scalar", sSw[:, wi:wi + 1], pM[:, wi:wi + 1],
              1.0 / (rows * cols), 1e-5, ALU.mult, ALU.max, r=(pM,), w=(sSw,))
        bb.op("vector", "reciprocal", sSw[:, 3 + wi:4 + wi],
              sSw[:, wi:wi + 1], r=(sSw,), w=(sSw,))
    if "sw" in dDbg:
        bb.dma("sync", "dbg", dDbg["sw"][:], sSw[:], r=(sSw,))


def _phase2_gateup(nc, bb, dW, dH2U, xqT, sStats, sSw, sRm, sSt, dDbg):
    ident16 = sSt["ident16"]
    xqT3 = xqT[:].rearrange("p (a t) -> p a t", a=8, t=L)
    bb.barrier()
    _e = ExitStack()
    _c = _e.enter_context
    pGs = [_c(nc.psum_tensor(f"pG{i}", [128, 512], F32)) for i in range(2)]
    pUs = [_c(nc.psum_tensor(f"pU{i}", [128, 512], F32)) for i in range(2)]
    pTa = _c(nc.psum_tensor("pTe", [128, 64], F32))
    pTb = _c(nc.psum_tensor("pTf", [128, 64], F32))
    ternTg = _c(nc.sbuf_tensor("ternTg", [128, 8 * 512], BF16))
    ternTu = _c(nc.sbuf_tensor("ternTu", [128, 8 * 512], BF16))
    wtA = _c(nc.sbuf_tensor("wtA", [128, 1024], F32))
    wtB = _c(nc.sbuf_tensor("wtB", [128, 1024], F32))
    qf32s = [_c(nc.sbuf_tensor(f"qf32{i}", [128, 1024], F32)) for i in range(2)]
    qi32s = [_c(nc.sbuf_tensor(f"qi32{i}", [128, 1024], I32)) for i in range(2)]
    qgs = [_c(nc.sbuf_tensor(f"qg{i}", [128, 1024], BF16)) for i in range(2)]
    wq3A = _c(nc.sbuf_tensor("wq3A", [128, 1024], BF16))
    wq3B = _c(nc.sbuf_tensor("wq3B", [128, 1024], BF16))
    gtA = _c(nc.sbuf_tensor("gtA", [128, 512], BF16))
    gtB = _c(nc.sbuf_tensor("gtB", [128, 512], BF16))
    utA = _c(nc.sbuf_tensor("utA", [128, 512], BF16))
    utB = _c(nc.sbuf_tensor("utB", [128, 512], BF16))
    h2sA = _c(nc.sbuf_tensor("h2sA", [128, 512], BF16))
    h2sB = _c(nc.sbuf_tensor("h2sB", [128, 512], BF16))
    gsc = _c(nc.sbuf_tensor("gsc", [128, 1], F32))
    cmx = _c(nc.sbuf_tensor("cmx", [128, 1], F32))
    with _e:
        tg3 = ternTg[:].rearrange("p (a o) -> p a o", a=8, o=512)
        tu3 = ternTu[:].rearrange("p (a o) -> p a o", a=8, o=512)
        pT = (pTa, pTb)
        for sl in range(8):
            for wi, wn, tT3, tTh in ((0, "w_gate", tg3, ternTg),
                                     (1, "w_up", tu3, ternTu)):
                for j in range(4):
                    par = (wi * 4 + j) % 2
                    wt = wtA if par == 0 else wtB
                    wq3 = wq3A if par == 0 else wq3B
                    o0 = sl * 512 + j * 128
                    bb.dma("sync", f"wm{par}", wt[:], dW[wn][o0:o0 + 128, :],
                           w=(wt,))
                    _quant_tern(nc, bb, wq3, wt, sSw[:, 3 + wi:4 + wi],
                                qf32s[par], qi32s[par], qgs[par], sSw)
                    for k in range(8):
                        pt = pT[k % 2]
                        bb.tr(pt[:].bitcast(BF16)[:, 0:128],
                              wq3[:, k * 128:(k + 1) * 128], ident16[:],
                              r=(wq3, ident16), w=(pt,))
                        bb.op("scalar", "activation",
                              tT3[:, k, j * 128:(j + 1) * 128],
                              pt[:].bitcast(BF16)[:, 0:128], AF.Copy,
                              r=(pt,), w=(tTh,))
            for ti in range(NT):
                t0 = ti * 128
                par = ti % 2
                gt = gtA if par == 0 else gtB
                ut = utA if par == 0 else utB
                h2s = h2sA if par == 0 else h2sB
                pG = pGs[ti % 2]
                pU = pUs[ti % 2]
                bb.op("vector", "tensor_scalar", gsc[:], sStats[:, ti:ti + 1],
                      sSw[:, 0:1], None, ALU.mult, r=(sStats, sSw), w=(gsc,))
                bb.mm([(pG[:], xqT3[:, k, t0:t0 + 128], tg3[:, k, :], k == 0,
                        k == 7) for k in range(8)], r=(xqT, ternTg), w=(pG,))
                bb.mm([(pU[:], xqT3[:, k, t0:t0 + 128], tu3[:, k, :], k == 0,
                        k == 7) for k in range(8)], r=(xqT, ternTu), w=(pU,))
                bb.op("scalar", "activation", gt[:], pG[:], AF.Silu,
                      scale=gsc[:], r=(pG, gsc), w=(gt,))
                bb.op("scalar", "activation", ut[:], pU[:], AF.Copy,
                      r=(pU,), w=(ut,))
                bb.op("vector", "tensor_tensor", h2s[:], gt[:], ut[:],
                      ALU.mult, r=(gt, ut), w=(h2s,))
                if sl == 0:
                    bb.op("vector", "tensor_reduce", sRm[:, ti:ti + 1], h2s[:],
                          AX.X, ALU.max, apply_absolute_value=True,
                          r=(h2s,), w=(sRm,))
                else:
                    bb.op("vector", "tensor_reduce", cmx[:], h2s[:], AX.X,
                          ALU.max, apply_absolute_value=True, r=(h2s,),
                          w=(cmx,))
                    bb.op("vector", "tensor_tensor", sRm[:, ti:ti + 1],
                          sRm[:, ti:ti + 1], cmx[:], ALU.max, r=(sRm, cmx),
                          w=(sRm,))
                bb.dma("sync", "h2st",
                       dH2U[t0:t0 + 128, sl * 512:(sl + 1) * 512], h2s[:],
                       r=(h2s,))
    bb.barrier()


def _phase2_down(nc, bb, dW, dH2U, dFFN, sStats, sSw, sRm, sSt, dDbg):
    ident16 = sSt["ident16"]
    bb.barrier()
    _e = ExitStack()
    _c = _e.enter_context
    pD0 = _c(nc.psum_tensor("pD0", [128, 1024], F32))
    pD1 = _c(nc.psum_tensor("pD1", [128, 1024], F32))
    pTc = _c(nc.psum_tensor("pTc", [128, 64], F32))
    pTd = _c(nc.psum_tensor("pTd", [128, 64], F32))
    ternDT = _c(nc.sbuf_tensor("ternDT", [128, 32 * 1024], BF16))
    h2ts = [_c(nc.sbuf_tensor(f"h2t{i}", [128, 4096], BF16)) for i in range(2)]
    h2qTs = [_c(nc.sbuf_tensor(f"h2qT{i}", [128, 32 * 128], BF16))
             for i in range(2)]
    qf32s = [_c(nc.sbuf_tensor(f"dqf32{i}", [128, 1024], F32)) for i in range(2)]
    qi32s = [_c(nc.sbuf_tensor(f"dqi32{i}", [128, 1024], I32)) for i in range(2)]
    qgs = [_c(nc.sbuf_tensor(f"dqg{i}", [128, 1024], BF16)) for i in range(2)]
    qbs = [_c(nc.sbuf_tensor(f"dqb{i}", [128, 1024], BF16)) for i in range(2)]
    wtA = _c(nc.sbuf_tensor("dwtA", [128, 1024], F32))
    wtB = _c(nc.sbuf_tensor("dwtB", [128, 1024], F32))
    fotA = _c(nc.sbuf_tensor("fotA", [128, 1024], F32))
    s2c = _c(nc.sbuf_tensor("s2c", [128, 1], F32))
    duc = _c(nc.sbuf_tensor("duc", [128, 1], F32))
    with _e:
        td3 = ternDT[:].rearrange("p (a d) -> p a d", a=32, d=1024)
        pT = (pTc, pTd)
        for dtile in range(8):
            d0 = dtile * 128
            for cc in range(4):
                par = (dtile * 4 + cc) % 2
                wt = wtA if par == 0 else wtB
                bb.dma("sync", f"wm{par}", wt[:],
                       dW["w_down"][d0:d0 + 128, cc * 1024:(cc + 1) * 1024],
                       w=(wt,))
                _quant_tern(nc, bb, qbs[par], wt, sSw[:, 5:6], qf32s[par],
                            qi32s[par], qgs[par], sSw)
                for k in range(8):
                    hc = cc * 8 + k
                    pt = pT[k % 2]
                    bb.tr(pt[:].bitcast(BF16)[:, 0:128],
                          qbs[par][:, k * 128:(k + 1) * 128], ident16[:],
                          r=(qbs[par], ident16), w=(pt,))
                    bb.op("scalar", "activation", td3[:, hc, d0:d0 + 128],
                          pt[:].bitcast(BF16)[:, 0:128], AF.Copy,
                          r=(pt,), w=(ternDT,))
        for ti in range(NT):
            t0 = ti * 128
            pD = pD0 if ti % 2 == 0 else pD1
            fot = fotA
            h2t = h2ts[ti % 2]
            h2qT = h2qTs[ti % 2]
            bb.dma("sync", "wm0", h2t[:], dH2U[t0:t0 + 128, :], w=(h2t,),
                   after=tuple(bb.qtokens("h2st")) if ti == 0 else ())
            bb.op("vector", "tensor_scalar", s2c[:], sRm[:, ti:ti + 1], 1e-5,
                  None, ALU.max, r=(sRm,), w=(s2c,))
            bb.op("vector", "tensor_scalar", duc[:], sStats[:, ti:ti + 1],
                  sSw[:, 1:2], None, ALU.mult, r=(sStats, sSw), w=(duc,))
            bb.op("vector", "tensor_scalar", duc[:], duc[:], sSw[:, 2:3],
                  1.0 / 127.0, ALU.mult, ALU.mult, r=(duc, sSw), w=(duc,))
            bb.op("vector", "tensor_tensor", sStats[:, NT + ti:NT + ti + 1],
                  duc[:], s2c[:], ALU.mult, r=(duc, s2c), w=(sStats,))
            bb.op("vector", "reciprocal", s2c[:], s2c[:], r=(s2c,), w=(s2c,))
            bb.op("vector", "tensor_scalar", s2c[:], s2c[:], 127.0, None,
                  ALU.mult, r=(s2c,), w=(s2c,))
            for qc in range(4):
                qf32, qi32, qg, qb = (qf32s[qc % 2], qi32s[qc % 2],
                                      qgs[qc % 2], qbs[qc % 2])
                bb.op("vector", "tensor_scalar", qf32[:],
                      h2t[:, qc * 1024:(qc + 1) * 1024], s2c[:], 0.5,
                      ALU.mult, ALU.add, r=(h2t, s2c), w=(qf32,))
                bb.op("vector", "tensor_copy", qi32[:], qf32[:],
                      r=(qf32,), w=(qi32,))
                bb.op("vector", "tensor_tensor", qg[:], qi32[:], qf32[:],
                      ALU.is_gt, r=(qi32, qf32), w=(qg,))
                bb.op("vector", "tensor_tensor", qb[:], qi32[:], qg[:],
                      ALU.subtract, r=(qi32, qg), w=(qb,))
                for k in range(8):
                    hc = qc * 8 + k
                    pt = pT[hc % 2]
                    bb.tr(pt[:].bitcast(BF16)[:, 0:128],
                          qb[:, k * 128:(k + 1) * 128], ident16[:],
                          r=(qb, ident16), w=(pt,))
                    bb.op("scalar", "activation",
                          h2qT[:, hc * 128:(hc + 1) * 128],
                          pt[:].bitcast(BF16)[:, 0:128], AF.Copy,
                          r=(pt,), w=(h2qT,))
            ch = []
            for hc in range(32):
                for nch in range(2):
                    ch.append((pD[:, nch * 512:(nch + 1) * 512],
                               h2qT[:, hc * 128:(hc + 1) * 128],
                               td3[:, hc, nch * 512:(nch + 1) * 512],
                               hc == 0, hc == 31))
            bb.mm(ch, r=(h2qT, ternDT), w=(pD,))
            bb.op("scalar", "activation", fot[:], pD[:], AF.Copy,
                  scale=sStats[:, NT + ti:NT + ti + 1], r=(pD, sStats),
                  w=(fot,))
            bb.dma("sync", "st", dFFN[t0:t0 + 128, :], fot[:], r=(fot,))
    bb.barrier()
    if "stats" in dDbg:
        bb.dma("sync", "dbg", dDbg["stats"][:], sStats[:], r=(sStats,))
    if "h2u" in dDbg:
        _dump_dram(nc, bb, dDbg["h2u"], dH2U, BF16, 4096)
    if "ffnout" in dDbg:
        _dump_dram(nc, bb, dDbg["ffnout"], dFFN, F32, D)


_DUMPN = [0]


def _dump_dram(nc, bb, dbg_ap, src_ap, dt_, cols):
    rows = src_ap.shape[0]
    _DUMPN[0] += 1
    with nc.sbuf_tensor(f"dumptmp{_DUMPN[0]}", [128, cols], dt_) as tmp:
        for k in range(rows // 128):
            bb.dma("sync", "ld", tmp[:], src_ap[k * 128:(k + 1) * 128, :], w=(tmp,))
            bb.dma("sync", "dbg", dbg_ap[k * 128:(k + 1) * 128, :], tmp[:], r=(tmp,))
    bb.barrier()


def _phase3_proj(nc, bb, dW, dS, xnT, sCkvT, sCkv, sScr, sKiT, sQiT, sWuv,
                 sWout8, sBK, sBK0, sSt, dCKV, dQT, dMUQ, dDbg):
    xnT3 = xnT[:].rearrange("p (a t) -> p a t", a=8, t=L)
    ident16 = sSt["ident16"]
    ident32 = sSt["ident32"]
    _e = ExitStack()
    _c = _e.enter_context
    pA = _c(nc.psum_tensor("pA", [128, 512], F32))
    pB = _c(nc.psum_tensor("pB", [128, 512], F32))
    pC = _c(nc.psum_tensor("pC", [128, 512], F32))
    pD = _c(nc.psum_tensor("pD", [128, 512], F32))
    pE = _c(nc.psum_tensor("pE", [128, 512], F32))
    WBUF = _c(nc.sbuf_tensor("WBUF", [128, 8 * 512], BF16))
    WUK = _c(nc.sbuf_tensor("WUK", [128, 2 * 512], BF16))
    WUKT = _c(nc.sbuf_tensor("WUKT", [128, 8 * 128], BF16))
    QCT = _c(nc.sbuf_tensor("QCT", [128, 4 * L], BF16))
    CQT = _c(nc.sbuf_tensor("CQT", [128, 4 * L], BF16))
    MUQ32 = _c(nc.sbuf_tensor("MUQ32", [128, 2 * L], BF16))
    MKT = _c(nc.sbuf_tensor("MKT", [32, L], BF16))
    spA = _c(nc.sbuf_tensor("spA", [128, 512], F32))
    spB = _c(nc.sbuf_tensor("spB", [128, 512], F32))
    v64 = _c(nc.sbuf_tensor("v64", [64, 512], BF16))
    v64b = _c(nc.sbuf_tensor("v64b", [64, 512], BF16))
    qstage = _c(nc.sbuf_tensor("qstage", [128, 4096], BF16))
    tq1 = _c(nc.sbuf_tensor("s_tq1", [64, L], BF16))
    tq2 = _c(nc.sbuf_tensor("s_tq2", [64, L], BF16))
    trigb = _c(nc.sbuf_tensor("s_trigb", [64, L], BF16))
    for _nm, _h in (("tq1", tq1), ("tq2", tq2), ("trigb", trigb)):
        bb.dma("sync", "ld", _h[:], dS[_nm][:], w=(_h,))
    if True:
        def wload(name, tiles, cols):
            for k in range(tiles):
                bb.dma("gpsimd", "cast", WBUF[:, k * cols:(k + 1) * cols],
                       dW[name][k * 128:(k + 1) * 128, :], w=(WBUF,))
            return WBUF
        wload2 = wload
        for k in range(2):
            bb.dma("gpsimd", "cast", WUK[:, k * 512:(k + 1) * 512],
                   dW["W_uk"][k * 128:(k + 1) * 128, :], w=(WUK,))
        for g in range(2):
            bb.dma("gpsimd", "cast", sWuv[g][:],
                   dW["W_uv"][g * 128:(g + 1) * 128, :], w=(sWuv[g],))
        for h in range(NH):
            bb.dma("gpsimd", "cast", sWout8[h][:],
                   dW["W_out"][h * 64:(h + 1) * 64, :], w=(sWout8[h],))

        WDKV = wload("W_dkv", 8, 256)
        for g in range(2):
            for nk in range(4):
                px = pA if nk % 2 == 0 else pD
                bb.mm([(px[:], WDKV[:, k * 256 + g * 128:k * 256 + g * 128 + 128],
                        xnT3[:, k, nk * 512:(nk + 1) * 512], k == 0, k == 7)
                       for k in range(8)], r=(WDKV, xnT), w=(px,))
                bb.op("vector", "tensor_copy",
                      sCkvT[g][:, nk * 512:(nk + 1) * 512], px[:], r=(px,),
                      w=(sCkvT[g],))
        for ti in range(NT):
            r0 = ti * 128
            px = pB if ti % 2 == 0 else pE
            bb.mm([(px[:, 0:256], xnT3[:, k, r0:r0 + 128],
                    WDKV[:, k * 256:(k + 1) * 256], k == 0, k == 7)
                   for k in range(8)], r=(WDKV, xnT), w=(px,))
            bb.op("vector", "tensor_copy", sCkv[ti][:], px[:, 0:256], r=(px,),
                  w=(sCkv[ti],))
            bb.op("vector", "tensor_copy", sScr[:, ti * 384:ti * 384 + 256],
                  sCkv[ti][:], r=(sCkv[ti],), w=(sScr,))
            bb.dma("sync", "st", dCKV[r0:r0 + 128, :], sCkv[ti][:], r=(sCkv[ti],))
        WDQ = wload("W_dq", 8, 512)
        for mi in range(4):
            for nk in range(4):
                px = pA if nk % 2 == 0 else pD
                bb.mm([(px[:], WDQ[:, k * 512 + mi * 128:k * 512 + mi * 128 + 128],
                        xnT3[:, k, nk * 512:(nk + 1) * 512], k == 0, k == 7)
                       for k in range(8)], r=(WDQ, xnT), w=(px,))
                bb.op("vector", "tensor_copy",
                      CQT[:, mi * L + nk * 512:mi * L + (nk + 1) * 512], px[:],
                      r=(px,), w=(CQT,))
        WUQ = wload("W_uq", 4, 512)
        cqT3 = CQT[:].rearrange("p (a t) -> p a t", a=4, t=L)
        for mi in range(4):
            for nk in range(4):
                px = pA if nk % 2 == 0 else pD
                bb.mm([(px[:], WUQ[:, k * 512 + mi * 128:k * 512 + mi * 128 + 128],
                        cqT3[:, k, nk * 512:(nk + 1) * 512], k == 0, k == 3)
                       for k in range(4)], r=(WUQ, CQT), w=(px,))
                bb.op("vector", "tensor_copy",
                      QCT[:, mi * L + nk * 512:mi * L + (nk + 1) * 512], px[:],
                      r=(px,), w=(QCT,))
        for g in range(2):
            for hp in range(4):
                bb.tr(pC[:].bitcast(BF16)[:, 0:128],
                      WUK[:, g * 512 + hp * 128:g * 512 + hp * 128 + 128],
                      ident16[:], r=(WUK, ident16), w=(pC,))
                bb.op("vector", "tensor_copy",
                      WUKT[:, (g * 4 + hp) * 128:(g * 4 + hp + 1) * 128],
                      pC[:].bitcast(BF16)[:, 0:128], r=(pC,), w=(WUKT,))
        qst3 = qstage[:].rearrange("p (t h) -> p t h", t=512, h=8)
        for g in range(2):
            for nk in range(4):
                for h in range(NH):
                    base = (h % 2) * 64
                    lhs = WUKT[base:base + 64,
                               (g * 4 + h // 2) * 128:(g * 4 + h // 2 + 1) * 128]
                    rhs = QCT[base:base + 64,
                              (h // 2) * L + nk * 512:(h // 2) * L + (nk + 1) * 512]
                    px = pA if h % 2 == 0 else pD
                    bb.mm([(px[:], lhs, rhs, True, True)], r=(WUKT, QCT), w=(px,))
                    bb.op("scalar", "activation", qst3[:, :, h], px[:], AF.Copy,
                          scale=SCALE, r=(px,), w=(qstage,))
                bb.dma("sync", "st", dQT[g, :, nk * 4096:(nk + 1) * 4096],
                       qstage[:, 0:4096], r=(qstage,))
        WQR = wload("W_qr", 4, 256)
        for half in range(2):
            for nk in range(4):
                px = pA if nk % 2 == 0 else pD
                bb.mm([(px[:], WQR[:, k * 256 + half * 128:k * 256 + half * 128 + 128],
                        cqT3[:, k, nk * 512:(nk + 1) * 512], k == 0, k == 3)
                       for k in range(4)], r=(WQR, CQT), w=(px,))
                _softplus(nc, bb, MUQ32[:, half * L + nk * 512:half * L + (nk + 1) * 512],
                          px, 128, spA, spB, MUQ32)
        dupr = sSt["dupr"]
        c2t = sSt["c2"]
        for var, tq_h in ((0, tq1), (1, tq2)):
            tq = tq_h
            for nk in range(4):
                for h in range(NH):
                    pb = 32 * (h % 4)
                    rhs = MUQ32[pb:pb + 32,
                                (h // 4) * L + nk * 512:(h // 4) * L + (nk + 1) * 512]
                    px = pB if h % 2 == 0 else pE
                    py = pC if h % 2 == 0 else pD
                    vv = v64 if h % 2 == 0 else v64b
                    bb.mm([(px[0:64, :], dupr[pb:pb + 32, :], rhs, True, True)],
                          r=(dupr, MUQ32), w=(px,),
                          tile_position=(pb, 0) if pb == 96 else None)
                    bb.op("vector", "tensor_tensor", vv[:], px[0:64, :],
                          tq[:, nk * 512:(nk + 1) * 512], ALU.mult,
                          r=(px, tq), w=(vv,))
                    bb.mm([(py[0:64, :], c2t[:], vv[:], True, True)],
                          r=(c2t, vv), w=(py,))
                    bb.op("scalar", "activation", qst3[0:64, :, h], py[0:64, :],
                          AF.Copy, scale=SCALE, r=(py,), w=(qstage,))
                bb.dma("sync", "st", dMUQ[var, :, nk * 4096:(nk + 1) * 4096],
                       qstage[0:64, 0:4096], r=(qstage,))
        IWQ = wload("idx_wq", 8, 256)
        for g in range(2):
            for nk in range(4):
                bb.mm([(pA[:], IWQ[:, k * 256 + g * 128:k * 256 + g * 128 + 128],
                        xnT3[:, k, nk * 512:(nk + 1) * 512], k == 0, k == 7)
                       for k in range(8)], r=(IWQ, xnT), w=(pA,))
                bb.op("vector", "tensor_copy",
                      sQiT[g][:, nk * 512:(nk + 1) * 512], pA[:], r=(pA,),
                      w=(sQiT[g],))
        IWK = wload("idx_wk", 8, 64)
        for hr in range(2):
            for nk in range(4):
                bb.mm([(pA[hr * 64:hr * 64 + 64, :], IWK[:, k * 64:(k + 1) * 64],
                        xnT3[:, k, nk * 512:(nk + 1) * 512], k == 0, k == 7)
                       for k in range(8)], r=(IWK, xnT), w=(pA,))
                bb.op("vector", "tensor_copy",
                      sKiT[hr * 64:hr * 64 + 64, nk * 512:(nk + 1) * 512],
                      pA[hr * 64:hr * 64 + 64, :], r=(pA,), w=(sKiT,))
        WKR = wload("W_kr", 8, 32)
        for nk in range(4):
            bb.mm([(pA[0:32, :], WKR[:, k * 32:(k + 1) * 32],
                    xnT3[:, k, nk * 512:(nk + 1) * 512], k == 0, k == 7)
                   for k in range(8)], r=(WKR, xnT), w=(pA,))
            _softplus(nc, bb, MKT[:, nk * 512:(nk + 1) * 512], pA, 32, spA, spB, MKT)
        for nk in range(4):
            bb.mm([(pB[0:64, :], dupr[0:32, :], MKT[:, nk * 512:(nk + 1) * 512],
                    True, True)], r=(dupr, MKT), w=(pB,))
            bb.op("vector", "tensor_tensor", sBK[:, nk * 512:(nk + 1) * 512],
                  pB[0:64, :], trigb[:, nk * 512:(nk + 1) * 512], ALU.mult,
                  r=(pB, trigb), w=(sBK,))
        bb.mm([(pB[0:64, 0:192], dupr[0:32, :], MKT[:, 0:192], True, True)],
              r=(dupr, MKT), w=(pB,))
        bb.op("vector", "tensor_tensor", sBK0[:], pB[0:64, 0:192],
              sSt["trigb0"][:], ALU.mult, r=(pB, sSt["trigb0"]), w=(sBK0,))
        for ti in range(NT):
            r0 = ti * 128
            bb.tr(pC[:].bitcast(BF16)[:, 0:32], MKT[:, r0:r0 + 128],
                  ident16[0:32, 0:32], r=(MKT, ident16), w=(pC,))
            mu = pC[:].bitcast(BF16)[:, 0:32]
            for dupi in range(2):
                dst = sScr[:, ti * 384 + 256 + dupi * 32:ti * 384 + 256 + dupi * 32 + 32]
                bb.op("vector", "tensor_copy",
                      dst.rearrange("p (a b) -> p a b", a=2, b=16),
                      mu[:, dupi * 16:dupi * 16 + 16]
                      .rearrange("p (a b) -> p a b", a=1, b=16)
                      .broadcast_to([128, 2, 16]),
                      r=(pC,), w=(sScr,))
    bb.barrier()
    _e.close()
    for nm, src in (("ckvT0", sCkvT[0]), ("kiT", sKiT), ("scr", sScr),
                    ("bk", sBK), ("qiT0", sQiT[0])):
        if nm in dDbg:
            bb.dma("sync", "dbg", dDbg[nm][:], src[:], r=(src,))
    bb.barrier()


def _softplus(nc, bb, out_ap, psum, rows, spA, spB, outbuf):
    src = psum[0:rows, :]
    a = spA[0:rows, :]
    b = spB[0:rows, :]
    bb.op("scalar", "activation", a, src, AF.Abs, r=(psum,), w=(spA,))
    bb.op("scalar", "activation", b, a, AF.Exp, scale=-1.0, r=(spA,), w=(spB,))
    bb.op("scalar", "activation", b, b, AF.Ln, bias=1.0, r=(spB,), w=(spB,))
    bb.op("vector", "scalar_tensor_tensor", out_ap, src, 0.0, b, ALU.max,
          ALU.add, r=(psum, spB), w=(outbuf,))


def _phase4_attn(nc, bb, dX, dOut, dFFN, dCKV, dQT, dMUQ, sCkvT, sCkv, sScr,
                 sKiT, sQiT, sWuv, sWout8, sBK, sBK0, sSt, dS, sStats, sIdxw,
                 dDbg):
    P4T = int(os.environ.get("MK_P4TILES", "16"))
    P4S = int(os.environ.get("MK_P4STEP", "99"))
    ident16 = sSt["ident16"]
    ident32 = sSt["ident32"]
    _e = ExitStack()
    _c = _e.enter_context
    pI = _c(nc.psum_tensor("pI", [128, 512], F32))
    pBand = _c(nc.psum_tensor("pBand", [128, 192], F32))
    pGath = _c(nc.psum_tensor("pGath", [128, 512], F32))
    pT32 = _c(nc.psum_tensor("pT32", [128, 128], F32))
    pT16 = _c(nc.psum_tensor("pT16", [128, 256], BF16))
    pLat = _c(nc.psum_tensor("pLat", [128, 256], F32))
    pOut = _c(nc.psum_tensor("pOut", [128, 1024], F32))
    sFut = _c(nc.sbuf_tensor("s4_fut", [128, L], F32))
    sPert = _c(nc.sbuf_tensor("s4_pert", [128, L], F32))
    sSt = dict(sSt, fut=sFut, pert=sPert)
    bb.dma("sync", "ld", sFut[:], dS["fut"][:], w=(sFut,))
    bb.dma("sync", "ld", sPert[:], dS["pert"][:], w=(sPert,))
    QTS = _c(nc.sbuf_tensor("QTS", [128, 2 * NH * 128], BF16))
    QTS2 = _c(nc.sbuf_tensor("QTS2", [128, 2 * NH * 128], BF16))
    MQS = _c(nc.sbuf_tensor("MQS", [128, 2 * NH * 128], BF16))
    MQS2 = _c(nc.sbuf_tensor("MQS2", [128, 2 * NH * 128], BF16))
    iacc = _c(nc.sbuf_tensor("iacc", [128, L], F32))
    tmp16 = _c(nc.sbuf_tensor("tmp16", [128, L], BF16))
    msk = _c(nc.sbuf_tensor("msk", [128, L], F32))
    scrA = _c(nc.sbuf_tensor("scrA", [128, 8 * GW], F32))
    ppan = _c(nc.sbuf_tensor("ppan", [128, 8 * GW], BF16))
    gT = _c(nc.sbuf_tensor("gT", [128, 3 * 4096], BF16))
    gV = _c(nc.sbuf_tensor("gV", [128, 32 * 256], BF16))
    vals = _c(nc.sbuf_tensor("vals", [128, 32], F32))
    idxu = _c(nc.sbuf_tensor("idxu", [128, 32], U32))
    idxf = _c(nc.sbuf_tensor("idxf", [128, 32], F32))
    idxS = _c(nc.sbuf_tensor("idxS", [16, 256], F32))
    idxA = _c(nc.sbuf_tensor("idxA", [128, 256], I16))
    idxB = _c(nc.sbuf_tensor("idxB", [128, 256], I16))
    idxAc = [_c(nc.sbuf_tensor(f"idxAc{c}", [128, 32], I16)) for c in range(8)]
    idxBc = [_c(nc.sbuf_tensor(f"idxBc{c}", [128, 64], I16)) for c in range(4)]
    pbTA = _c(nc.sbuf_tensor("pbTA", [128, 128], BF16))
    pbTB = _c(nc.sbuf_tensor("pbTB", [64, 128], BF16))
    pgT = _c(nc.sbuf_tensor("pgT", [128, 128], BF16))
    olt0 = _c(nc.sbuf_tensor("olt0", [128, NH * 128], BF16))
    olt1 = _c(nc.sbuf_tensor("olt1", [128, NH * 128], BF16))
    oat = _c(nc.sbuf_tensor("oat", [64, NH * 128], BF16))
    xt = _c(nc.sbuf_tensor("xt", [128, D], F32))
    ft = _c(nc.sbuf_tensor("ft", [128, D], F32))
    smax = _c(nc.sbuf_tensor("smax", [128, 8], F32))
    ssum = _c(nc.sbuf_tensor("ssum", [128, 8], F32))
    if True:
        olt = [olt0, olt1]
        bb.op("vector", "memset", iacc[:], 0.0, w=(iacc,))
        bb.op("vector", "memset", msk[:], 0.0, w=(msk,))
        nidx_reg = nc.gpsimd.alloc_register("nidx")
        nc.gpsimd.reg_mov(nidx_reg, 1024)
        nidx512 = nc.gpsimd.alloc_register("nidx512")
        nc.gpsimd.reg_mov(nidx512, 512)
        bb.op("vector", "memset", idxA[:], 0, w=(idxA,))
        bb.op("vector", "memset", idxB[:], 0, w=(idxB,))
        gT6 = gT[:].rearrange("p (rh c2 j q rl) -> p rh c2 j q rl",
                              rh=2, c2=4, j=3, q=32, rl=16)
        gV3 = gV[:].rearrange("p (c e) -> p c e", c=32, e=256)
        trigk_b = (sSt["trigk"][:].rearrange("p (rh rl) -> p rh rl", rh=2, rl=16)
                   .unsqueeze(2).broadcast_to([64, 2, 128, 16]))
        QTSb = (QTS, QTS2)
        MQSb = (MQS, MQS2)
        st8 = {"prev": None, "g1": [], "g2": []}

        def emit_A1(ti):
            t0 = ti * 128
            QTSl, MQSl = QTSb[ti % 2], MQSb[ti % 2]
            for g in range(2):
                bb.dma("sync", "ld", QTSl[:, g * 1024:(g + 1) * 1024],
                       dQT[g, :, t0 * 8:t0 * 8 + 1024], w=(QTSl,))
            for var in range(2):
                bb.dma("sync", "ld", MQSl[0:64, var * 1024:(var + 1) * 1024],
                       dMUQ[var, :, t0 * 8:t0 * 8 + 1024], w=(MQSl,))
            Wm = 128 if ti == 0 else t0 + 64
            nI = (Wm + 511) // 512
            for h in range(NIH):
                base = (h % 2) * 64
                lhs = sQiT[h // 2][base:base + 64, t0:t0 + 128]
                for nk in range(nI):
                    wseg = min(512, Wm - nk * 512)
                    pix = pI if (h * nI + nk) % 2 == 0 else pGath
                    bb.mm([(pix[:, 0:wseg], lhs,
                            sKiT[base:base + 64, nk * 512:nk * 512 + wseg],
                            True, True)], r=(sQiT[h // 2], sKiT), w=(pix,))
                    bb.op("scalar", "activation", tmp16[:, nk * 512:nk * 512 + wseg],
                          pix[:, 0:wseg], AF.Relu, r=(pix,), w=(tmp16,))
                if h == 0:
                    bb.op("vector", "scalar_tensor_tensor", iacc[:, 0:Wm],
                          tmp16[:, 0:Wm], sIdxw[:, 0:1], sSt["pert"][:, 0:Wm],
                          ALU.mult, ALU.add, r=(tmp16, sIdxw, sSt["pert"]),
                          w=(iacc,))
                else:
                    bb.op("vector", "scalar_tensor_tensor", iacc[:, 0:Wm],
                          tmp16[:, 0:Wm], sIdxw[:, h:h + 1], iacc[:, 0:Wm],
                          ALU.mult, ALU.add, r=(tmp16, sIdxw, iacc), w=(iacc,))
            if ti == 1 and "iacc1" in dDbg:
                bb.dma("sync", "dbg", dDbg["iacc1"][:], iacc[:], r=(iacc,))
            bb.op("gpsimd", "affine_select", msk[:, 0:Wm], iacc[:, 0:Wm],
                  pattern=[[-1, Wm]], compare_op=ALU.is_ge, fill=-2.0e8,
                  base=t0 - 64, channel_multiplier=1, r=(iacc,), w=(msk,))
            bb.op("gpsimd", "affine_select", iacc[:, 0:Wm], sSt["fut"][:, 0:Wm],
                  pattern=[[1, Wm]], compare_op=ALU.is_ge, fill=0.0,
                  base=-t0 - 1, channel_multiplier=-1, r=(sSt["fut"],),
                  w=(iacc,))
            bb.op("vector", "tensor_tensor", msk[:, 0:Wm], msk[:, 0:Wm],
                  iacc[:, 0:Wm], ALU.add, r=(msk, iacc), w=(msk,))
            for it in range(4):
                bb.op("vector", "max", out=vals[:, it * 8:(it + 1) * 8],
                      in_=msk[:, 0:Wm], r=(msk,), w=(vals,))
                bb.op("vector", "max_index", out=idxu[:, it * 8:(it + 1) * 8],
                      in_max=vals[:, it * 8:(it + 1) * 8], in_values=msk[:, 0:Wm],
                      r=(msk, vals), w=(idxu,))
                if it < 3:
                    bb.op("vector", "match_replace", out=msk[:, 0:Wm],
                          in_to_replace=vals[:, it * 8:(it + 1) * 8],
                          in_values=msk[:, 0:Wm], imm_value=REPL, r=(msk, vals),
                          w=(msk,))
            if ti == 1 and "idx1" in dDbg:
                bb.dma("sync", "dbg", dDbg["idx1"][:], idxu[:], r=(idxu,))

        def emit_A1b(ti):
            bb.op("vector", "tensor_copy", idxf[:], idxu[:], r=(idxu,), w=(idxf,))
            rep = sSt["rep16"]
            for hr in range(2):
                bb.tr(pT32[0:16, 0:128], idxf[:, hr * 16:(hr + 1) * 16],
                      ident32[:], r=(idxf, ident32), w=(pT32,))
                bb.op("vector", "tensor_copy",
                      idxS[0:16, hr * 128:(hr + 1) * 128], pT32[0:16, 0:128],
                      r=(pT32,), w=(idxS,))
                bb.mm([(pT32[:, 0:128], rep[:],
                        idxS[0:16, hr * 128:(hr + 1) * 128], True, True)],
                      r=(sSt["rep16"], idxS), w=(pT32,))
                bb.op("vector", "tensor_copy",
                      idxA[:, hr * 128:(hr + 1) * 128], pT32[:, 0:128],
                      r=(pT32,), w=(idxA,))
            bb.op("vector", "tensor_copy",
                  idxB[:].rearrange("p (q r) -> p q r", q=128, r=2),
                  idxA[:].rearrange("p (r q) -> p r q", r=2, q=128)
                  .transpose([0, 2, 1]), r=(idxA,), w=(idxB,))
            for c in range(8):
                bb.op("vector", "tensor_copy", idxAc[c][:],
                      idxA[:, c * 32:(c + 1) * 32], r=(idxA,), w=(idxAc[c],))
            for c in range(4):
                bb.op("vector", "tensor_copy", idxBc[c][:],
                      idxB[:, c * 64:(c + 1) * 64], r=(idxB,), w=(idxBc[c],))

        def emit_G1(ti):
            if st8["prev"] is not None:
                bb.wait("gpsimd", st8["prev"])
            st8["g1"] = []
            for c in range(8):
                tok = bb.gather("gq", r=(sScr, idxAc[c]), w=(gT,),
                                out_ap=gT[:, c * 1536:(c + 1) * 1536]
                                .rearrange("p (j n) -> p j n", j=3, n=512),
                                in_ap=sScr[:], idxs_ap=idxAc[c][:],
                                num_idxs=512, num_idxs_reg=nidx512, elem_size=384,
                                transpose=True, sbuf_tokens_per_rank=128,
                                sbuf_free_dim_per_rank=768,
                                sbuf_free_dim_pad_per_rank=0, sbuf_byte_offset=0)
                st8["g1"].append(tok)

        def emit_G2(ti):
            for c in range(8):
                mu_sl = gT[0:64, c * 1536 + 1024:c * 1536 + 1536].rearrange(
                    "p (q rl) -> p q rl", q=32, rl=16)
                tk_sl = (sSt["trigk"][0:64, (c // 4) * 16:(c // 4) * 16 + 16]
                         .unsqueeze(1).broadcast_to([64, 32, 16]))
                bb.op("vector", "tensor_tensor", mu_sl, mu_sl, tk_sl, ALU.mult,
                      r=(gT, sSt["trigk"]), w=(gT,), after=tuple(st8["g1"]))
            st8["g2"] = []
            for c in range(4):
                tok = bb.gather(
                    "gq", r=(idxBc[c],), w=(gV,),
                    out_ap=gV3[:, c * 8:(c + 1) * 8, :], in_ap=dCKV[:],
                    idxs_ap=idxBc[c][:], num_idxs=1024,
                    num_idxs_reg=nidx_reg, elem_size=256)
                st8["g2"].append(tok)
            st8["prev"] = tok

        def emit_B1(ti):
            t0 = ti * 128
            QTSl, MQSl = QTSb[ti % 2], MQSb[ti % 2]
            g1toks = tuple(st8["g1"])
            for g in range(8):
                a, width, mname, qside = band_params(ti, g)
                off = g * 16
                lq = [QTSl[:, gg * 1024 + off * 8:gg * 1024 + off * 8 + 128]
                      for gg in range(2)]
                lm = [MQSl[0:64, v * 1024 + off * 8:v * 1024 + off * 8 + 128]
                      for v in range(2)]
                w0 = 128 * a
                if qside == 1:
                    bk_ap = sBK0[:, 0:width]
                else:
                    bk_ap = sBK[:, w0:w0 + width]
                bb.mm([(pBand[:, 0:width], lq[0], sCkvT[0][:, w0:w0 + width],
                        True, False),
                       (pBand[:, 0:width], lq[1], sCkvT[1][:, w0:w0 + width],
                        False, False),
                       (pBand[:, 0:width], lm[qside - 1], bk_ap, False, True)],
                      r=(QTSl, MQSl, sCkvT[0], sCkvT[1],
                         sBK0 if qside == 1 else sBK),
                      w=(pBand,))
                mask = sSt[mname]
                bb.op("vector", "scalar_tensor_tensor",
                      scrA[:, g * GW:g * GW + width], pBand[:, 0:width], 1.0,
                      mask[:, 0:width], ALU.mult, ALU.add, r=(pBand, mask),
                      w=(scrA,))
                if width < 192:
                    bb.op("vector", "memset", scrA[:, g * GW + width:g * GW + 192],
                          BIGNEG, w=(scrA,))
                if qside == 1:
                    bb.op("vector", "memset", scrA[:, g * GW + 192:(g + 1) * GW],
                          BIGNEG, w=(scrA,))
                else:
                    nap = [gT6[:, :, g // 2, jj,
                               (g % 2) * 16:(g % 2) * 16 + 16, :]
                           .transpose([0, 2, 1, 3]) for jj in range(3)]
                    pg = pGath if g % 2 == 0 else pI
                    bb.mm([(pg[:], lq[0], nap[0], True, False),
                           (pg[:], lq[1], nap[1], False, False),
                           (pg[:], lm[0], nap[2][0:64], False, True)],
                          r=(QTSl, MQSl, gT), w=(pg,), after=g1toks)
                    dg = sSt["diagmask"]
                    for Bq in range(4):
                        bb.op("vector", "scalar_tensor_tensor",
                              scrA[Bq * 32:(Bq + 1) * 32,
                                   g * GW + 192:(g + 1) * GW],
                              pg[Bq * 32:(Bq + 1) * 32, Bq * 128:(Bq + 1) * 128],
                              1.0, dg[Bq * 32:(Bq + 1) * 32, :], ALU.mult,
                              ALU.add, r=(pg, dg), w=(scrA,))
            if ti == 1 and "gt1" in dDbg:
                bb.dma("sync", "dbg", dDbg["gt1"][:], gT[:], r=(gT,))

        def emit_B2(ti):
            sc3 = scrA[:].rearrange("p (g n) -> p g n", g=8, n=GW)
            bb.op("vector", "tensor_reduce", smax[:], sc3, AX.X, ALU.max,
                  r=(scrA,), w=(smax,))
            bb.op("vector", "tensor_tensor", sc3, sc3,
                  smax[:].unsqueeze(2).broadcast_to([128, 8, GW]), ALU.subtract,
                  r=(scrA, smax), w=(scrA,))
            bb.op("scalar", "activation", ppan[:], scrA[:], AF.Exp, r=(scrA,),
                  w=(ppan,))
            pp3 = ppan[:].rearrange("p (g n) -> p g n", g=8, n=GW)
            bb.op("vector", "tensor_reduce", ssum[:], pp3, AX.X, ALU.add,
                  r=(ppan,), w=(ssum,))
            bb.op("vector", "reciprocal", ssum[:], ssum[:], r=(ssum,), w=(ssum,))
            bb.op("vector", "tensor_tensor", pp3, pp3,
                  ssum[:].unsqueeze(2).broadcast_to([128, 8, GW]), ALU.mult,
                  r=(ppan, ssum), w=(ppan,))
            if ti == 1:
                for nm, src in (("scr1", scrA), ("ppan1", ppan)):
                    if nm in dDbg:
                        bb.dma("sync", "dbg", dDbg[nm][:], src[:], r=(src,))

        def emit_B3(ti):
            g2toks = tuple(st8["g2"])
            for g in range(8):
                a, width, mname, qside = band_params(ti, g)
                gb = g * GW
                bb.tr(pT16[0:128, 0:128], ppan[:, gb:gb + 128], ident16[:],
                      r=(ppan, ident16), w=(pT16,))
                bb.op("scalar", "activation", pbTA[:], pT16[:, 0:128],
                      AF.Copy, r=(pT16,), w=(pbTA,))
                if width == 192:
                    bb.tr(pT16[0:64, 128:256], ppan[:, gb + 128:gb + 192],
                          ident16[:], r=(ppan, ident16), w=(pT16,))
                    bb.op("scalar", "activation", pbTB[:], pT16[0:64, 128:256],
                          AF.Copy, r=(pT16,), w=(pbTB,))
                if qside == 2:
                    bb.tr(pT16[0:128, 128:256], ppan[:, gb + 192:gb + GW],
                          ident16[:], r=(ppan, ident16), w=(pT16,))
                    bb.op("scalar", "activation", pgT[:], pT16[:, 128:256],
                          AF.Copy, r=(pT16,), w=(pgT,))
                for gam in range(2):
                    pl = pLat[:, gam * 128:(gam + 1) * 128]
                    ch = [(pl, sCkv[a][:, gam * 128:(gam + 1) * 128], pbTA[:],
                           True, False)]
                    if width == 192:
                        ch.append((pl, sCkv[a + 1][0:64, gam * 128:(gam + 1) * 128],
                                   pbTB[:], False, False))
                    ch[-1] = (*ch[-1][:4], qside == 1)
                    bb.mm(ch, r=(pbTA, pbTB, sCkv[a],
                                 sCkv[min(a + 1, NT - 1)]), w=(pLat,))
                    if qside == 2:
                        for C in range(4):
                            bb.mm([(pLat[:, gam * 128 + C * 32:gam * 128 + (C + 1) * 32],
                                    gV[:, (g * 4 + C) * 256 + gam * 128:
                                       (g * 4 + C) * 256 + (gam + 1) * 128],
                                    pgT[:, C * 32:(C + 1) * 32], False, C == 3)],
                                  r=(gV, pgT), w=(pLat,), after=g2toks)
                    bb.op("scalar", "activation",
                          olt[gam][:].rearrange("p (h t) -> p h t", h=NH, t=128)
                          [:, :, g * 16:(g + 1) * 16].transpose([0, 2, 1]),
                          pLat[:, gam * 128:(gam + 1) * 128]
                          .rearrange("p (dq h) -> p dq h", dq=16, h=NH),
                          AF.Copy, r=(pLat,), w=(olt[gam],))
            if ti == 1 and "gv1" in dDbg:
                bb.dma("sync", "dbg", dDbg["gv1"][:], gV[:], r=(gV,))
            if ti == 1 and "olt1" in dDbg:
                bb.dma("sync", "dbg", dDbg["olt1"][:], olt[0][:], r=(olt[0],))

        def emit_B4(ti):
            t0 = ti * 128
            for h in range(NH):
                pv = pLat[0:64, (h % 2) * 128:(h % 2) * 128 + 128]
                bb.mm([(pv, sWuv[gam][:, h * 64:(h + 1) * 64],
                        olt[gam][:, h * 128:(h + 1) * 128], gam == 0, gam == 1)
                       for gam in range(2)],
                      r=(sWuv[0], sWuv[1], olt[0], olt[1]), w=(pLat,))
                bb.op("scalar", "activation", oat[:, h * 128:(h + 1) * 128],
                      pv, AF.Copy, r=(pLat,), w=(oat,))
            bb.mm([(pOut[:, nch * 512:(nch + 1) * 512],
                    oat[0:64, h * 128:(h + 1) * 128],
                    sWout8[h][:, nch * 512:(nch + 1) * 512], h == 0, h == NH - 1)
                   for h in range(NH) for nch in range(2)],
                  r=(oat,) + tuple(sWout8), w=(pOut,))
            bb.dma("sync", "ld", xt[:], dX[t0:t0 + 128, :], w=(xt,))
            bb.dma("sync", "ld", ft[:], dFFN[t0:t0 + 128, :], w=(ft,))
            bb.op("vector", "scalar_tensor_tensor", ft[:], pOut[:], 1.0, ft[:],
                  ALU.mult, ALU.add, r=(pOut, ft), w=(ft,))
            bb.op("vector", "tensor_tensor", xt[:], xt[:], ft[:], ALU.add,
                  r=(xt, ft), w=(xt,))
            bb.dma("sync", "outq", dOut[t0:t0 + 128, :], xt[:], r=(xt,))
            if ti == 1:
                if "qts1" in dDbg:
                    bb.dma("sync", "dbg", dDbg["qts1"][:], QTSb[1][:],
                           r=(QTSb[1],))
                if "mqs1" in dDbg:
                    bb.dma("sync", "dbg", dDbg["mqs1"][:], MQSb[1][0:64, :],
                           r=(MQSb[1],))
                if "sca1" in dDbg:
                    bb.dma("sync", "dbg", dDbg["sca1"][:], scrA[:], r=(scrA,))

        emit_A1(0)
        emit_A1b(0)
        emit_G1(0)
        emit_G2(0)
        for ti in range(NT):
            emit_B1(ti)
            if ti + 1 < NT:
                emit_A1(ti + 1)
                emit_A1b(ti + 1)
                emit_G1(ti + 1)
            emit_B2(ti)
            emit_B3(ti)
            if ti + 1 < NT:
                emit_G2(ti + 1)
            emit_B4(ti)
        bb.barrier()
    _e.close()


def kernel(**inputs):
    statics = build_statics()
    nc, _dbg = build_program()
    x = np.ascontiguousarray(np.asarray(inputs["L0_attn_x"], dtype=np.float32))
    base = {}
    for nm, sh in WEIGHT_SHAPES.items():
        key = ("L0_ffn_" + nm) if nm.startswith("w_") else ("L0_attn_" + nm)
        base[nm] = np.ascontiguousarray(
            np.asarray(inputs[key], dtype=np.float32).reshape(sh))
    for nm, arr in statics.items():
        base["st_" + nm] = arr
    in_maps = [dict(base, x=x[b]) for b in range(8)]
    res = run_bass_kernel_spmd(nc, in_maps, core_ids=list(range(8)))
    out = np.stack([np.asarray(res.results[b]["out"]) for b in range(8)], axis=0)
    return out.astype(np.float32)


# version 18 (final)
# speedup vs baseline: 1.0983x; 1.0471x over previous
import os
from contextlib import ExitStack

import numpy as np
import ml_dtypes

import concourse.bass as bass
import concourse.mybir as mybir
from concourse.bass_utils import run_bass_kernel_spmd
from concourse.library_config import mlp as _MLP_LIB

F32 = mybir.dt.float32
BF16 = mybir.dt.bfloat16
I16 = mybir.dt.int16
I32 = mybir.dt.int32
U32 = mybir.dt.uint32
AF = mybir.ActivationFunctionType
ALU = mybir.AluOpType
AX = mybir.AxisListType

L, D, DCKV, DCQ = 2048, 1024, 256, 512
NH, DH, DR, HALF, NIH = 8, 64, 32, 16, 4
NT = L // 128
SCALE = float((DH + DR) ** -0.5)
ROPE_BASE = 10000.0
BIGNEG = -1.0e9
REPL = -1.0e30
GW = 320
_CONSTS = (1e-5, 1.0, -1.0, 1.0 / 1024.0, float((64 + 32) ** -0.5), 0.0, 127.0, 128.5)

ENGS = ("tensor", "vector", "scalar", "gpsimd", "sync")


def _trig4(angles):
    c, s = np.cos(angles), np.sin(angles)
    return np.concatenate([c, s, c, s], axis=0)


def build_statics():
    th = 1.0 / ROPE_BASE ** (2.0 * np.arange(HALF, dtype=np.float64) / DR)
    t = np.arange(L, dtype=np.float64)
    bf = ml_dtypes.bfloat16
    st = {}
    st["ident32"] = np.eye(128, dtype=np.float32)
    st["ident16"] = np.eye(128).astype(bf)
    st["ones16"] = np.ones((128, 128), dtype=bf)
    st["ones32"] = np.ones((128, 128), dtype=np.float32)
    st["tq1"] = _trig4(th[:, None] * t[None, :]).astype(bf)
    st["tq2"] = _trig4(2.0 * th[:, None] * t[None, :]).astype(bf)
    st["trigb"] = _trig4(th[:, None] * (t[None, :] + 63.0)).astype(bf)
    st["trigb0"] = _trig4(th[:, None] * t[None, :192]).astype(bf)
    st["trigk"] = _trig4(th[:, None] * (64.0 + np.arange(32.0)[None, :])).astype(bf)
    c2 = np.zeros((64, 64), dtype=np.float32)
    for m in range(16):
        c2[m, m] = 1.0
        c2[48 + m, m] = -1.0
    for m in range(16, 48):
        i = (m - 16) % 16
        c2[16 + i, m] = 1.0
        c2[32 + i, m] = 1.0
    for m in range(48, 64):
        i = m - 48
        c2[i, m] = -1.0
        c2[48 + i, m] = 1.0
    st["c2"] = c2.astype(bf)
    dup = np.zeros((32, 64), dtype=np.float32)
    for m in range(64):
        dup[(m % 16) if m < 32 else 16 + (m % 16), m] = 1.0
    st["dupr"] = np.tile(dup, (4, 1)).astype(bf)
    s = np.arange(L, dtype=np.float64)
    st["fut"] = np.tile((1.0e8 - 16.0 * s).astype(np.float32), (128, 1))
    st["pert"] = np.tile((-1.0e-20 * (s + 1.0)).astype(np.float32), (128, 1))
    p = np.arange(128)
    dq = p // 8
    col = np.arange(192)
    for v in range(8):
        lo = dq[:, None] + 1 + 16 * v
        hi = dq[:, None] + 64 + 16 * v
        st[f"bm{v}"] = np.where((col[None, :] >= lo) & (col[None, :] <= hi),
                                0.0, BIGNEG).astype(np.float32)
    st["mst96"] = np.tile(np.where(col < 96, 0.0, BIGNEG).astype(np.float32),
                          (128, 1))
    colg = np.arange(128)
    st["diagmask"] = np.where((colg // 32)[None, :] == (dq % 4)[:, None],
                              0.0, BIGNEG).astype(np.float32)
    st["constv"] = np.tile(np.array(_CONSTS, dtype=np.float32), (128, 1))
    st["rep16"] = (np.arange(128)[None, :] % 16 == np.arange(16)[:, None]
                   ).astype(np.float32)
    return st


STATIC_SHAPES = {
    "ident32": ([128, 128], F32), "ident16": ([128, 128], BF16),
    "ones16": ([128, 128], BF16), "ones32": ([128, 128], F32),
    "tq1": ([64, L], BF16), "tq2": ([64, L], BF16), "trigb": ([64, L], BF16),
    "trigb0": ([64, 192], BF16), "trigk": ([64, 32], BF16),
    "c2": ([64, 64], BF16), "dupr": ([128, 64], BF16),
    "fut": ([128, L], F32), "pert": ([128, L], F32),
    **{f"bm{v}": ([128, 192], F32) for v in range(8)},
    "mst96": ([128, 192], F32), "diagmask": ([128, 128], F32),
    "constv": ([128, 8], F32), "rep16": ([16, 128], F32),
}

WEIGHT_SHAPES = {
    "W_dkv": (D, DCKV), "W_uk": (DCKV, 512), "W_uv": (DCKV, 512),
    "W_dq": (D, DCQ), "W_uq": (DCQ, 512), "W_qr": (DCQ, 256),
    "W_kr": (D, DR), "W_out": (512, D), "idx_wq": (D, 256),
    "idx_wk": (D, DH), "idx_w": (1, NIH), "raw_delta": (1, HALF),
    "w_gate": (4096, D), "w_up": (4096, D), "w_down": (D, 4096),
}


class BB:

    NSLOT = 8

    def __init__(self, nc, stack):
        self.nc = nc
        self.cnt = {e: 0 for e in ENGS}
        self.water = {e: {p: 0 for p in ENGS} for e in ENGS}
        self.sem = {e: stack.enter_context(nc.semaphore(f"es_{e}")) for e in ENGS}
        self.dq = {}
        self.dwater = {e: {} for e in ENGS}
        self.stack = stack
        self.buf = {}

    def eng(self, n):
        return getattr(self.nc, n)

    def _drec(self, q):
        if q not in self.dq:
            sems = [self.stack.enter_context(self.nc.semaphore(f"ds_{q}{i}"))
                    for i in range(self.NSLOT)]
            self.dq[q] = {"sems": sems, "uses": [0] * self.NSLOT, "idx": 0}
        return self.dq[q]

    def _dma_slot(self, eng, q):
        rec = self._drec(q)
        i = rec["idx"]
        rec["idx"] = (i + 1) % self.NSLOT
        if rec["uses"][i] > 0:
            self.wait(eng, ("D", (q, i), 16 * rec["uses"][i]))
        rec["uses"][i] += 1
        return rec["sems"][i], ("D", (q, i), 16 * rec["uses"][i])

    def _rec(self, h):
        k = id(h)
        if k not in self.buf:
            self.buf[k] = {"w": None, "r": []}
        return self.buf[k]

    def _pre(self, eng, reads, writes, extra):
        toks = list(extra)
        for h in reads:
            rr = self._rec(h)
            if rr["w"] is not None:
                toks.append(rr["w"])
        for h in writes:
            rr = self._rec(h)
            if rr["w"] is not None:
                toks.append(rr["w"])
            toks.extend(rr["r"])
        self.wait(eng, *toks)

    def _post(self, tok, reads, writes):
        for h in writes:
            rr = self._rec(h)
            rr["w"] = tok
            rr["r"] = []
        for h in reads:
            rr = self._rec(h)
            if tok not in rr["r"]:
                rr["r"].append(tok)

    def op(self, eng, method, *args, r=(), w=(), after=(), **kwargs):
        self._pre(eng, r, w, after)
        inst = getattr(self.eng(eng), method)(*args, **kwargs)
        self.cnt[eng] += 1
        inst.then_inc(self.sem[eng], 1)
        tok = (eng, self.cnt[eng])
        self._post(tok, r, w)
        return tok

    def mm(self, chain, r=(), w=(), after=(), tile_position=None):
        self._pre("tensor", r, w, after)
        for i, (o, lh, rh, s0, s1) in enumerate(chain):
            inst = self.nc.tensor.matmul(o, lh, rh, start=s0, stop=s1,
                                         tile_position=tile_position)
            if i == len(chain) - 1:
                self.cnt["tensor"] += 1
                inst.then_inc(self.sem["tensor"], 1)
        tok = ("tensor", self.cnt["tensor"])
        self._post(tok, r, w)
        return tok

    def tr(self, out, in_, ident, r=(), w=(), after=()):
        self._pre("tensor", r, w, after)
        inst = self.nc.tensor.transpose(out, in_, ident)
        self.cnt["tensor"] += 1
        inst.then_inc(self.sem["tensor"], 1)
        tok = ("tensor", self.cnt["tensor"])
        self._post(tok, r, w)
        return tok

    def dma(self, eng, q, out, in_, r=(), w=(), after=(), transpose=False, **kw):
        self._pre(eng, r, w, after)
        sem, tok = self._dma_slot(eng, q)
        if transpose:
            inst = self.eng(eng).dma_start_transpose(out, in_)
        else:
            inst = self.eng(eng).dma_start(out=out, in_=in_, **kw)
        inst.then_inc(sem, 16)
        self._post(tok, r, w)
        return tok

    def gather(self, q, r=(), w=(), after=(), **kwargs):
        self._pre("gpsimd", r, w, after)
        sem, tok = self._dma_slot("gpsimd", q)
        inst = self.nc.gpsimd.dma_gather(**kwargs)
        inst.then_inc(sem, 16)
        self._post(tok, r, w)
        return tok

    def wait(self, eng, *tokens):
        e = self.eng(eng)
        dmax, emax = {}, {}
        for t in tokens:
            if t is None:
                continue
            if t[0] == "D":
                _, nm, v = t
                dmax[nm] = max(dmax.get(nm, 0), v)
            else:
                p, v = t
                emax[p] = max(emax.get(p, 0), v)
        for nm, v in dmax.items():
            wt = self.dwater[eng]
            if wt.get(nm, 0) < v:
                q, i = nm
                e.wait_ge(self.dq[q]["sems"][i], v)
                wt[nm] = v
        for p, v in emax.items():
            if self.water[eng][p] < v:
                e.wait_ge(self.sem[p], v)
                self.water[eng][p] = v

    def barrier(self):
        toks = [(e, self.cnt[e]) for e in ENGS]
        dtoks = [("D", (q, i), 16 * rec["uses"][i])
                 for q, rec in self.dq.items()
                 for i in range(self.NSLOT) if rec["uses"][i] > 0]
        for e in ENGS:
            self.wait(e, *toks, *dtoks)

    def fence(self, queues=()):
        toks = [(e, self.cnt[e]) for e in ENGS]
        dtoks = [("D", (q, i), 16 * self.dq[q]["uses"][i])
                 for q in queues if q in self.dq
                 for i in range(self.NSLOT) if self.dq[q]["uses"][i] > 0]
        for e in ENGS:
            self.wait(e, *toks, *dtoks)

    def qtokens(self, q):
        if q not in self.dq:
            return []
        rec = self.dq[q]
        return [("D", (q, i), 16 * rec["uses"][i])
                for i in range(self.NSLOT) if rec["uses"][i] > 0]


def band_params(ti, g):
    q0 = 128 * ti + 16 * g
    if ti == 0 and g < 4:
        return 0, 192, "mst96", 1
    a = (q0 - 64) // 128
    width = min(192, L - 128 * a)
    v = (q0 - 128 * a - 64) // 16
    return a, width, f"bm{v}", 2


def build_program(debug=()):
    nc = bass.Bass("TRN2", target_bir_lowering=False)

    dX = nc.dram_tensor("x", [L, D], F32, kind="ExternalInput").ap()
    dOut = nc.dram_tensor("out", [L, D], F32, kind="ExternalOutput").ap()
    dW = {nm: nc.dram_tensor(nm, list(sh), F32, kind="ExternalInput").ap()
          for nm, sh in WEIGHT_SHAPES.items()}
    dS = {nm: nc.dram_tensor("st_" + nm, list(sh), dt_, kind="ExternalInput").ap()
          for nm, (sh, dt_) in STATIC_SHAPES.items()}
    dH2U = nc.dram_tensor("h2u", [L, 4096], BF16, kind="Internal").ap()
    dCKV = nc.dram_tensor("ckvscr", [L, DCKV], BF16, kind="Internal").ap()
    dFFN = nc.dram_tensor("ffnout", [L, D], F32, kind="Internal").ap()
    dQT = nc.dram_tensor("qtscr", [2, 128, NH * L], BF16, kind="Internal").ap()
    dMUQ = nc.dram_tensor("muqscr", [2, 64, NH * L], BF16, kind="Internal").ap()
    dDbg = {nm: nc.dram_tensor("dbg_" + nm, list(shp), dt_, kind="ExternalOutput").ap()
            for nm, shp, dt_ in debug}

    est = ExitStack()
    bb = BB(nc, est)
    A = nc.alloc_sbuf_tensor

    sSt = {nm: A("s_" + nm, list(STATIC_SHAPES[nm][0]), STATIC_SHAPES[nm][1])
           for nm in dS if nm not in ("tq1", "tq2", "trigb", "fut", "pert")}
    sCkvT = [A(f"ckvT{g}", [128, L], BF16) for g in range(2)]
    sCkv = [A(f"ckv{t}", [128, DCKV], BF16) for t in range(NT)]
    sScr = A("scr", [128, 16 * 384], BF16)
    sKiT = A("kiT", [128, L], BF16)
    sQiT = [A(f"qiT{g}", [128, L], BF16) for g in range(2)]
    sWuv = [A(f"wuv{g}", [128, 512], BF16) for g in range(2)]
    sWout8 = [A(f"wout{h}", [64, D], BF16) for h in range(NH)]
    sBK = A("bk", [64, L], BF16)
    sBK0 = A("bk0", [64, 192], BF16)
    sStats = A("stats", [128, 2 * NT], F32)
    sIdxw = A("idxw", [128, NIH], F32)
    sSw = A("swv", [128, 8], F32)
    sRm = A("rmaxs", [128, NT], F32)
    sMu = A("mus", [128, NT], F32)
    sQs = A("qss", [128, NT], F32)

    with est, nc.Block():
        nc.gpsimd.load_library(_MLP_LIB)
        for nm in dS:
            if nm in ("tq1", "tq2", "trigb", "fut", "pert"):
                continue
            bb.dma("sync", "ld", sSt[nm][:], dS[nm][:], w=(sSt[nm],))
        for ci, cv in enumerate(_CONSTS):
            nc.const_aps.register(F32, float(cv), sSt["constv"][:, ci:ci + 1])
        bb.dma("sync", "ld", sIdxw[0:1, :], dW["idx_w"][:], w=(sIdxw,))
        bb.op("gpsimd", "partition_broadcast", sIdxw[:], sIdxw[0:1, :], 128,
              r=(sIdxw,), w=(sIdxw,))
        bb.barrier()
        bb.op("vector", "memset", sStats[:], 0.0, w=(sStats,))
        bb.op("vector", "memset", sSw[:], 0.0, w=(sSw,))
        bb.op("vector", "memset", sRm[:], 0.0, w=(sRm,))

        bb.op("vector", "memset", sScr[:], 0.0, w=(sScr,))
        with nc.sbuf_tensor("xnT", [128, 8 * L], BF16) as xnT:
            mest = ExitStack()
            with mest:
                _phase2_means(nc, bb, dW, sSw, sSt, dDbg, mest)
                _phase1_ln(nc, bb, dX, xnT, sStats, sMu, sQs, sSt, dDbg)
                bb.fence(queues=("ld",))
                _phase3_proj(nc, bb, dW, dS, xnT, sCkvT, sCkv, sScr, sKiT,
                             sQiT, sWuv, sWout8, sBK, sBK0, sSt, dCKV, dQT,
                             dMUQ, dDbg)
            with nc.sbuf_tensor("xqT", [128, 8 * L], BF16) as xqT:
                _phase1_xq(nc, bb, dX, xqT, sMu, sQs, sSt, dDbg)
                _phase2_gateup(nc, bb, dW, dH2U, xqT, sStats, sSw, sRm, sSt,
                               dDbg)
        _phase2_down(nc, bb, dW, dH2U, dFFN, sStats, sSw, sRm, sSt, dDbg)
        _phase4_attn(nc, bb, dX, dOut, dFFN, dCKV, dQT, dMUQ, sCkvT, sCkv,
                     sScr, sKiT, sQiT, sWuv, sWout8, sBK, sBK0, sSt, dS,
                     sStats, sIdxw, dDbg)
    mybir.codegen_inst_isa_subclasses(nc)
    return nc, [nm for nm, _, _ in debug]


def _phase1_ln(nc, bb, dX, xnT, sStats, sMu, sQs, sSt, dDbg):
    ident16 = sSt["ident16"]
    with (
        nc.sbuf_tensor("xtmA", [128, D], F32) as xtmA,
        nc.sbuf_tensor("xtmB", [128, D], F32) as xtmB,
        nc.sbuf_tensor("xctm", [128, D], F32) as xctm,
        nc.sbuf_tensor("sqtm", [128, D], F32) as sqtm,
        nc.sbuf_tensor("xn16A", [128, D], BF16) as xn16A,
        nc.sbuf_tensor("xn16B", [128, D], BF16) as xn16B,
        nc.sbuf_tensor("c1", [128, 1], F32) as c1,
        nc.sbuf_tensor("c2c", [128, 1], F32) as c2c,
        nc.sbuf_tensor("c3", [128, 1], F32) as c3,
        nc.psum_tensor("pTa", [128, 64], F32) as pTa,
        nc.psum_tensor("pTb", [128, 64], F32) as pTb,
    ):
        pT = (pTa, pTb)
        xnT3 = xnT[:].rearrange("p (a t) -> p a t", a=8, t=L)
        for ti in range(NT):
            r0 = ti * 128
            xtm = xtmA if ti % 2 == 0 else xtmB
            xn16 = xn16A if ti % 2 == 0 else xn16B
            bb.dma("sync", "ld", xtm[:], dX[r0:r0 + 128, :], w=(xtm,))
            bb.op("vector", "tensor_reduce", c1[:], xtm[:], AX.X, ALU.add,
                  r=(xtm,), w=(c1,))
            bb.op("vector", "tensor_scalar", c1[:], c1[:], 1.0 / D, None, ALU.mult,
                  r=(c1,), w=(c1,))
            bb.op("vector", "tensor_copy", sMu[:, ti:ti + 1], c1[:], r=(c1,),
                  w=(sMu,))
            bb.op("vector", "tensor_scalar", xctm[:], xtm[:], c1[:], None,
                  ALU.subtract, r=(xtm, c1), w=(xctm,))
            bb.op("scalar", "activation", sqtm[:], xctm[:], AF.Square,
                  r=(xctm,), w=(sqtm,))
            bb.op("vector", "tensor_reduce", c2c[:], sqtm[:], AX.X, ALU.add,
                  r=(sqtm,), w=(c2c,))
            bb.op("scalar", "activation", c2c[:], c2c[:], AF.Sqrt, bias=1e-5,
                  scale=1.0 / D, r=(c2c,), w=(c2c,))
            bb.op("vector", "reciprocal", c2c[:], c2c[:], r=(c2c,), w=(c2c,))
            bb.op("vector", "tensor_reduce", c3[:], xctm[:], AX.X, ALU.max,
                  apply_absolute_value=True, r=(xctm,), w=(c3,))
            bb.op("vector", "reciprocal", sQs[:, ti:ti + 1], c3[:], r=(c3,),
                  w=(sQs,))
            bb.op("vector", "tensor_scalar", sQs[:, ti:ti + 1],
                  sQs[:, ti:ti + 1], 127.0, None, ALU.mult, r=(sQs,), w=(sQs,))
            bb.op("scalar", "activation", xn16[:], xctm[:], AF.Copy,
                  scale=c2c[:], r=(xctm, c2c), w=(xn16,))
            bb.op("vector", "tensor_scalar", sStats[:, ti:ti + 1], c3[:], c2c[:],
                  1.0 / 127.0, ALU.mult, ALU.mult, r=(c3, c2c), w=(sStats,))
            for k in range(8):
                pt = pT[k % 2]
                bb.tr(pt[:].bitcast(BF16)[:, 0:128],
                      xn16[:, k * 128:(k + 1) * 128], ident16[:],
                      r=(xn16, ident16), w=(pt,))
                bb.op("scalar", "activation", xnT3[:, k, r0:r0 + 128],
                      pt[:].bitcast(BF16)[:, 0:128], AF.Copy, r=(pt,), w=(xnT,))
    if "xnT" in dDbg:
        bb.dma("sync", "dbg", dDbg["xnT"][:], xnT[:], r=(xnT,))


def _phase1_xq(nc, bb, dX, xqT, sMu, sQs, sSt, dDbg):
    ident16 = sSt["ident16"]
    with (
        nc.sbuf_tensor("qxtA", [128, D], F32) as qxtA,
        nc.sbuf_tensor("qxtB", [128, D], F32) as qxtB,
        nc.sbuf_tensor("qsq", [128, D], F32) as qsq,
        nc.sbuf_tensor("qx32", [128, D], I32) as qx32,
        nc.sbuf_tensor("qxgt", [128, D], BF16) as qxgt,
        nc.sbuf_tensor("xq16A", [128, D], BF16) as xq16A,
        nc.sbuf_tensor("xq16B", [128, D], BF16) as xq16B,
        nc.psum_tensor("pTg", [128, 64], F32) as pTg,
        nc.psum_tensor("pTh", [128, 64], F32) as pTh,
    ):
        pT = (pTg, pTh)
        xqT3 = xqT[:].rearrange("p (a t) -> p a t", a=8, t=L)
        for ti in range(NT):
            r0 = ti * 128
            xtm = qxtA if ti % 2 == 0 else qxtB
            xq16 = xq16A if ti % 2 == 0 else xq16B
            bb.dma("sync", "ld", xtm[:], dX[r0:r0 + 128, :], w=(xtm,))
            bb.op("vector", "tensor_scalar", xtm[:], xtm[:], sMu[:, ti:ti + 1],
                  None, ALU.subtract, r=(xtm, sMu), w=(xtm,))
            bb.op("vector", "tensor_scalar", qsq[:], xtm[:], sQs[:, ti:ti + 1],
                  0.5, ALU.mult, ALU.add, r=(xtm, sQs), w=(qsq,))
            bb.op("scalar", "activation", qx32[:], qsq[:], AF.Copy,
                  r=(qsq,), w=(qx32,))
            bb.op("vector", "tensor_tensor", qxgt[:], qx32[:], qsq[:],
                  ALU.is_gt, r=(qx32, qsq), w=(qxgt,))
            bb.op("vector", "tensor_tensor", xq16[:], qx32[:], qxgt[:],
                  ALU.subtract, r=(qx32, qxgt), w=(xq16,))
            for k in range(8):
                pt = pT[k % 2]
                bb.tr(pt[:].bitcast(BF16)[:, 0:128],
                      xq16[:, k * 128:(k + 1) * 128], ident16[:],
                      r=(xq16, ident16), w=(pt,))
                bb.op("vector", "tensor_copy", xqT3[:, k, r0:r0 + 128],
                      pt[:].bitcast(BF16)[:, 0:128], r=(pt,), w=(xqT,))
    if "xqT" in dDbg:
        bb.dma("sync", "dbg", dDbg["xqT"][:], xqT[:], r=(xqT,))


def _quant_tern(nc, bb, out16, wtile, recip_ap, qf32, qi32, qg, sSw):
    bb.op("vector", "tensor_scalar", qf32[:], wtile[:], recip_ap, 1.4999,
          ALU.mult, ALU.min, r=(wtile, sSw), w=(qf32,))
    bb.op("vector", "tensor_scalar", qf32[:], qf32[:], -1.4999, 0.5,
          ALU.max, ALU.add, r=(qf32,), w=(qf32,))
    bb.op("vector", "tensor_copy", qi32[:], qf32[:], r=(qf32,), w=(qi32,))
    bb.op("vector", "tensor_tensor", qg[:], qi32[:], qf32[:], ALU.is_gt,
          r=(qi32, qf32), w=(qg,))
    bb.op("vector", "tensor_tensor", out16[:], qi32[:], qg[:], ALU.subtract,
          r=(qi32, qg), w=(out16,))


def _phase2_means(nc, bb, dW, sSw, sSt, dDbg, est):
    sOnes = sSt["ones16"]
    _c = est.enter_context
    pM = _c(nc.psum_tensor("pM", [128, 8], F32))
    mts = [_c(nc.sbuf_tensor(f"mt{i}", [128, 512], F32)) for i in range(4)]
    macc = _c(nc.sbuf_tensor("macc", [128, 4], F32))
    msum = _c(nc.sbuf_tensor("msum", [128, 1], F32))
    bb.op("vector", "memset", macc[:], 0.0, w=(macc,))
    for wi, wn in enumerate(("w_gate", "w_up", "w_down")):
        rows, cols = WEIGHT_SHAPES[wn]
        blocks = [(k * 128, cc * 512) for k in range(rows // 128)
                  for cc in range(cols // 512)]
        for bi, (rr, cc) in enumerate(blocks):
            wt = mts[bi % 4]
            bb.dma("sync", f"wm{bi % 4}", wt[:],
                   dW[wn][rr:rr + 128, cc:cc + 512], w=(wt,))
            bb.op("vector", "tensor_reduce", msum[:], wt[:], AX.X, ALU.add,
                  apply_absolute_value=True, r=(wt,), w=(msum,))
            if bi == 0:
                bb.op("vector", "tensor_copy", macc[:, wi:wi + 1], msum[:],
                      r=(msum,), w=(macc,))
            else:
                bb.op("vector", "tensor_tensor", macc[:, wi:wi + 1],
                      macc[:, wi:wi + 1], msum[:], ALU.add, r=(macc, msum),
                      w=(macc,))
    bb.mm([(pM[:, 0:4], sSt["ones32"][:], macc[:], True, True)],
          r=(sSt["ones32"], macc), w=(pM,))
    for wi, wn in enumerate(("w_gate", "w_up", "w_down")):
        rows, cols = WEIGHT_SHAPES[wn]
        bb.op("vector", "tensor_scalar", sSw[:, wi:wi + 1], pM[:, wi:wi + 1],
              1.0 / (rows * cols), 1e-5, ALU.mult, ALU.max, r=(pM,), w=(sSw,))
        bb.op("vector", "reciprocal", sSw[:, 3 + wi:4 + wi],
              sSw[:, wi:wi + 1], r=(sSw,), w=(sSw,))
    if "sw" in dDbg:
        bb.dma("sync", "dbg", dDbg["sw"][:], sSw[:], r=(sSw,))


def _phase2_gateup(nc, bb, dW, dH2U, xqT, sStats, sSw, sRm, sSt, dDbg):
    ident16 = sSt["ident16"]
    xqT3 = xqT[:].rearrange("p (a t) -> p a t", a=8, t=L)
    bb.barrier()
    _e = ExitStack()
    _c = _e.enter_context
    pGs = [_c(nc.psum_tensor(f"pG{i}", [128, 512], F32)) for i in range(2)]
    pUs = [_c(nc.psum_tensor(f"pU{i}", [128, 512], F32)) for i in range(2)]
    pTa = _c(nc.psum_tensor("pTe", [128, 64], F32))
    pTb = _c(nc.psum_tensor("pTf", [128, 64], F32))
    ternTg = _c(nc.sbuf_tensor("ternTg", [128, 8 * 512], BF16))
    ternTu = _c(nc.sbuf_tensor("ternTu", [128, 8 * 512], BF16))
    wtA = _c(nc.sbuf_tensor("wtA", [128, 1024], F32))
    wtB = _c(nc.sbuf_tensor("wtB", [128, 1024], F32))
    qf32s = [_c(nc.sbuf_tensor(f"qf32{i}", [128, 1024], F32)) for i in range(2)]
    qi32s = [_c(nc.sbuf_tensor(f"qi32{i}", [128, 1024], I32)) for i in range(2)]
    qgs = [_c(nc.sbuf_tensor(f"qg{i}", [128, 1024], BF16)) for i in range(2)]
    wq3A = _c(nc.sbuf_tensor("wq3A", [128, 1024], BF16))
    wq3B = _c(nc.sbuf_tensor("wq3B", [128, 1024], BF16))
    gtA = _c(nc.sbuf_tensor("gtA", [128, 512], BF16))
    gtB = _c(nc.sbuf_tensor("gtB", [128, 512], BF16))
    utA = _c(nc.sbuf_tensor("utA", [128, 512], BF16))
    utB = _c(nc.sbuf_tensor("utB", [128, 512], BF16))
    h2sA = _c(nc.sbuf_tensor("h2sA", [128, 512], BF16))
    h2sB = _c(nc.sbuf_tensor("h2sB", [128, 512], BF16))
    gsc = _c(nc.sbuf_tensor("gsc", [128, 1], F32))
    cmx = _c(nc.sbuf_tensor("cmx", [128, 1], F32))
    with _e:
        tg3 = ternTg[:].rearrange("p (a o) -> p a o", a=8, o=512)
        tu3 = ternTu[:].rearrange("p (a o) -> p a o", a=8, o=512)
        pT = (pTa, pTb)
        for sl in range(8):
            for wi, wn, tT3, tTh in ((0, "w_gate", tg3, ternTg),
                                     (1, "w_up", tu3, ternTu)):
                for j in range(4):
                    par = (wi * 4 + j) % 2
                    wt = wtA if par == 0 else wtB
                    wq3 = wq3A if par == 0 else wq3B
                    o0 = sl * 512 + j * 128
                    bb.dma("sync", f"wm{par}", wt[:], dW[wn][o0:o0 + 128, :],
                           w=(wt,))
                    _quant_tern(nc, bb, wq3, wt, sSw[:, 3 + wi:4 + wi],
                                qf32s[par], qi32s[par], qgs[par], sSw)
                    for k in range(8):
                        pt = pT[k % 2]
                        bb.tr(pt[:].bitcast(BF16)[:, 0:128],
                              wq3[:, k * 128:(k + 1) * 128], ident16[:],
                              r=(wq3, ident16), w=(pt,))
                        bb.op("scalar", "activation",
                              tT3[:, k, j * 128:(j + 1) * 128],
                              pt[:].bitcast(BF16)[:, 0:128], AF.Copy,
                              r=(pt,), w=(tTh,))
            for ti in range(NT):
                t0 = ti * 128
                par = ti % 2
                gt = gtA if par == 0 else gtB
                ut = utA if par == 0 else utB
                h2s = h2sA if par == 0 else h2sB
                pG = pGs[ti % 2]
                pU = pUs[ti % 2]
                bb.op("vector", "tensor_scalar", gsc[:], sStats[:, ti:ti + 1],
                      sSw[:, 0:1], None, ALU.mult, r=(sStats, sSw), w=(gsc,))
                bb.mm([(pG[:], xqT3[:, k, t0:t0 + 128], tg3[:, k, :], k == 0,
                        k == 7) for k in range(8)], r=(xqT, ternTg), w=(pG,))
                bb.mm([(pU[:], xqT3[:, k, t0:t0 + 128], tu3[:, k, :], k == 0,
                        k == 7) for k in range(8)], r=(xqT, ternTu), w=(pU,))
                bb.op("scalar", "activation", gt[:], pG[:], AF.Silu,
                      scale=gsc[:], r=(pG, gsc), w=(gt,))
                bb.op("scalar", "activation", ut[:], pU[:], AF.Copy,
                      r=(pU,), w=(ut,))
                bb.op("vector", "tensor_tensor", h2s[:], gt[:], ut[:],
                      ALU.mult, r=(gt, ut), w=(h2s,))
                if sl == 0:
                    bb.op("vector", "tensor_reduce", sRm[:, ti:ti + 1], h2s[:],
                          AX.X, ALU.max, apply_absolute_value=True,
                          r=(h2s,), w=(sRm,))
                else:
                    bb.op("vector", "tensor_reduce", cmx[:], h2s[:], AX.X,
                          ALU.max, apply_absolute_value=True, r=(h2s,),
                          w=(cmx,))
                    bb.op("vector", "tensor_tensor", sRm[:, ti:ti + 1],
                          sRm[:, ti:ti + 1], cmx[:], ALU.max, r=(sRm, cmx),
                          w=(sRm,))
                bb.dma("sync", "h2st",
                       dH2U[t0:t0 + 128, sl * 512:(sl + 1) * 512], h2s[:],
                       r=(h2s,))
    bb.barrier()


def _phase2_down(nc, bb, dW, dH2U, dFFN, sStats, sSw, sRm, sSt, dDbg):
    ident16 = sSt["ident16"]
    bb.barrier()
    _e = ExitStack()
    _c = _e.enter_context
    pD0 = _c(nc.psum_tensor("pD0", [128, 1024], F32))
    pD1 = _c(nc.psum_tensor("pD1", [128, 1024], F32))
    pTc = _c(nc.psum_tensor("pTc", [128, 64], F32))
    pTd = _c(nc.psum_tensor("pTd", [128, 64], F32))
    ternDT = _c(nc.sbuf_tensor("ternDT", [128, 32 * 1024], BF16))
    h2ts = [_c(nc.sbuf_tensor(f"h2t{i}", [128, 4096], BF16)) for i in range(2)]
    h2qTs = [_c(nc.sbuf_tensor(f"h2qT{i}", [128, 32 * 128], BF16))
             for i in range(2)]
    qf32s = [_c(nc.sbuf_tensor(f"dqf32{i}", [128, 1024], F32)) for i in range(2)]
    qi32s = [_c(nc.sbuf_tensor(f"dqi32{i}", [128, 1024], I32)) for i in range(2)]
    qgs = [_c(nc.sbuf_tensor(f"dqg{i}", [128, 1024], BF16)) for i in range(2)]
    qbs = [_c(nc.sbuf_tensor(f"dqb{i}", [128, 1024], BF16)) for i in range(2)]
    wtA = _c(nc.sbuf_tensor("dwtA", [128, 1024], F32))
    wtB = _c(nc.sbuf_tensor("dwtB", [128, 1024], F32))
    fotA = _c(nc.sbuf_tensor("fotA", [128, 1024], F32))
    s2c = _c(nc.sbuf_tensor("s2c", [128, 1], F32))
    duc = _c(nc.sbuf_tensor("duc", [128, 1], F32))
    with _e:
        td3 = ternDT[:].rearrange("p (a d) -> p a d", a=32, d=1024)
        pT = (pTc, pTd)
        for dtile in range(8):
            d0 = dtile * 128
            for cc in range(4):
                par = (dtile * 4 + cc) % 2
                wt = wtA if par == 0 else wtB
                bb.dma("sync", f"wm{par}", wt[:],
                       dW["w_down"][d0:d0 + 128, cc * 1024:(cc + 1) * 1024],
                       w=(wt,))
                _quant_tern(nc, bb, qbs[par], wt, sSw[:, 5:6], qf32s[par],
                            qi32s[par], qgs[par], sSw)
                for k in range(8):
                    hc = cc * 8 + k
                    pt = pT[k % 2]
                    bb.tr(pt[:].bitcast(BF16)[:, 0:128],
                          qbs[par][:, k * 128:(k + 1) * 128], ident16[:],
                          r=(qbs[par], ident16), w=(pt,))
                    bb.op("scalar", "activation", td3[:, hc, d0:d0 + 128],
                          pt[:].bitcast(BF16)[:, 0:128], AF.Copy,
                          r=(pt,), w=(ternDT,))
        for ti in range(NT):
            t0 = ti * 128
            pD = pD0 if ti % 2 == 0 else pD1
            fot = fotA
            h2t = h2ts[ti % 2]
            h2qT = h2qTs[ti % 2]
            bb.dma("sync", "wm0", h2t[:], dH2U[t0:t0 + 128, :], w=(h2t,),
                   after=tuple(bb.qtokens("h2st")) if ti == 0 else ())
            bb.op("vector", "tensor_scalar", s2c[:], sRm[:, ti:ti + 1], 1e-5,
                  None, ALU.max, r=(sRm,), w=(s2c,))
            bb.op("vector", "tensor_scalar", duc[:], sStats[:, ti:ti + 1],
                  sSw[:, 1:2], None, ALU.mult, r=(sStats, sSw), w=(duc,))
            bb.op("vector", "tensor_scalar", duc[:], duc[:], sSw[:, 2:3],
                  1.0 / 127.0, ALU.mult, ALU.mult, r=(duc, sSw), w=(duc,))
            bb.op("vector", "tensor_tensor", sStats[:, NT + ti:NT + ti + 1],
                  duc[:], s2c[:], ALU.mult, r=(duc, s2c), w=(sStats,))
            bb.op("vector", "reciprocal", s2c[:], s2c[:], r=(s2c,), w=(s2c,))
            bb.op("vector", "tensor_scalar", s2c[:], s2c[:], 127.0, None,
                  ALU.mult, r=(s2c,), w=(s2c,))
            for qc in range(4):
                qf32, qi32, qg, qb = (qf32s[qc % 2], qi32s[qc % 2],
                                      qgs[qc % 2], qbs[qc % 2])
                bb.op("vector", "tensor_scalar", qf32[:],
                      h2t[:, qc * 1024:(qc + 1) * 1024], s2c[:], 0.5,
                      ALU.mult, ALU.add, r=(h2t, s2c), w=(qf32,))
                bb.op("vector", "tensor_copy", qi32[:], qf32[:],
                      r=(qf32,), w=(qi32,))
                bb.op("vector", "tensor_tensor", qg[:], qi32[:], qf32[:],
                      ALU.is_gt, r=(qi32, qf32), w=(qg,))
                bb.op("vector", "tensor_tensor", qb[:], qi32[:], qg[:],
                      ALU.subtract, r=(qi32, qg), w=(qb,))
                for k in range(8):
                    hc = qc * 8 + k
                    pt = pT[hc % 2]
                    bb.tr(pt[:].bitcast(BF16)[:, 0:128],
                          qb[:, k * 128:(k + 1) * 128], ident16[:],
                          r=(qb, ident16), w=(pt,))
                    bb.op("scalar", "activation",
                          h2qT[:, hc * 128:(hc + 1) * 128],
                          pt[:].bitcast(BF16)[:, 0:128], AF.Copy,
                          r=(pt,), w=(h2qT,))
            ch = []
            for hc in range(32):
                for nch in range(2):
                    ch.append((pD[:, nch * 512:(nch + 1) * 512],
                               h2qT[:, hc * 128:(hc + 1) * 128],
                               td3[:, hc, nch * 512:(nch + 1) * 512],
                               hc == 0, hc == 31))
            bb.mm(ch, r=(h2qT, ternDT), w=(pD,))
            bb.op("scalar", "activation", fot[:], pD[:], AF.Copy,
                  scale=sStats[:, NT + ti:NT + ti + 1], r=(pD, sStats),
                  w=(fot,))
            bb.dma("sync", "st", dFFN[t0:t0 + 128, :], fot[:], r=(fot,))
    bb.barrier()
    if "stats" in dDbg:
        bb.dma("sync", "dbg", dDbg["stats"][:], sStats[:], r=(sStats,))
    if "h2u" in dDbg:
        _dump_dram(nc, bb, dDbg["h2u"], dH2U, BF16, 4096)
    if "ffnout" in dDbg:
        _dump_dram(nc, bb, dDbg["ffnout"], dFFN, F32, D)


_DUMPN = [0]


def _dump_dram(nc, bb, dbg_ap, src_ap, dt_, cols):
    rows = src_ap.shape[0]
    _DUMPN[0] += 1
    with nc.sbuf_tensor(f"dumptmp{_DUMPN[0]}", [128, cols], dt_) as tmp:
        for k in range(rows // 128):
            bb.dma("sync", "ld", tmp[:], src_ap[k * 128:(k + 1) * 128, :], w=(tmp,))
            bb.dma("sync", "dbg", dbg_ap[k * 128:(k + 1) * 128, :], tmp[:], r=(tmp,))
    bb.barrier()


def _phase3_proj(nc, bb, dW, dS, xnT, sCkvT, sCkv, sScr, sKiT, sQiT, sWuv,
                 sWout8, sBK, sBK0, sSt, dCKV, dQT, dMUQ, dDbg):
    xnT3 = xnT[:].rearrange("p (a t) -> p a t", a=8, t=L)
    ident16 = sSt["ident16"]
    ident32 = sSt["ident32"]
    _e = ExitStack()
    _c = _e.enter_context
    pA = _c(nc.psum_tensor("pA", [128, 512], F32))
    pB = _c(nc.psum_tensor("pB", [128, 512], F32))
    pC = _c(nc.psum_tensor("pC", [128, 512], F32))
    pD = _c(nc.psum_tensor("pD", [128, 512], F32))
    pE = _c(nc.psum_tensor("pE", [128, 512], F32))
    WBUF = _c(nc.sbuf_tensor("WBUF", [128, 8 * 512], BF16))
    WUK = _c(nc.sbuf_tensor("WUK", [128, 2 * 512], BF16))
    WUKT = _c(nc.sbuf_tensor("WUKT", [128, 8 * 128], BF16))
    QCT = _c(nc.sbuf_tensor("QCT", [128, 4 * L], BF16))
    CQT = _c(nc.sbuf_tensor("CQT", [128, 4 * L], BF16))
    MUQ32 = _c(nc.sbuf_tensor("MUQ32", [128, 2 * L], BF16))
    MKT = _c(nc.sbuf_tensor("MKT", [32, L], BF16))
    spA = _c(nc.sbuf_tensor("spA", [128, 512], F32))
    spB = _c(nc.sbuf_tensor("spB", [128, 512], F32))
    v64 = _c(nc.sbuf_tensor("v64", [64, 512], BF16))
    v64b = _c(nc.sbuf_tensor("v64b", [64, 512], BF16))
    qstage = _c(nc.sbuf_tensor("qstage", [128, 4096], BF16))
    tq1 = _c(nc.sbuf_tensor("s_tq1", [64, L], BF16))
    tq2 = _c(nc.sbuf_tensor("s_tq2", [64, L], BF16))
    trigb = _c(nc.sbuf_tensor("s_trigb", [64, L], BF16))
    for _nm, _h in (("tq1", tq1), ("tq2", tq2), ("trigb", trigb)):
        bb.dma("sync", "ld", _h[:], dS[_nm][:], w=(_h,))
    if True:
        def wload(name, tiles, cols):
            for k in range(tiles):
                bb.dma("gpsimd", "cast", WBUF[:, k * cols:(k + 1) * cols],
                       dW[name][k * 128:(k + 1) * 128, :], w=(WBUF,))
            return WBUF
        wload2 = wload
        for k in range(2):
            bb.dma("gpsimd", "cast", WUK[:, k * 512:(k + 1) * 512],
                   dW["W_uk"][k * 128:(k + 1) * 128, :], w=(WUK,))
        for g in range(2):
            bb.dma("gpsimd", "cast", sWuv[g][:],
                   dW["W_uv"][g * 128:(g + 1) * 128, :], w=(sWuv[g],))
        for h in range(NH):
            bb.dma("gpsimd", "cast", sWout8[h][:],
                   dW["W_out"][h * 64:(h + 1) * 64, :], w=(sWout8[h],))

        WDKV = wload("W_dkv", 8, 256)
        for g in range(2):
            for nk in range(4):
                px = pA if nk % 2 == 0 else pD
                bb.mm([(px[:], WDKV[:, k * 256 + g * 128:k * 256 + g * 128 + 128],
                        xnT3[:, k, nk * 512:(nk + 1) * 512], k == 0, k == 7)
                       for k in range(8)], r=(WDKV, xnT), w=(px,))
                bb.op("vector", "tensor_copy",
                      sCkvT[g][:, nk * 512:(nk + 1) * 512], px[:], r=(px,),
                      w=(sCkvT[g],))
        for ti in range(NT):
            r0 = ti * 128
            px = pB if ti % 2 == 0 else pE
            bb.mm([(px[:, 0:256], xnT3[:, k, r0:r0 + 128],
                    WDKV[:, k * 256:(k + 1) * 256], k == 0, k == 7)
                   for k in range(8)], r=(WDKV, xnT), w=(px,))
            bb.op("vector", "tensor_copy", sCkv[ti][:], px[:, 0:256], r=(px,),
                  w=(sCkv[ti],))
            bb.op("vector", "tensor_copy", sScr[:, ti * 384:ti * 384 + 256],
                  sCkv[ti][:], r=(sCkv[ti],), w=(sScr,))
            bb.dma("sync", "st", dCKV[r0:r0 + 128, :], sCkv[ti][:], r=(sCkv[ti],))
        WDQ = wload("W_dq", 8, 512)
        for mi in range(4):
            for nk in range(4):
                px = pA if nk % 2 == 0 else pD
                bb.mm([(px[:], WDQ[:, k * 512 + mi * 128:k * 512 + mi * 128 + 128],
                        xnT3[:, k, nk * 512:(nk + 1) * 512], k == 0, k == 7)
                       for k in range(8)], r=(WDQ, xnT), w=(px,))
                bb.op("vector", "tensor_copy",
                      CQT[:, mi * L + nk * 512:mi * L + (nk + 1) * 512], px[:],
                      r=(px,), w=(CQT,))
        WUQ = wload("W_uq", 4, 512)
        cqT3 = CQT[:].rearrange("p (a t) -> p a t", a=4, t=L)
        for mi in range(4):
            for nk in range(4):
                px = pA if nk % 2 == 0 else pD
                bb.mm([(px[:], WUQ[:, k * 512 + mi * 128:k * 512 + mi * 128 + 128],
                        cqT3[:, k, nk * 512:(nk + 1) * 512], k == 0, k == 3)
                       for k in range(4)], r=(WUQ, CQT), w=(px,))
                bb.op("vector", "tensor_copy",
                      QCT[:, mi * L + nk * 512:mi * L + (nk + 1) * 512], px[:],
                      r=(px,), w=(QCT,))
        for g in range(2):
            for hp in range(4):
                bb.tr(pC[:].bitcast(BF16)[:, 0:128],
                      WUK[:, g * 512 + hp * 128:g * 512 + hp * 128 + 128],
                      ident16[:], r=(WUK, ident16), w=(pC,))
                bb.op("vector", "tensor_copy",
                      WUKT[:, (g * 4 + hp) * 128:(g * 4 + hp + 1) * 128],
                      pC[:].bitcast(BF16)[:, 0:128], r=(pC,), w=(WUKT,))
        qst3 = qstage[:].rearrange("p (t h) -> p t h", t=512, h=8)
        for g in range(2):
            for nk in range(4):
                for h in range(NH):
                    base = (h % 2) * 64
                    lhs = WUKT[base:base + 64,
                               (g * 4 + h // 2) * 128:(g * 4 + h // 2 + 1) * 128]
                    rhs = QCT[base:base + 64,
                              (h // 2) * L + nk * 512:(h // 2) * L + (nk + 1) * 512]
                    px = pA if h % 2 == 0 else pD
                    bb.mm([(px[:], lhs, rhs, True, True)], r=(WUKT, QCT), w=(px,))
                    bb.op("scalar", "activation", qst3[:, :, h], px[:], AF.Copy,
                          scale=SCALE, r=(px,), w=(qstage,))
                bb.dma("sync", "st", dQT[g, :, nk * 4096:(nk + 1) * 4096],
                       qstage[:, 0:4096], r=(qstage,))
        WQR = wload("W_qr", 4, 256)
        for half in range(2):
            for nk in range(4):
                px = pA if nk % 2 == 0 else pD
                bb.mm([(px[:], WQR[:, k * 256 + half * 128:k * 256 + half * 128 + 128],
                        cqT3[:, k, nk * 512:(nk + 1) * 512], k == 0, k == 3)
                       for k in range(4)], r=(WQR, CQT), w=(px,))
                _softplus(nc, bb, MUQ32[:, half * L + nk * 512:half * L + (nk + 1) * 512],
                          px, 128, spA, spB, MUQ32)
        dupr = sSt["dupr"]
        c2t = sSt["c2"]
        for var, tq_h in ((0, tq1), (1, tq2)):
            tq = tq_h
            for nk in range(4):
                for h in range(NH):
                    pb = 32 * (h % 4)
                    rhs = MUQ32[pb:pb + 32,
                                (h // 4) * L + nk * 512:(h // 4) * L + (nk + 1) * 512]
                    px = pB if h % 2 == 0 else pE
                    py = pC if h % 2 == 0 else pD
                    vv = v64 if h % 2 == 0 else v64b
                    bb.mm([(px[0:64, :], dupr[pb:pb + 32, :], rhs, True, True)],
                          r=(dupr, MUQ32), w=(px,),
                          tile_position=(pb, 0) if pb == 96 else None)
                    bb.op("vector", "tensor_tensor", vv[:], px[0:64, :],
                          tq[:, nk * 512:(nk + 1) * 512], ALU.mult,
                          r=(px, tq), w=(vv,))
                    bb.mm([(py[0:64, :], c2t[:], vv[:], True, True)],
                          r=(c2t, vv), w=(py,))
                    bb.op("scalar", "activation", qst3[0:64, :, h], py[0:64, :],
                          AF.Copy, scale=SCALE, r=(py,), w=(qstage,))
                bb.dma("sync", "st", dMUQ[var, :, nk * 4096:(nk + 1) * 4096],
                       qstage[0:64, 0:4096], r=(qstage,))
        IWQ = wload("idx_wq", 8, 256)
        for g in range(2):
            for nk in range(4):
                bb.mm([(pA[:], IWQ[:, k * 256 + g * 128:k * 256 + g * 128 + 128],
                        xnT3[:, k, nk * 512:(nk + 1) * 512], k == 0, k == 7)
                       for k in range(8)], r=(IWQ, xnT), w=(pA,))
                bb.op("vector", "tensor_copy",
                      sQiT[g][:, nk * 512:(nk + 1) * 512], pA[:], r=(pA,),
                      w=(sQiT[g],))
        IWK = wload("idx_wk", 8, 64)
        for hr in range(2):
            for nk in range(4):
                bb.mm([(pA[hr * 64:hr * 64 + 64, :], IWK[:, k * 64:(k + 1) * 64],
                        xnT3[:, k, nk * 512:(nk + 1) * 512], k == 0, k == 7)
                       for k in range(8)], r=(IWK, xnT), w=(pA,))
                bb.op("vector", "tensor_copy",
                      sKiT[hr * 64:hr * 64 + 64, nk * 512:(nk + 1) * 512],
                      pA[hr * 64:hr * 64 + 64, :], r=(pA,), w=(sKiT,))
        WKR = wload("W_kr", 8, 32)
        for nk in range(4):
            bb.mm([(pA[0:32, :], WKR[:, k * 32:(k + 1) * 32],
                    xnT3[:, k, nk * 512:(nk + 1) * 512], k == 0, k == 7)
                   for k in range(8)], r=(WKR, xnT), w=(pA,))
            _softplus(nc, bb, MKT[:, nk * 512:(nk + 1) * 512], pA, 32, spA, spB, MKT)
        for nk in range(4):
            bb.mm([(pB[0:64, :], dupr[0:32, :], MKT[:, nk * 512:(nk + 1) * 512],
                    True, True)], r=(dupr, MKT), w=(pB,))
            bb.op("vector", "tensor_tensor", sBK[:, nk * 512:(nk + 1) * 512],
                  pB[0:64, :], trigb[:, nk * 512:(nk + 1) * 512], ALU.mult,
                  r=(pB, trigb), w=(sBK,))
        bb.mm([(pB[0:64, 0:192], dupr[0:32, :], MKT[:, 0:192], True, True)],
              r=(dupr, MKT), w=(pB,))
        bb.op("vector", "tensor_tensor", sBK0[:], pB[0:64, 0:192],
              sSt["trigb0"][:], ALU.mult, r=(pB, sSt["trigb0"]), w=(sBK0,))
        for ti in range(NT):
            r0 = ti * 128
            bb.tr(pC[:].bitcast(BF16)[:, 0:32], MKT[:, r0:r0 + 128],
                  ident16[0:32, 0:32], r=(MKT, ident16), w=(pC,))
            mu = pC[:].bitcast(BF16)[:, 0:32]
            for dupi in range(2):
                dst = sScr[:, ti * 384 + 256 + dupi * 32:ti * 384 + 256 + dupi * 32 + 32]
                bb.op("vector", "tensor_copy",
                      dst.rearrange("p (a b) -> p a b", a=2, b=16),
                      mu[:, dupi * 16:dupi * 16 + 16]
                      .rearrange("p (a b) -> p a b", a=1, b=16)
                      .broadcast_to([128, 2, 16]),
                      r=(pC,), w=(sScr,))
    bb.barrier()
    _e.close()
    for nm, src in (("ckvT0", sCkvT[0]), ("kiT", sKiT), ("scr", sScr),
                    ("bk", sBK), ("qiT0", sQiT[0])):
        if nm in dDbg:
            bb.dma("sync", "dbg", dDbg[nm][:], src[:], r=(src,))
    bb.barrier()


def _softplus(nc, bb, out_ap, psum, rows, spA, spB, outbuf):
    src = psum[0:rows, :]
    a = spA[0:rows, :]
    b = spB[0:rows, :]
    bb.op("scalar", "activation", a, src, AF.Abs, r=(psum,), w=(spA,))
    bb.op("scalar", "activation", b, a, AF.Exp, scale=-1.0, r=(spA,), w=(spB,))
    bb.op("scalar", "activation", b, b, AF.Ln, bias=1.0, r=(spB,), w=(spB,))
    bb.op("vector", "scalar_tensor_tensor", out_ap, src, 0.0, b, ALU.max,
          ALU.add, r=(psum, spB), w=(outbuf,))


def _phase4_attn(nc, bb, dX, dOut, dFFN, dCKV, dQT, dMUQ, sCkvT, sCkv, sScr,
                 sKiT, sQiT, sWuv, sWout8, sBK, sBK0, sSt, dS, sStats, sIdxw,
                 dDbg):
    P4T = int(os.environ.get("MK_P4TILES", "16"))
    P4S = int(os.environ.get("MK_P4STEP", "99"))
    ident16 = sSt["ident16"]
    ident32 = sSt["ident32"]
    _e = ExitStack()
    _c = _e.enter_context
    pI = _c(nc.psum_tensor("pI", [128, 512], F32))
    pBand = _c(nc.psum_tensor("pBand", [128, 192], F32))
    pGath = _c(nc.psum_tensor("pGath", [128, 512], F32))
    pT32 = _c(nc.psum_tensor("pT32", [128, 128], F32))
    pT16 = _c(nc.psum_tensor("pT16", [128, 256], BF16))
    pLat = _c(nc.psum_tensor("pLat", [128, 256], F32))
    pOut = _c(nc.psum_tensor("pOut", [128, 1024], F32))
    sFut = _c(nc.sbuf_tensor("s4_fut", [128, L], F32))
    sPert = _c(nc.sbuf_tensor("s4_pert", [128, L], F32))
    sSt = dict(sSt, fut=sFut, pert=sPert)
    bb.dma("sync", "ld", sFut[:], dS["fut"][:], w=(sFut,))
    bb.dma("sync", "ld", sPert[:], dS["pert"][:], w=(sPert,))
    QTS = _c(nc.sbuf_tensor("QTS", [128, 2 * NH * 128], BF16))
    QTS2 = _c(nc.sbuf_tensor("QTS2", [128, 2 * NH * 128], BF16))
    MQS = _c(nc.sbuf_tensor("MQS", [128, 2 * NH * 128], BF16))
    MQS2 = _c(nc.sbuf_tensor("MQS2", [128, 2 * NH * 128], BF16))
    iacc = _c(nc.sbuf_tensor("iacc", [128, L], F32))
    tmp16 = _c(nc.sbuf_tensor("tmp16", [128, L], BF16))
    msk = _c(nc.sbuf_tensor("msk", [128, L], F32))
    scrA = _c(nc.sbuf_tensor("scrA", [128, 8 * GW], F32))
    ppan = _c(nc.sbuf_tensor("ppan", [128, 8 * GW], BF16))
    gT = _c(nc.sbuf_tensor("gT", [128, 3 * 4096], BF16))
    gV = _c(nc.sbuf_tensor("gV", [128, 32 * 256], BF16))
    vals = _c(nc.sbuf_tensor("vals", [128, 32], F32))
    idxu = _c(nc.sbuf_tensor("idxu", [128, 32], U32))
    idxf = _c(nc.sbuf_tensor("idxf", [128, 32], F32))
    idxS = _c(nc.sbuf_tensor("idxS", [16, 256], F32))
    idxA = _c(nc.sbuf_tensor("idxA", [128, 256], I16))
    idxB = _c(nc.sbuf_tensor("idxB", [128, 256], I16))
    idxAc = [_c(nc.sbuf_tensor(f"idxAc{c}", [128, 32], I16)) for c in range(8)]
    idxBc = [_c(nc.sbuf_tensor(f"idxBc{c}", [128, 64], I16)) for c in range(4)]
    pbTA = _c(nc.sbuf_tensor("pbTA", [128, 128], BF16))
    pbTB = _c(nc.sbuf_tensor("pbTB", [64, 128], BF16))
    pgT = _c(nc.sbuf_tensor("pgT", [128, 128], BF16))
    olt0 = _c(nc.sbuf_tensor("olt0", [128, NH * 128], BF16))
    olt1 = _c(nc.sbuf_tensor("olt1", [128, NH * 128], BF16))
    oat = _c(nc.sbuf_tensor("oat", [64, NH * 128], BF16))
    xt = _c(nc.sbuf_tensor("xt", [128, D], F32))
    ft = _c(nc.sbuf_tensor("ft", [128, D], F32))
    smax = _c(nc.sbuf_tensor("smax", [128, 8], F32))
    ssum = _c(nc.sbuf_tensor("ssum", [128, 8], F32))
    if True:
        olt = [olt0, olt1]
        bb.op("vector", "memset", iacc[:], 0.0, w=(iacc,))
        bb.op("vector", "memset", msk[:], 0.0, w=(msk,))
        nidx_reg = nc.gpsimd.alloc_register("nidx")
        nc.gpsimd.reg_mov(nidx_reg, 1024)
        nidx512 = nc.gpsimd.alloc_register("nidx512")
        nc.gpsimd.reg_mov(nidx512, 512)
        bb.op("vector", "memset", idxA[:], 0, w=(idxA,))
        bb.op("vector", "memset", idxB[:], 0, w=(idxB,))
        gT6 = gT[:].rearrange("p (rh c2 j q rl) -> p rh c2 j q rl",
                              rh=2, c2=4, j=3, q=32, rl=16)
        gV3 = gV[:].rearrange("p (c e) -> p c e", c=32, e=256)
        trigk_b = (sSt["trigk"][:].rearrange("p (rh rl) -> p rh rl", rh=2, rl=16)
                   .unsqueeze(2).broadcast_to([64, 2, 128, 16]))
        QTSb = (QTS, QTS2)
        MQSb = (MQS, MQS2)
        st8 = {"prev": None, "g1": [], "g2": []}

        def emit_A1(ti):
            t0 = ti * 128
            QTSl, MQSl = QTSb[ti % 2], MQSb[ti % 2]
            for g in range(2):
                bb.dma("sync", "ld", QTSl[:, g * 1024:(g + 1) * 1024],
                       dQT[g, :, t0 * 8:t0 * 8 + 1024], w=(QTSl,))
            for var in range(2):
                bb.dma("sync", "ld", MQSl[0:64, var * 1024:(var + 1) * 1024],
                       dMUQ[var, :, t0 * 8:t0 * 8 + 1024], w=(MQSl,))
            Wm = 128 if ti == 0 else t0 + 64
            nI = (Wm + 511) // 512
            for h in range(NIH):
                base = (h % 2) * 64
                lhs = sQiT[h // 2][base:base + 64, t0:t0 + 128]
                for nk in range(nI):
                    wseg = min(512, Wm - nk * 512)
                    pix = pI if (h * nI + nk) % 2 == 0 else pGath
                    bb.mm([(pix[:, 0:wseg], lhs,
                            sKiT[base:base + 64, nk * 512:nk * 512 + wseg],
                            True, True)], r=(sQiT[h // 2], sKiT), w=(pix,))
                    bb.op("scalar", "activation", tmp16[:, nk * 512:nk * 512 + wseg],
                          pix[:, 0:wseg], AF.Relu, r=(pix,), w=(tmp16,))
                if h == 0:
                    bb.op("vector", "scalar_tensor_tensor", iacc[:, 0:Wm],
                          tmp16[:, 0:Wm], sIdxw[:, 0:1], sSt["pert"][:, 0:Wm],
                          ALU.mult, ALU.add, r=(tmp16, sIdxw, sSt["pert"]),
                          w=(iacc,))
                else:
                    bb.op("vector", "scalar_tensor_tensor", iacc[:, 0:Wm],
                          tmp16[:, 0:Wm], sIdxw[:, h:h + 1], iacc[:, 0:Wm],
                          ALU.mult, ALU.add, r=(tmp16, sIdxw, iacc), w=(iacc,))
            if ti == 1 and "iacc1" in dDbg:
                bb.dma("sync", "dbg", dDbg["iacc1"][:], iacc[:], r=(iacc,))
            bb.op("gpsimd", "affine_select", msk[:, 0:Wm], iacc[:, 0:Wm],
                  pattern=[[-1, Wm]], compare_op=ALU.is_ge, fill=-2.0e8,
                  base=t0 - 64, channel_multiplier=1, r=(iacc,), w=(msk,))
            bb.op("gpsimd", "affine_select", iacc[:, 0:Wm], sSt["fut"][:, 0:Wm],
                  pattern=[[1, Wm]], compare_op=ALU.is_ge, fill=0.0,
                  base=-t0 - 1, channel_multiplier=-1, r=(sSt["fut"],),
                  w=(iacc,))
            bb.op("vector", "tensor_tensor", msk[:, 0:Wm], msk[:, 0:Wm],
                  iacc[:, 0:Wm], ALU.add, r=(msk, iacc), w=(msk,))
            for it in range(4):
                bb.op("vector", "max", out=vals[:, it * 8:(it + 1) * 8],
                      in_=msk[:, 0:Wm], r=(msk,), w=(vals,))
                bb.op("vector", "max_index", out=idxu[:, it * 8:(it + 1) * 8],
                      in_max=vals[:, it * 8:(it + 1) * 8], in_values=msk[:, 0:Wm],
                      r=(msk, vals), w=(idxu,))
                if it < 3:
                    bb.op("vector", "match_replace", out=msk[:, 0:Wm],
                          in_to_replace=vals[:, it * 8:(it + 1) * 8],
                          in_values=msk[:, 0:Wm], imm_value=REPL, r=(msk, vals),
                          w=(msk,))
            if ti == 1 and "idx1" in dDbg:
                bb.dma("sync", "dbg", dDbg["idx1"][:], idxu[:], r=(idxu,))

        def emit_A1b(ti):
            bb.op("vector", "tensor_copy", idxf[:], idxu[:], r=(idxu,), w=(idxf,))
            rep = sSt["rep16"]
            for hr in range(2):
                bb.tr(pT32[0:16, 0:128], idxf[:, hr * 16:(hr + 1) * 16],
                      ident32[:], r=(idxf, ident32), w=(pT32,))
                bb.op("vector", "tensor_copy",
                      idxS[0:16, hr * 128:(hr + 1) * 128], pT32[0:16, 0:128],
                      r=(pT32,), w=(idxS,))
                bb.mm([(pT32[:, 0:128], rep[:],
                        idxS[0:16, hr * 128:(hr + 1) * 128], True, True)],
                      r=(sSt["rep16"], idxS), w=(pT32,))
                bb.op("vector", "tensor_copy",
                      idxA[:, hr * 128:(hr + 1) * 128], pT32[:, 0:128],
                      r=(pT32,), w=(idxA,))
            bb.op("vector", "tensor_copy",
                  idxB[:].rearrange("p (q r) -> p q r", q=128, r=2),
                  idxA[:].rearrange("p (r q) -> p r q", r=2, q=128)
                  .transpose([0, 2, 1]), r=(idxA,), w=(idxB,))
            for c in range(8):
                bb.op("vector", "tensor_copy", idxAc[c][:],
                      idxA[:, c * 32:(c + 1) * 32], r=(idxA,), w=(idxAc[c],))
            for c in range(4):
                bb.op("vector", "tensor_copy", idxBc[c][:],
                      idxB[:, c * 64:(c + 1) * 64], r=(idxB,), w=(idxBc[c],))

        def emit_G1(ti):
            if st8["prev"] is not None:
                bb.wait("gpsimd", st8["prev"])
            st8["g1"] = []
            for c in range(8):
                tok = bb.gather("gq", r=(sScr, idxAc[c]), w=(gT,),
                                out_ap=gT[:, c * 1536:(c + 1) * 1536]
                                .rearrange("p (j n) -> p j n", j=3, n=512),
                                in_ap=sScr[:], idxs_ap=idxAc[c][:],
                                num_idxs=512, num_idxs_reg=nidx512, elem_size=384,
                                transpose=True, sbuf_tokens_per_rank=128,
                                sbuf_free_dim_per_rank=768,
                                sbuf_free_dim_pad_per_rank=0, sbuf_byte_offset=0)
                st8["g1"].append(tok)

        def emit_G2(ti):
            for c in range(8):
                mu_sl = gT[0:64, c * 1536 + 1024:c * 1536 + 1536].rearrange(
                    "p (q rl) -> p q rl", q=32, rl=16)
                tk_sl = (sSt["trigk"][0:64, (c // 4) * 16:(c // 4) * 16 + 16]
                         .unsqueeze(1).broadcast_to([64, 32, 16]))
                bb.op("vector", "tensor_tensor", mu_sl, mu_sl, tk_sl, ALU.mult,
                      r=(gT, sSt["trigk"]), w=(gT,), after=tuple(st8["g1"]))
            st8["g2"] = []
            for c in range(4):
                tok = bb.gather(
                    "gq", r=(idxBc[c],), w=(gV,),
                    out_ap=gV3[:, c * 8:(c + 1) * 8, :], in_ap=dCKV[:],
                    idxs_ap=idxBc[c][:], num_idxs=1024,
                    num_idxs_reg=nidx_reg, elem_size=256)
                st8["g2"].append(tok)
            st8["prev"] = tok

        def emit_B1(ti):
            t0 = ti * 128
            QTSl, MQSl = QTSb[ti % 2], MQSb[ti % 2]
            g1toks = tuple(st8["g1"])
            for g in range(8):
                a, width, mname, qside = band_params(ti, g)
                off = g * 16
                lq = [QTSl[:, gg * 1024 + off * 8:gg * 1024 + off * 8 + 128]
                      for gg in range(2)]
                lm = [MQSl[0:64, v * 1024 + off * 8:v * 1024 + off * 8 + 128]
                      for v in range(2)]
                w0 = 128 * a
                if qside == 1:
                    bk_ap = sBK0[:, 0:width]
                else:
                    bk_ap = sBK[:, w0:w0 + width]
                bb.mm([(pBand[:, 0:width], lq[0], sCkvT[0][:, w0:w0 + width],
                        True, False),
                       (pBand[:, 0:width], lq[1], sCkvT[1][:, w0:w0 + width],
                        False, False),
                       (pBand[:, 0:width], lm[qside - 1], bk_ap, False, True)],
                      r=(QTSl, MQSl, sCkvT[0], sCkvT[1],
                         sBK0 if qside == 1 else sBK),
                      w=(pBand,))
                mask = sSt[mname]
                bb.op("vector", "scalar_tensor_tensor",
                      scrA[:, g * GW:g * GW + width], pBand[:, 0:width], 1.0,
                      mask[:, 0:width], ALU.mult, ALU.add, r=(pBand, mask),
                      w=(scrA,))
                if width < 192:
                    bb.op("vector", "memset", scrA[:, g * GW + width:g * GW + 192],
                          BIGNEG, w=(scrA,))
                if qside == 1:
                    bb.op("vector", "memset", scrA[:, g * GW + 192:(g + 1) * GW],
                          BIGNEG, w=(scrA,))
                else:
                    nap = [gT6[:, :, g // 2, jj,
                               (g % 2) * 16:(g % 2) * 16 + 16, :]
                           .transpose([0, 2, 1, 3]) for jj in range(3)]
                    pg = pGath if g % 2 == 0 else pI
                    bb.mm([(pg[:], lq[0], nap[0], True, False),
                           (pg[:], lq[1], nap[1], False, False),
                           (pg[:], lm[0], nap[2][0:64], False, True)],
                          r=(QTSl, MQSl, gT), w=(pg,), after=g1toks)
                    dg = sSt["diagmask"]
                    for Bq in range(4):
                        bb.op("vector", "scalar_tensor_tensor",
                              scrA[Bq * 32:(Bq + 1) * 32,
                                   g * GW + 192:(g + 1) * GW],
                              pg[Bq * 32:(Bq + 1) * 32, Bq * 128:(Bq + 1) * 128],
                              1.0, dg[Bq * 32:(Bq + 1) * 32, :], ALU.mult,
                              ALU.add, r=(pg, dg), w=(scrA,))
            if ti == 1 and "gt1" in dDbg:
                bb.dma("sync", "dbg", dDbg["gt1"][:], gT[:], r=(gT,))

        def emit_B2(ti):
            sc3 = scrA[:].rearrange("p (g n) -> p g n", g=8, n=GW)
            bb.op("vector", "tensor_reduce", smax[:], sc3, AX.X, ALU.max,
                  r=(scrA,), w=(smax,))
            bb.op("vector", "tensor_tensor", sc3, sc3,
                  smax[:].unsqueeze(2).broadcast_to([128, 8, GW]), ALU.subtract,
                  r=(scrA, smax), w=(scrA,))
            bb.op("scalar", "activation", ppan[:], scrA[:], AF.Exp, r=(scrA,),
                  w=(ppan,))
            pp3 = ppan[:].rearrange("p (g n) -> p g n", g=8, n=GW)
            bb.op("vector", "tensor_reduce", ssum[:], pp3, AX.X, ALU.add,
                  r=(ppan,), w=(ssum,))
            bb.op("vector", "reciprocal", ssum[:], ssum[:], r=(ssum,), w=(ssum,))
            bb.op("vector", "tensor_tensor", pp3, pp3,
                  ssum[:].unsqueeze(2).broadcast_to([128, 8, GW]), ALU.mult,
                  r=(ppan, ssum), w=(ppan,))
            if ti == 1:
                for nm, src in (("scr1", scrA), ("ppan1", ppan)):
                    if nm in dDbg:
                        bb.dma("sync", "dbg", dDbg[nm][:], src[:], r=(src,))

        def emit_B3(ti):
            g2toks = tuple(st8["g2"])
            for g in range(8):
                a, width, mname, qside = band_params(ti, g)
                gb = g * GW
                bb.tr(pT16[0:128, 0:128], ppan[:, gb:gb + 128], ident16[:],
                      r=(ppan, ident16), w=(pT16,))
                bb.op("scalar", "activation", pbTA[:], pT16[:, 0:128],
                      AF.Copy, r=(pT16,), w=(pbTA,))
                if width == 192:
                    bb.tr(pT16[0:64, 128:256], ppan[:, gb + 128:gb + 192],
                          ident16[:], r=(ppan, ident16), w=(pT16,))
                    bb.op("scalar", "activation", pbTB[:], pT16[0:64, 128:256],
                          AF.Copy, r=(pT16,), w=(pbTB,))
                if qside == 2:
                    bb.tr(pT16[0:128, 128:256], ppan[:, gb + 192:gb + GW],
                          ident16[:], r=(ppan, ident16), w=(pT16,))
                    bb.op("scalar", "activation", pgT[:], pT16[:, 128:256],
                          AF.Copy, r=(pT16,), w=(pgT,))
                for gam in range(2):
                    pl = pLat[:, gam * 128:(gam + 1) * 128]
                    ch = [(pl, sCkv[a][:, gam * 128:(gam + 1) * 128], pbTA[:],
                           True, False)]
                    if width == 192:
                        ch.append((pl, sCkv[a + 1][0:64, gam * 128:(gam + 1) * 128],
                                   pbTB[:], False, False))
                    ch[-1] = (*ch[-1][:4], qside == 1)
                    bb.mm(ch, r=(pbTA, pbTB, sCkv[a],
                                 sCkv[min(a + 1, NT - 1)]), w=(pLat,))
                    if qside == 2:
                        for C in range(4):
                            bb.mm([(pLat[:, gam * 128 + C * 32:gam * 128 + (C + 1) * 32],
                                    gV[:, (g * 4 + C) * 256 + gam * 128:
                                       (g * 4 + C) * 256 + (gam + 1) * 128],
                                    pgT[:, C * 32:(C + 1) * 32], False, C == 3)],
                                  r=(gV, pgT), w=(pLat,), after=g2toks)
                    bb.op("scalar", "activation",
                          olt[gam][:].rearrange("p (h t) -> p h t", h=NH, t=128)
                          [:, :, g * 16:(g + 1) * 16].transpose([0, 2, 1]),
                          pLat[:, gam * 128:(gam + 1) * 128]
                          .rearrange("p (dq h) -> p dq h", dq=16, h=NH),
                          AF.Copy, r=(pLat,), w=(olt[gam],))
            if ti == 1 and "gv1" in dDbg:
                bb.dma("sync", "dbg", dDbg["gv1"][:], gV[:], r=(gV,))
            if ti == 1 and "olt1" in dDbg:
                bb.dma("sync", "dbg", dDbg["olt1"][:], olt[0][:], r=(olt[0],))

        def emit_B4(ti):
            t0 = ti * 128
            for h in range(NH):
                pv = pLat[0:64, (h % 2) * 128:(h % 2) * 128 + 128]
                bb.mm([(pv, sWuv[gam][:, h * 64:(h + 1) * 64],
                        olt[gam][:, h * 128:(h + 1) * 128], gam == 0, gam == 1)
                       for gam in range(2)],
                      r=(sWuv[0], sWuv[1], olt[0], olt[1]), w=(pLat,))
                bb.op("scalar", "activation", oat[:, h * 128:(h + 1) * 128],
                      pv, AF.Copy, r=(pLat,), w=(oat,))
            bb.mm([(pOut[:, nch * 512:(nch + 1) * 512],
                    oat[0:64, h * 128:(h + 1) * 128],
                    sWout8[h][:, nch * 512:(nch + 1) * 512], h == 0, h == NH - 1)
                   for h in range(NH) for nch in range(2)],
                  r=(oat,) + tuple(sWout8), w=(pOut,))
            bb.dma("sync", "ld", xt[:], dX[t0:t0 + 128, :], w=(xt,))
            bb.dma("sync", "ld", ft[:], dFFN[t0:t0 + 128, :], w=(ft,))
            bb.op("vector", "scalar_tensor_tensor", ft[:], pOut[:], 1.0, ft[:],
                  ALU.mult, ALU.add, r=(pOut, ft), w=(ft,))
            bb.op("vector", "tensor_tensor", xt[:], xt[:], ft[:], ALU.add,
                  r=(xt, ft), w=(xt,))
            bb.dma("sync", "outq", dOut[t0:t0 + 128, :], xt[:], r=(xt,))
            if ti == 1:
                if "qts1" in dDbg:
                    bb.dma("sync", "dbg", dDbg["qts1"][:], QTSb[1][:],
                           r=(QTSb[1],))
                if "mqs1" in dDbg:
                    bb.dma("sync", "dbg", dDbg["mqs1"][:], MQSb[1][0:64, :],
                           r=(MQSb[1],))
                if "sca1" in dDbg:
                    bb.dma("sync", "dbg", dDbg["sca1"][:], scrA[:], r=(scrA,))

        emit_A1(0)
        emit_A1b(0)
        emit_G1(0)
        emit_G2(0)
        for ti in range(NT):
            emit_B1(ti)
            if ti + 1 < NT:
                emit_A1(ti + 1)
                emit_A1b(ti + 1)
                emit_G1(ti + 1)
            emit_B2(ti)
            emit_B3(ti)
            if ti + 1 < NT:
                emit_G2(ti + 1)
            emit_B4(ti)
        bb.barrier()
    _e.close()


def kernel(**inputs):
    statics = build_statics()
    nc, _dbg = build_program()
    x = np.ascontiguousarray(np.asarray(inputs["L0_attn_x"], dtype=np.float32))
    base = {}
    for nm, sh in WEIGHT_SHAPES.items():
        key = ("L0_ffn_" + nm) if nm.startswith("w_") else ("L0_attn_" + nm)
        base[nm] = np.ascontiguousarray(
            np.asarray(inputs[key], dtype=np.float32).reshape(sh))
    for nm, arr in statics.items():
        base["st_" + nm] = arr
    in_maps = [dict(base, x=x[b]) for b in range(8)]
    res = run_bass_kernel_spmd(nc, in_maps, core_ids=list(range(8)))
    out = np.stack([np.asarray(res.results[b]["out"]) for b in range(8)], axis=0)
    return out.astype(np.float32)
